# version 27
# baseline (speedup 1.0000x reference)
import numpy as np
from contextlib import ExitStack

import concourse.bass as bass
import concourse.mybir as mybir
from concourse.bass_utils import run_bass_kernel_spmd

F32 = mybir.dt.float32
BF16 = mybir.dt.bfloat16
AF = mybir.ActivationFunctionType
ALU = mybir.AluOpType
AX = mybir.AxisListType.X

T = 4096
NT = T // 128
D = 1024
HF = 4096
EPS = 1e-6
LNEPS = 1e-5
LSEQ = 16384.0
NB = 8
BT = 512

ENGINES = ("sync", "gpsimd", "tensor", "vector", "scalar")

_STOP_AFTER = 0


class Sem:
    def __init__(self, name):
        self.name, self.n, self.h = name, 0, None

    def ev(self):
        return (self, self.n)


def rap(view, extra_off, dims):
    if isinstance(view, bass.AP):
        return bass.AP(view.tensor, view.offset + extra_off, dims)
    return bass.AP(view, extra_off, dims)


class Builder:
    def __init__(self):
        self.nc = bass.Bass("TRN2", target_bir_lowering=False)
        self.q = {e: [] for e in ENGINES}
        self.LD = Sem("LD")
        self.GD = Sem("GD")
        self.CC = Sem("CC")
        self.PE = Sem("PE")
        self.DV = Sem("DV")
        self.AC = Sem("AC")
        self.PO = Sem("PO")
        self.sems = [self.LD, self.GD, self.CC, self.PE, self.DV, self.AC,
                     self.PO]

    def step(self, eng, emit, waits=(), incs=()):
        waits = [w for w in waits if w is not None]
        for s, v in waits:
            assert v <= s.n, f"wait on future event {s.name}:{v} > {s.n}"
        self.q[eng].append((waits, emit, list(incs)))
        for s, k in incs:
            s.n += k
        return incs[0][0].ev() if incs else None

    def dma(self, eng, out, in_, waits=(), slow=False):
        sem = self.GD if eng == "gpsimd" else self.LD
        return self.step(
            eng,
            lambda g, o=out, i=in_, sl=slow: g.dma_start(
                out=o, in_=i, allow_slow_non_contiguous=sl),
            waits, [(sem, 16)])

    def mm(self, out, lhsT, rhs, start, stop, waits=()):
        return self.step(
            "tensor",
            lambda g, o=out, l=lhsT, r=rhs, s0=start, s1=stop:
            g.matmul(o, l, r, start=s0, stop=s1, skip_group_check=True),
            waits, [(self.PE, 1)])

    def tr(self, out, in_, waits=()):
        k = in_.shape[0]
        return self.step(
            "tensor",
            lambda g, o=out, i=in_, kk=k:
            g.transpose(o, i, self.ident[0:kk, 0:kk]),
            waits, [(self.PE, 1)])

    def dv(self, emit, waits=()):
        return self.step("vector", emit, waits, [(self.DV, 1)])

    def ac(self, emit, waits=()):
        return self.step("scalar", emit, waits, [(self.AC, 1)])

    def po(self, emit, waits=()):
        return self.step("gpsimd", emit, waits, [(self.PO, 1)])

    def build(self):
        nc = self.nc
        dm = {}
        self.x_in = nc.dram_tensor("x", [T, D], F32, kind="ExternalInput").ap()
        for nm in ("wq", "wk", "wv", "wo"):
            dm[nm] = nc.dram_tensor(nm, [D, D], F32, kind="ExternalInput").ap()
        for nm in ("bq", "bk", "bv", "bo"):
            dm[nm] = nc.dram_tensor(nm, [1, D], F32, kind="ExternalInput").ap()
        dm["wfc"] = nc.dram_tensor("wfc", [HF, D], F32, kind="ExternalInput").ap()
        dm["gwfc"] = nc.dram_tensor("gwfc", [HF // 8, D], F32,
                                    kind="ExternalInput").ap()
        dm["gwproj"] = nc.dram_tensor("gwproj", [D // 8, HF], F32,
                                      kind="ExternalInput").ap()
        dm["lnw"] = nc.dram_tensor("lnw", [1, HF], F32, kind="ExternalInput").ap()
        dm["lnb"] = nc.dram_tensor("lnb", [1, HF], F32, kind="ExternalInput").ap()
        dm["wproj"] = nc.dram_tensor("wproj", [D, HF], F32,
                                     kind="ExternalInput").ap()
        self.dm = dm
        self.y_out = nc.dram_tensor("y", [T, D], F32, kind="ExternalOutput").ap()

        self.x1_d = nc.dram_tensor("x1d", [T, D], F32, kind="Internal").ap()
        self.x2_d = nc.dram_tensor("x2d", [T, D], F32, kind="Internal").ap()
        self.qfm_d = nc.dram_tensor("qfmd", [128, 8 * T], BF16, kind="Internal").ap()
        self.kfm_d = nc.dram_tensor("kfmd", [128, 8 * T], BF16, kind="Internal").ap()
        self.qtm_d = nc.dram_tensor("qtmd", [T, D], BF16, kind="Internal").ap()
        self.ktm_d = nc.dram_tensor("ktmd", [T, D], BF16, kind="Internal").ap()
        self.vtm_d = nc.dram_tensor("vtmd", [T, D], BF16, kind="Internal").ap()
        self.si_d = nc.dram_tensor("sid", [T, 16], F32, kind="Internal").ap()
        self.sa_d = nc.dram_tensor("sad", [T, 16], F32, kind="Internal").ap()
        self.wfcq_d = nc.dram_tensor("wfcqd", [128, 8 * HF], BF16,
                                     kind="Internal").ap()
        self.watt_d = nc.dram_tensor("wattd", [4 * D, D], BF16,
                                     kind="Internal").ap()
        self.w2t_d = nc.dram_tensor("w2td", [128, 32 * D], BF16,
                                    kind="Internal").ap()
        self.gred_d = nc.dram_tensor("gredd", [128, 1], F32, kind="Internal").ap()
        self.gred2 = nc.dram_tensor("gred2", [1, 512], F32, kind="Internal").ap()
        self.arg_i = nc.dram_tensor("argi", [1, 2], F32, kind="Internal").ap()
        self.arg_o = nc.dram_tensor("argo", [1, 2], F32, kind="Internal").ap()
        self.c2_d = nc.dram_tensor("c2d", [128, 16], F32, kind="Internal").ap()
        self.zf_d = nc.dram_tensor("zfd", [16, 1], F32, kind="Internal").ap()
        self.ph_d = nc.dram_tensor("phd", [16, 2], F32, kind="Internal").ap()
        self.ar1i = nc.dram_tensor("ar1i", [128, 16], F32, kind="Internal").ap()
        self.ar1o = nc.dram_tensor("ar1o", [128, 16], F32, kind="Internal").ap()
        self.ar2i = nc.dram_tensor("ar2i", [16, 2048], F32, kind="Internal").ap()
        self.ar2o = nc.dram_tensor("ar2o", [16, 2048], F32, kind="Internal").ap()
        self.ar3i = nc.dram_tensor("ar3i", [128, 513], F32, kind="Internal").ap()
        self.ar3o = nc.dram_tensor("ar3o", [128, 513], F32, kind="Internal").ap()
        self.repl = [[0, 1, 2, 3], [4, 5, 6, 7]]

        with ExitStack() as es:
            self._alloc(es)
            self._program()
            self._emit(es)
        return nc

    def _alloc(self, es):
        nc = self.nc
        e = es.enter_context

        def sb(name, shape, dt):
            return e(nc.sbuf_tensor(name, shape, dt))

        self.arw = sb("arw", [128, 65536], BF16)
        arw = self.arw
        self.wfcq = arw[:, 0:32768]
        self.w2t = arw[:, 32768:65536]
        self.wqf = arw[:, 0:8192]
        self.wkf = arw[:, 8192:16384]
        self.wvf = arw[:, 16384:24576]
        self.wof = arw[:, 24576:32768]
        self.rb = [arw[:, 32768 + i * 4096: 32768 + (i + 1) * 4096]
                   for i in range(8)]
        self.wfcq3 = arw[:, 32768:65536]
        self.w2t3 = arw[:, 0:32768]

        self.art = sb("art", [128, 18432], BF16)
        art = self.art
        artf_h = art.bitcast(F32)
        self.artf = artf_h
        self.AFW = 9216
        self.ATW = 18432
        self.h_b = [art[:, 0:4096], art[:, 4096:8192]]
        self.sqb = art[:, 8192:10240]
        self.gT_b = [art[:, 10240:14336], art[:, 14336:18432]]
        self.wnat = rap(artf_h, 0, [[self.AFW, 128], [1, 4096]])
        self.wnat_half = [rap(artf_h, s * 2048, [[self.AFW, 128], [1, 2048]])
                          for s in range(2)]
        self.qsc = rap(artf_h, 4096, [[self.AFW, 128], [1, 2048]])
        self.wqnat = art[:, 12288:16384]
        self.qtm_b = [art[:, 0:1024], art[:, 1024:2048]]
        self.ktm_b = [art[:, 2048:3072], art[:, 3072:4096]]
        self.vtm_b = [art[:, 4096:5120], art[:, 5120:6144]]
        self.qfm_b = [art[:, 6144:7168], art[:, 7168:8192]]
        self.kfm_b = [art[:, 8192:9216], art[:, 9216:10240]]
        self.bqb = art[:, 10240:11264]
        self.bkb = art[:, 11264:12288]
        self.bvb = art[:, 12288:13312]
        self.dhm = art[:, 0:1024]
        self.sio_tm = art[:, 1024:1152]
        self.etm = art[:, 1152:1280]
        self.vsc_b = [art[:, 2048:3072], art[:, 3072:4096]]
        self.attntm_b = [art[:, 4096:5120], art[:, 5120:6144]]
        self.attnfm_b = [art[:, 6144:7168], art[:, 7168:8192]]
        self.kvblocks = art[:, 9216:10240]
        def fview(c0, n):
            return rap(artf_h, c0, [[self.AFW, 128], [1, n]])
        self.siorec = fview(5120, 128)
        self.satm = fview(5248, 64)
        self.stot = fview(5312, 64)
        self.sitm_ld = fview(5376, 64)
        self.satm_ld = fview(5440, 64)
        self.kvr = fview(5504, 512)
        self.kvsb = self.kvr
        self.bob = fview(6016, 1024)
        self.qsi_sb = rap(artf_h, 7040, [[self.AFW, 16], [1, 1024]])
        self.kso_sb = rap(artf_h, 8064, [[self.AFW, 32], [1, 1024]])

        self.xbuf = [sb(f"xbuf{i}", [128, D], F32) for i in range(4)]
        self.xnbf = [sb(f"xnbf{i}", [128, D], BF16) for i in range(2)]
        self.xnT = [sb(f"xnT{i}", [128, D], BF16) for i in range(2)]
        self.c2h = sb("c2h", [128, D], F32)
        self.wbc = sb("wbc", [128, HF], BF16)
        self.ident = sb("ident", [128, 128], BF16)
        self.identf = sb("identf", [128, 128], F32)
        self.ones2 = sb("ones2", [128, 2], F32)
        self.c2row = sb("c2row", [16, 128], F32)
        self.zeros = sb("zeros", [128, 1], F32)
        self.stat = sb("stat", [128, 64], F32)
        self.sv = sb("sv", [128, 48], F32)
        self.svrow = sb("svrow", [1, 132], F32)
        self.c2sb = sb("c2sb", [128, 16], F32)
        self.cs2b = sb("cs2b", [128, D], F32)
        self.B1 = sb("B1", [128, 256], BF16)
        self.B2 = sb("B2", [128, 256], BF16)
        self.warr = sb("warr", [128, 16], F32)
        self.accA = sb("accA", [128, 16], F32)
        self.hd16 = sb("hd16", [16, 2], F32)
        self.cvec = sb("cvec", [16, 8], F32)
        self.zacc = sb("zacc", [16, 4], F32)
        self.zbc = sb("zbc", [128, 16], F32)
        self.cst = sb("cst", [128, 3], F32)
        self.onesrow = self.wbc[0:1, 0:128]
        self.brow = self.wbc[0:1, 128:3328]

        self.pfc = [e(nc.psum_tensor(f"pfc{i}", [128, 1024], F32))
                    for i in range(2)]
        self.ppj = e(nc.psum_tensor("ppj", [128, 1024], F32))
        self.ptr = [e(nc.psum_tensor(f"ptr{i}", [128, 1024], BF16))
                    for i in range(2)]

        for s in self.sems:
            s.h = e(nc.semaphore(s.name))

        self.pfcb = [p.bitcast(BF16) for p in self.pfc]
        self.pd_a = self.pfc[0][0:16, 0:512]
        self.pd_b = self.pfc[0][0:16, 512:1024]
        self.pws_q = self.pfc[1][0:32, :]
        self.pws_k = self.ppj[0:32, :]
        self.pkv = self.pfc[1][:, :]

    def _program(self):
        self._phase0()
        dst1 = self.y_out if _STOP_AFTER == 1 else self.x1_d
        wready = [self.ev_wfc, self.ev_w2, self.ev_c2h]
        self.ev_p1 = self._ffn_phase(self.x_in, dst1, self.wfcq, self.w2t,
                                     wready, init_war=None)
        if _STOP_AFTER == 1:
            self._final()
            return
        self._attention()
        if _STOP_AFTER == 2:
            self._final()
            return
        barrier = self.ev_p2[NT - 1]
        self._ffn_phase(self.x2_d, self.y_out, self.wfcq3, self.w2t3,
                        self.pf_fc, wready_pj=self.pf_pj,
                        src_ready=self.ev_p2, init_war=barrier)
        self._final()

    def _final(self):
        self.step("sync", lambda g: g.nop(), [self.LD.ev()])
        self.step("gpsimd", lambda g: g.nop(), [self.GD.ev()])

    def _phase0(self):
        v = self.dv
        v(lambda g: g.memset(self.zeros[:], 0.0))
        v(lambda g: g.memset(self.cst[:, 0:1], LNEPS))
        v(lambda g: g.memset(self.cst[:, 1:2], -1.0))
        v(lambda g: g.memset(self.cst[:, 2:3], 4.0 * LNEPS))
        self.po(lambda g: g.memset(self.ident[:], 0.0))
        self.ev_id = self.po(
            lambda g: g.affine_select(out=self.ident[:], in_=self.ident[:],
                                      compare_op=ALU.not_equal, fill=1.0,
                                      base=0, pattern=[[-1, 128]],
                                      channel_multiplier=1))
        self.po(lambda g: g.memset(self.identf[:], 0.0))
        self.ev_idf = self.po(
            lambda g: g.affine_select(out=self.identf[:], in_=self.identf[:],
                                      compare_op=ALU.not_equal, fill=1.0,
                                      base=0, pattern=[[-1, 128]],
                                      channel_multiplier=1))
        v(lambda g: g.memset(self.ones2[:], 0.0))
        v(lambda g: g.memset(self.ones2[0:64, 0:1], 1.0))
        self.ev_ones2 = v(lambda g: g.memset(self.ones2[64:128, 1:2], 1.0))
        self.ev_wbcl = self.dma("gpsimd", self.wbc[:],
                                rap(self.dm["lnw"], 0, [[0, 128], [1, HF]]))
        cs2b16 = self.cs2b.bitcast(BF16)
        c2h16 = self.c2h.bitcast(BF16)
        self.bbcq = [cs2b16[:, 0:1024], cs2b16[:, 1024:2048],
                     c2h16[:, 0:1024], c2h16[:, 1024:2048]]
        ev_b = None
        for q in range(4):
            ev_b = self.dma("gpsimd", self.bbcq[q],
                            rap(self.dm["lnb"], q * 1024,
                                [[0, 128], [1, 1024]]))
        self.ev_bbcl = ev_b

        self.ev_ppj = None
        self.ev_ptr = [None, None]
        self._trbatch = 0

        ntot = float(HF * D)
        arwf0 = self.arw.bitcast(F32)
        gB = rap(arwf0, 16384, [[32768, 128], [1, 4096]])
        ev_lA = self.dma("sync", self.wnat,
                         rap(self.dm["gwfc"], 0,
                             [[D, 128], [128 * D, 4], [1, D]]))
        ev_lB = self.dma("gpsimd", gB,
                         rap(self.dm["gwproj"], 0, [[HF, 128], [1, HF]]))
        ev_rA = v(lambda g: g.tensor_reduce(out=self.sv[:, 0:1],
                                            in_=self.wnat, axis=AX, op=ALU.add,
                                            apply_absolute_value=True), [ev_lA])
        ev_rB = v(lambda g: g.tensor_reduce(out=self.sv[:, 1:2],
                                            in_=gB, axis=AX, op=ALU.add,
                                            apply_absolute_value=True), [ev_lB])
        ev_s1 = self.dma("gpsimd", rap(self.gred2, 0, [[2, 128], [1, 2]]),
                         self.sv[:, 0:2], [ev_rA, ev_rB])
        qrow = rap(self.artf, 4096, [[self.AFW, 1], [1, 256]])
        ev_l1 = self.dma("gpsimd", qrow,
                         rap(self.gred2, 0, [[0, 1], [1, 256]]), [ev_s1])
        vAB = rap(self.artf, 4096, [[self.AFW, 1], [1, 2], [2, 128]])
        orow = rap(self.artf, 4096 + 256, [[self.AFW, 1], [1, 2]])
        ev_r2 = v(lambda g: g.tensor_reduce(out=orow, in_=vAB, axis=AX,
                                            op=ALU.add), [ev_l1])
        ev_si = self.dma("gpsimd", self.arg_i, orow, [ev_r2])
        ev_cc = self.step("gpsimd",
                          lambda g: g.collective_compute(
                              "AllReduce", ALU.add,
                              replica_groups=[[0, 1, 2, 3, 4, 5, 6, 7]],
                              ins=[self.arg_i.opt()], outs=[self.arg_o.opt()]),
                          [ev_si], [(self.CC, 1)])
        grow = rap(self.artf, 4096 + 260, [[self.AFW, 1], [1, 2]])
        trow = rap(self.artf, 4096 + 262, [[self.AFW, 1], [1, 2]])
        ev_lg = self.dma("gpsimd", grow, self.arg_o, [ev_cc])
        ev_t = v(lambda g: g.tensor_scalar(out=trow, in0=grow,
                                           scalar1=0.5 / ntot, scalar2=0.5e-6,
                                           op0=ALU.mult, op1=ALU.add), [ev_lg])
        ev_ts = self.dma("gpsimd", rap(self.gred2, 256, [[0, 1], [1, 2]]),
                         trow, [ev_t])
        ev_tl = self.dma("gpsimd", self.sv[:, 2:4],
                         rap(self.gred2, 256, [[0, 128], [1, 2]]), [ev_ts])
        ev_nthr = v(lambda g: g.tensor_scalar(out=self.sv[:, 4:6],
                                              in0=self.sv[:, 2:4],
                                              scalar1=-1.0, scalar2=None,
                                              op0=ALU.mult), [ev_tl])

        x_v0 = self.x_in.rearrange("(n p) c -> n p c", p=128)
        self.ev_xpre = [self.dma("sync", self.xbuf[t][:], x_v0[t], [])
                        for t in range(4)]

        self.ev_wfc = self._quantize(
            self.dm["wfc"], HF, D, self.wfcq, False,
            thr=self.sv[:, 2:3], nthr=self.sv[:, 4:5], ev_thr=ev_nthr,
            bufB_off=16384, prev_state=(ev_rB, ev_rB, None, None, None))
        fc_state = self._q_last
        self.ev_w2 = self._quantize(
            self.dm["wproj"], D, HF, self.w2t, True,
            thr=self.sv[:, 3:4], nthr=self.sv[:, 5:6], ev_thr=ev_nthr,
            prev_state=fc_state)
        self.ev_sqb_list = [self._q_last[2], self._q_last[3]]

        ev_c2t = self.step(
            "tensor",
            lambda g: g.transpose(self.ppj[0:16, 0:128], self.c2sb[:],
                                  self.identf[:]),
            self.evs_c2 + [self.ev_idf, self.ev_ppj], [(self.PE, 1)])
        ev_c2r = v(lambda g: g.tensor_copy(self.c2row[:],
                                           self.ppj[0:16, 0:128]),
                   [ev_c2t])
        self.ev_ppj = ev_c2r
        ev_c2s = self.dma("gpsimd",
                          rap(self.c2_d, 0, [[128, 16], [1, 128]]),
                          self.c2row[:], [ev_c2r])
        ev_c2l = self.dma("gpsimd", self.c2h[:],
                          rap(self.c2_d, 0, [[0, 128], [1, 1024]]),
                          [ev_c2s])
        ev_cs2l = self.dma("gpsimd", self.cs2b[:],
                           rap(self.c2_d, 1024, [[0, 128], [1, 1024]]),
                           [ev_c2s])
        self.ev_cs2b = ev_cs2l
        self.ev_c2h = v(lambda g: g.tensor_scalar(
            out=self.c2h[:], in0=self.c2h[:], scalar1=0.25, scalar2=None,
            op0=ALU.mult), [ev_c2l])
        self.ev_wfc_st = self.dma("gpsimd", self.wfcq_d[:], self.wfcq,
                                  [self.ev_wfc])
        self.ev_w2_st = self.dma("gpsimd", self.w2t_d[:], self.w2t,
                                 [self.ev_w2])
        self.ev_cast = [self.dma("gpsimd",
                                 rap(self.watt_d, w * D * D, [[D, D], [1, D]]),
                                 self.dm[nm], [])
                        for w, nm in enumerate(("wq", "wk", "wv", "wo"))]

    def _quantize(self, w_dram, nrow, ncol, fm_dst, is_proj, thr, nthr,
                  ev_thr, bufB_off=None, prev_state=None):
        v, a = self.dv, self.ac
        nchunk = nrow // 128
        per_load = 4096 // ncol
        nloads = nchunk // per_load
        arwf = self.arw.bitcast(F32)
        nbuf = 2 if bufB_off is not None else 1
        if prev_state is not None:
            prev_subA, prev_subB, prev_sg2, prev_sg3, prev_tr = prev_state
        else:
            prev_subA = prev_subB = prev_sg2 = prev_sg3 = prev_tr = None

        def buf_q(j, s2):
            if bufB_off is not None and j % 2 == 1:
                return rap(arwf, bufB_off + s2 * 1024,
                           [[32768, 128], [1, 1024]])
            return rap(self.artf, s2 * 1024, [[self.AFW, 128], [1, 1024]])

        def buf_half(j, h):
            if bufB_off is not None and j % 2 == 1:
                return rap(arwf, bufB_off + h * 2048,
                           [[32768, 128], [1, 2048]])
            return rap(self.artf, h * 2048, [[self.AFW, 128], [1, 2048]])

        def load_half(j, h):
            if per_load > 1:
                return rap(w_dram,
                           j * per_load * 128 * ncol + h * 2 * 128 * ncol,
                           [[ncol, 128], [128 * ncol, 2], [1, ncol]])
            return rap(w_dram, j * 128 * ncol + h * 2048,
                       [[ncol, 128], [1, 2048]])

        if is_proj:
            self.evs_c2 = []
        ev_chunk_done = prev_tr
        ev_fm = None
        prev_cs = None
        subs = {}
        signs = {}
        c2last = {}
        sgb = [self.art[:, 8192 + k * 1024:8192 + (k + 1) * 1024]
               for k in range(4)]
        wq_quarter = lambda qq: self.wqnat[:, qq * 1024:(qq + 1) * 1024]
        for j in range(nloads):
            w_ld = []
            if j - nbuf >= 0:
                w_ld.append(subs[(j - nbuf, 3)])
            elif bufB_off is None or j % 2 == 0:
                w_ld.append(prev_subA)
            else:
                w_ld.append(prev_subB)
            ldh = [self.dma("sync", buf_half(j, 0), load_half(j, 0), w_ld),
                   self.dma("gpsimd", buf_half(j, 1), load_half(j, 1), w_ld)]
            sub = []
            for s2 in range(4):
                wsl = buf_q(j, s2)
                s1v = sgb[2 * (s2 % 2)]
                s2v = sgb[2 * (s2 % 2) + 1]
                if s2 >= 2:
                    w_sc = [signs[(j, s2 - 2)]]
                elif j > 0:
                    w_sc = [signs[(j - 1, s2 + 2)]]
                else:
                    w_sc = [prev_sg2 if s2 == 0 else prev_sg3]
                a(lambda g, aa=wsl, o=s1v, nn=nthr:
                  g.activation(out=o, in_=aa, func=AF.Sign, bias=nn,
                               scale=1.0),
                  [ldh[s2 // 2]] + w_sc
                  + ([ev_thr] if (j == 0 and s2 == 0) else []))
                ev_s2 = a(lambda g, aa=wsl, o=s2v, tt=thr:
                          g.activation(out=o, in_=aa, func=AF.Sign, bias=tt,
                                       scale=1.0), [])
                subs[(j, s2)] = ev_s2
                ev_ad = self.po(lambda g, o=wq_quarter(s2), i1=s1v, i2=s2v:
                                g.tensor_tensor(out=o, in0=i1, in1=i2,
                                                op=ALU.add),
                                [ev_s2] + ([ev_chunk_done, prev_cs]
                                           if s2 == 0 else []))
                signs[(j, s2)] = ev_ad
                sub.append(ev_ad)
            ev_q = sub[-1]

            evq_r = list(sub)
            if is_proj:
                ev_cq = None
                ev_mul = None
                for q2 in range(4):
                    ev_cq = v(lambda g, qq2=q2: g.scalar_tensor_tensor(
                        out=rap(self.artf, 8192, [[self.AFW, 128], [1, 1024]]),
                        in0=self.wqnat[:, qq2 * 1024:(qq2 + 1) * 1024],
                        scalar=1.0, in1=self.bbcq[qq2],
                        op0=ALU.mult, op1=ALU.mult,
                        accum_out=self.sv[:, 8 + qq2:9 + qq2]),
                        [sub[q2], self.ev_bbcl, ev_cq])
                    ev_mul = v(lambda g, qq2=q2: g.scalar_tensor_tensor(
                        out=self.wqnat[:, qq2 * 1024:(qq2 + 1) * 1024],
                        in0=self.wqnat[:, qq2 * 1024:(qq2 + 1) * 1024],
                        scalar=1.0,
                        in1=self.wbc[:, qq2 * 1024:(qq2 + 1) * 1024],
                        op0=ALU.mult, op1=ALU.mult,
                        accum_out=self.sv[:, 16 + qq2:17 + qq2]),
                        [ev_cq, self.ev_wbcl])
                    evq_r[q2] = ev_mul
                c2last[j] = ev_cq
                self._c2scratch_last = ev_cq
                ev_cb = self.po(lambda g: g.tensor_add(
                    self.sv[:, 12:13], self.sv[:, 8:9], self.sv[:, 9:10]),
                    [ev_cq])
                ev_cb2 = self.po(lambda g: g.tensor_add(
                    self.sv[:, 13:14], self.sv[:, 10:11], self.sv[:, 11:12]),
                    [ev_cb])
                ev_cc = self.po(lambda g, jj=j: g.tensor_add(
                    self.c2sb[:, jj:jj + 1], self.sv[:, 12:13],
                    self.sv[:, 13:14]), [ev_cb2])
                self.evs_c2.append(ev_cc)
                ev_cs1 = self.po(lambda g: g.tensor_add(
                    self.sv[:, 20:21], self.sv[:, 16:17], self.sv[:, 17:18]),
                    [ev_mul])
                ev_cs2b = self.po(lambda g: g.tensor_add(
                    self.sv[:, 21:22], self.sv[:, 18:19], self.sv[:, 19:20]),
                    [ev_cs1])
                ev_cs = self.po(lambda g, jj=j: g.tensor_add(
                    self.c2sb[:, 8 + jj:9 + jj], self.sv[:, 20:21],
                    self.sv[:, 21:22]), [ev_cs2b])
                self.evs_c2.append(ev_cs)
                prev_cs = ev_cs
                ev_q = ev_mul

            percol = nrow
            last_tr = None
            for cc in range(per_load):
                rowchunk = j * per_load + cc
                nblk = ncol // 128
                for b0 in range(0, nblk, 8):
                    nb_in = min(8, nblk - b0)
                    par = self._trbatch % 2
                    self._trbatch += 1
                    pb = self.ptr[par]
                    quarter = cc if per_load > 1 else b0 // 8
                    for b in range(nb_in):
                        w0 = [evq_r[3], self.ev_id]
                        if b == 0:
                            w0.append(self.ev_ptr[par])
                        last_tr = self.tr(
                            pb[:, b * 128:(b + 1) * 128],
                            self.wqnat[:, cc * ncol + (b0 + b) * 128:
                                       cc * ncol + (b0 + b + 1) * 128], w0)
                    dst = rap(fm_dst, b0 * percol + rowchunk * 128,
                              [[65536, 128], [percol, nb_in], [1, 128]])
                    src = rap(pb[:, 0:1], 0, [[1024, 128], [128, nb_in],
                                              [1, 128]])
                    w_fm = [last_tr]
                    if j == 0 and cc == 0 and b0 == 0:
                        w_fm.append(prev_subB)
                    if par == 0:
                        ev_fm = self.ac(lambda g, d=dst, s=src: g.copy(d, s),
                                        w_fm)
                    else:
                        ev_fm = self.dv(lambda g, d=dst, s=src:
                                        g.tensor_copy(d, s), w_fm)
                    self.ev_ptr[par] = ev_fm
            ev_chunk_done = last_tr
        jA = nloads - nbuf
        self._q_last = (subs[(jA, 3)], subs[(nloads - 1, 3)],
                        signs[(nloads - 1, 2)], signs[(nloads - 1, 3)],
                        ev_chunk_done)
        return ev_fm

    def _ln(self, xb, dcols, slot, waits):
        st = self.stat
        mu = st[:, slot:slot + 1]
        rstd = st[:, slot + 1:slot + 2]
        t1 = st[:, slot + 2:slot + 3]
        t2 = st[:, slot + 3:slot + 4]
        q1 = st[:, slot + 4:slot + 5]
        q2 = st[:, slot + 5:slot + 6]
        p, v, a = self.po, self.dv, self.ac
        ev_s = v(lambda g: g.tensor_reduce(out=t1, in_=xb, axis=AX, op=ALU.add),
                 list(waits))
        sqsc = rap(self.artf, 5120, [[self.AFW, 128], [1, dcols]])
        ev_q2 = v(lambda g, s4=sqsc: g.scalar_tensor_tensor(
            out=s4, in0=xb, scalar=1.0, in1=xb,
            op0=ALU.mult, op1=ALU.mult, accum_out=t2), list(waits))
        inv = 1.0 / dcols
        ev_mu = p(lambda g: g.tensor_scalar(out=mu, in0=t1, scalar1=inv,
                                            scalar2=None, op0=ALU.mult), [ev_s])
        ev_e2 = p(lambda g: g.tensor_scalar(out=t2, in0=t2, scalar1=inv,
                                            scalar2=None, op0=ALU.mult), [ev_q2])
        ev_m2 = p(lambda g: g.tensor_mul(t1, mu, mu), [ev_mu])
        ev_var = p(lambda g: g.tensor_sub(t2, t2, t1), [ev_e2, ev_m2])
        ev_sd = a(lambda g: g.activation(out=t1, in_=t2, func=AF.Sqrt,
                                         bias=self.cst[:, 0:1], scale=1.0),
                  [ev_var])
        ev_rs = v(lambda g: g.reciprocal(rstd, t1), [ev_sd])
        return ev_rs, mu, rstd, ev_q2

    def _ffn_phase(self, src_ap, dst_ap, wfc, w2t, wready,
                   src_ready=None, init_war=None, wready_pj=None):
        if wready_pj is None:
            wready_pj = wready
        x_v = src_ap.rearrange("(n p) c -> n p c", p=128)
        d_v = dst_ap.rearrange("(n p) c -> n p c", p=128)
        ev_done = [None] * NT
        ev_hfree = [None, None]
        ev_gTfree = [None, None]
        ev_pfc = [None, None]
        fa = [None] * (NT + 1)
        fx = [None] * (NT + 1)
        hl = [None] * NT
        v, a = self.dv, self.ac
        st = self.stat

        def xslot(t):
            return 16 * (t % 2)

        def hslot(t):
            return 32 + 16 * (t % 2)

        flx = [None] * (NT + 1)

        def frontA(t):
            par = t % 4
            if t < 4 and src_ready is None and init_war is None:
                flx[t] = self.ev_xpre[t]
                return
            w = [ev_done[t - 4] if t >= 4 else init_war]
            if src_ready is not None:
                w.append(src_ready[t])
            flx[t] = self.dma("sync", self.xbuf[par][:], x_v[t], w)

        fsum = [None] * (NT + 1)

        def frontA_sum(t):
            par = t % 4
            ev_xld = flx[t]
            s = xslot(t)
            t1 = st[:, s + 2:s + 3]
            mu = st[:, s:s + 1]
            p = self.po
            ev_s = v(lambda g: g.tensor_reduce(out=t1, in_=self.xbuf[par][:],
                                               axis=AX, op=ALU.add), [ev_xld])
            ev_mu = p(lambda g: g.tensor_scalar(out=mu, in0=t1,
                                                scalar1=1.0 / D, scalar2=None,
                                                op0=ALU.mult), [ev_s])
            ev_xc = v(lambda g, pp=par: g.tensor_scalar(
                out=self.xnbf[t % 2][:], in0=self.xbuf[pp][:], scalar1=mu,
                scalar2=None, op0=ALU.subtract),
                [ev_mu, fx_war[t % 2]])
            ev_m2 = p(lambda g: g.tensor_mul(t1, mu, mu), [ev_mu])
            fsum[t] = (ev_m2, ev_xld)
            xcev[t] = ev_xc

        def frontA_stats(t):
            par = t % 4
            ev_m2, ev_xld = fsum[t]
            s = xslot(t)
            t1 = st[:, s + 2:s + 3]
            var = st[:, s + 3:s + 4]
            q1 = st[:, s + 4:s + 5]
            q2 = st[:, s + 5:s + 6]
            p = self.po
            sqsc = rap(self.artf, 4096, [[self.AFW, 128], [1, 1024]])
            ev_sq = v(lambda g, s=sqsc: g.scalar_tensor_tensor(
                out=s, in0=self.xbuf[par][:], scalar=1.0,
                in1=self.xbuf[par][:], op0=ALU.mult, op1=ALU.mult,
                accum_out=q1),
                [ev_xld, sdev[t - 2] if t >= 2 else None]
                + getattr(self, "ev_sqb_list", []))
            ev_e2 = p(lambda g: g.tensor_scalar(out=var, in0=q1,
                                                scalar1=1.0 / D, scalar2=None,
                                                op0=ALU.mult), [ev_sq])
            ev_var = p(lambda g: g.tensor_sub(var, var, t1),
                       [ev_e2, ev_m2])
            ev_pre = v(lambda g, pp=par: g.tensor_add(
                self.xbuf[pp][:], self.xbuf[pp][:], self.c2h[:]),
                [xcev[t], ev_sq, self.ev_c2h])
            preev[t] = ev_pre
            fa[t] = (ev_var, ev_sq, ev_xld)

        def finish_x(t, extra=()):
            s = xslot(t)
            rstd = st[:, s + 1:s + 2]
            var = st[:, s + 3:s + 4]
            sd = st[:, s + 6:s + 7]
            ev_sd = a(lambda g: g.activation(out=sd, in_=var, func=AF.Sqrt,
                                             bias=self.cst[:, 2:3], scale=4.0),
                      [fa[t][0]] + list(extra))
            sdev[t] = ev_sd
            ev_rs = v(lambda g: g.reciprocal(rstd, sd), [ev_sd])
            rsev[t] = ev_rs
            return ev_rs

        fx_war = [None, None]
        preev = [None] * NT
        bk = [None] * NT
        rsev = [None] * NT
        sdev = [None] * NT
        xcev = [None] * NT

        def xnT_tr(t, ev_xn):
            par2 = t % 2
            evt = None
            for b in range(8):
                w0 = [ev_xn, self.ev_id]
                if b == 0:
                    w0.append(self.ev_ptr[par2])
                evt = self.tr(self.ptr[par2][:, b * 128:(b + 1) * 128],
                              self.xnbf[par2][:, b * 128:(b + 1) * 128], w0)
            ev_xnT = v(lambda g, pp=par2: g.tensor_copy(
                self.xnT[pp][:], self.ptr[pp][:]), [evt])
            self.ev_ptr[par2] = ev_xnT
            fx_war[par2] = evt
            fx[t] = ev_xnT

        def fc_quarters(t, qlist):
            par2 = t % 2
            ev_xnT = fx[t]
            hs = st[:, hslot(t): hslot(t) + 4]
            ev_gelu = None
            for qq in qlist:
                pb = self.pfc[qq % 2]
                w0 = [ev_xnT, ev_pfc[qq % 2]] + list(wready)
                evm = None
                for dc in range(8):
                    for half in range(2):
                        col0 = dc * 4096 + qq * 1024 + half * 512
                        evm = self.mm(
                            pb[:, half * 512:(half + 1) * 512],
                            self.xnT[par2][:, dc * 128:(dc + 1) * 128],
                            wfc[:, col0:col0 + 512],
                            start=(dc == 0), stop=(dc == 7),
                            waits=w0 if (dc == 0 and half == 0) else [])
                rsx = st[:, xslot(t) + 1:xslot(t) + 2]
                ev_gelu = a(lambda g, pp=t % 2, q2=qq, pb2=pb, h2=hs, r3=rsx:
                            g.activation(
                                out=self.h_b[pp][:, q2 * 1024:(q2 + 1) * 1024],
                                in_=pb2[:], func=AF.Gelu, scale=r3,
                                accum_out=h2[:, q2:q2 + 1]),
                            [evm, rsev[t],
                             ev_hfree[t % 2] if qq == 0 else None])
                ev_pfc[qq % 2] = ev_gelu
            return ev_gelu

        def hln(t, ev_gelu, last):
            par2 = t % 2
            s = hslot(t)
            hs = st[:, s:s + 4]
            t1 = st[:, s + 4:s + 5]
            var = st[:, s + 5:s + 6]
            muh = st[:, s + 6:s + 7]
            rsh = st[:, s + 7:s + 8]
            q1 = st[:, s + 8:s + 9]
            q2s = st[:, s + 9:s + 10]
            sd = st[:, s + 10:s + 11]
            nr2 = st[:, s + 11:s + 12]
            p = self.po
            ev_hsum = v(lambda g: g.tensor_reduce(out=t1, in_=hs, axis=AX,
                                                  op=ALU.add), [ev_gelu])
            sqsc = rap(self.artf, 4096, [[self.AFW, 128], [1, 1024]])
            qq4 = [q1, q2s, st[:, s + 12:s + 13], st[:, s + 13:s + 14]]
            ev_sq2 = None
            for k4 in range(4):
                ev_sq2 = v(lambda g, kk=k4, s4=sqsc: g.scalar_tensor_tensor(
                    out=s4, in0=self.h_b[par2][:, kk * 1024:(kk + 1) * 1024],
                    scalar=1.0,
                    in1=self.h_b[par2][:, kk * 1024:(kk + 1) * 1024],
                    op0=ALU.mult, op1=ALU.mult, accum_out=qq4[kk]),
                    [ev_gelu if k4 == 0 else ev_sq2])
            ev_sa1 = p(lambda g: g.tensor_add(st[:, s + 14:s + 15],
                                              qq4[0], qq4[1]), [ev_sq2])
            ev_sa2 = p(lambda g: g.tensor_add(st[:, s + 15:s + 16],
                                              qq4[2], qq4[3]), [ev_sa1])
            ev_ss = p(lambda g: g.tensor_add(var, st[:, s + 14:s + 15],
                                             st[:, s + 15:s + 16]), [ev_sa2])
            ev_mu = p(lambda g: g.tensor_scalar(out=muh, in0=t1,
                                                scalar1=1.0 / HF, scalar2=None,
                                                op0=ALU.mult), [ev_hsum])
            ev_e2 = p(lambda g: g.tensor_scalar(out=var, in0=var,
                                                scalar1=1.0 / HF, scalar2=None,
                                                op0=ALU.mult), [ev_ss])
            ev_m2 = p(lambda g: g.tensor_mul(t1, muh, muh), [ev_mu])
            ev_var = p(lambda g: g.tensor_sub(var, var, t1), [ev_e2, ev_m2])
            ev_sd = a(lambda g: g.activation(out=sd, in_=var, func=AF.Sqrt,
                                             bias=self.cst[:, 0:1], scale=1.0),
                      [ev_var])
            if not last:
                finish_x(t + 1, extra=[ev_sd])
            ev_rs = v(lambda g: g.reciprocal(rsh, sd), [ev_sd])
            ev_nr2 = v(lambda g: g.tensor_scalar(out=nr2, in0=rsh,
                                                 scalar1=-0.25, scalar2=None,
                                                 op0=ALU.mult), [ev_rs])
            hl[t] = (ev_gelu, ev_mu, ev_nr2)
            return None

        def backA(t):
            par2 = t % 2
            ev_h, ev_muh, ev_nr2 = hl[t]
            ev_last_t = None
            evs = []
            for rnd in range(4):
                pb = self.ptr[par2 if rnd % 2 == 0 else 1 - par2][:, 0:1024]
                for b in range(8):
                    blk = rnd * 8 + b
                    w0 = [ev_h, self.ev_id]
                    if b == 0:
                        if rnd < 2:
                            w0.append(self.ev_ptr[par2] if rnd == 0
                                      else self.ev_ptr[1 - par2])
                        else:
                            w0.append(evs[rnd - 2])
                    ev_last_t = self.tr(
                        pb[:, b * 128:(b + 1) * 128],
                        self.h_b[par2][:, blk * 128:(blk + 1) * 128], w0)
                w1 = [ev_last_t, ev_gTfree[par2] if rnd == 0 else None]
                if rnd % 2 == 0:
                    ev_ev = a(lambda g, pp=par2, r2=rnd, pb2=pb: g.copy(
                        self.gT_b[pp][:, r2 * 1024:(r2 + 1) * 1024], pb2[:]),
                        w1)
                else:
                    ev_ev = v(lambda g, pp=par2, r2=rnd, pb2=pb: g.tensor_copy(
                        self.gT_b[pp][:, r2 * 1024:(r2 + 1) * 1024], pb2[:]),
                        w1)
                evs.append(ev_ev)
            self.ev_ptr[par2] = evs[2]
            self.ev_ptr[1 - par2] = evs[3]
            ev_hfree[par2] = ev_last_t
            bk[t] = (evs, ev_muh, ev_nr2)
            dbg = getattr(self, "_dbg", None)
            if dbg is not None and t == 0:
                ev_gTfree[par2] = self.dma("sync", dbg["gt"],
                                           self.gT_b[par2][:], [evs[3]])

        def back(t):
            par2 = t % 2
            evs_gT, ev_muh, ev_nr2 = bk[t]
            s = hslot(t)
            muh = st[:, s + 6:s + 7]
            nr2 = st[:, s + 11:s + 12]
            evm = None
            for hc in range(32):
                for half in range(2):
                    if hc % 8 == 0 and half == 0:
                        w0 = [evs_gT[hc // 8]]
                        if hc == 0:
                            w0 += [self.ev_ppj] + list(wready_pj)
                    else:
                        w0 = []
                    evm = self.mm(
                        self.ppj[:, half * 512:(half + 1) * 512],
                        self.gT_b[par2][:, hc * 128:(hc + 1) * 128],
                        w2t[:, hc * 1024 + half * 512:
                            hc * 1024 + half * 512 + 512],
                        start=(hc == 0), stop=(hc == 31),
                        waits=w0)
            ev_gTfree[par2] = evm
            self.last_proj_ev = evm
            ev_u1 = v(lambda g: g.scalar_tensor_tensor(
                out=self.ppj[:], in0=self.cs2b[:], scalar=muh,
                in1=self.ppj[:], op0=ALU.mult, op1=ALU.subtract),
                [evm, ev_muh, self.ev_cs2b])
            xb = self.xbuf[t % 4]
            ev_x2 = v(lambda g: g.scalar_tensor_tensor(
                out=xb[:], in0=self.ppj[:], scalar=nr2,
                in1=xb[:], op0=ALU.mult, op1=ALU.add),
                [ev_u1, ev_nr2, preev[t]])
            self.ev_ppj = ev_x2
            ev_done[t] = self.dma("sync", d_v[t], xb[:], [ev_x2])

        dbg = getattr(self, "_dbg", None)
        frontA(0)
        frontA_sum(0)
        frontA_stats(0)
        xnT_tr(0, xcev[0])
        ev_r0 = finish_x(0)
        frontA(1)
        frontA_sum(1)
        frontA_stats(1)
        xnT_tr(1, xcev[1])
        finish_x(1, extra=[ev_r0])
        for t in range(NT):
            if t + 2 < NT:
                frontA(t + 2)
                frontA_sum(t + 2)
            ev_g01 = fc_quarters(t, (0, 1))
            ev_g23 = fc_quarters(t, (2, 3))
            if dbg is not None and t == 0:
                ev_hfree[0] = self.dma("sync", dbg["h"], self.h_b[0][:],
                                       [ev_g23])
            if t >= 1:
                backA(t - 1)
            if t >= 1:
                back(t - 1)
            if t + 2 < NT:
                frontA_stats(t + 2)
            hln(t, ev_g23, last=(t == 0 or t + 1 >= NT))
            if t + 2 < NT:
                xnT_tr(t + 2, xcev[t + 2])
            if t == NT - 1:
                self.last_fc_ev = ev_g23
        backA(NT - 1)
        back(NT - 1)
        return ev_done

    def _attention(self):
        v, a = self.dv, self.ac
        ev_cast = self.ev_cast

        wmap = {"wq": 0, "wk": 1, "wv": 2, "wo": 3}

        def ldw_fm(dst, name):
            return [self.dma(
                "sync" if h == 0 else "gpsimd",
                dst[:, h * 4096:(h + 1) * 4096].rearrange(
                    "p (c m) -> p c m", m=D),
                rap(self.watt_d, wmap[name] * D * D + h * 4 * 128 * D,
                    [[D, 128], [128 * D, 4], [1, D]]),
                [self.last_fc_ev, self.ev_wfc_st, ev_cast[wmap[name]]])
                for h in range(2)]
        ev_wqkv = [ldw_fm(self.wqf, "wq"), ldw_fm(self.wkf, "wk"),
                   ldw_fm(self.wvf, "wv")]
        self.ev_onesrow = v(lambda g: g.memset(self.onesrow, 1.0))
        ev_bq = self.dma("gpsimd", self.brow[0:1, 0:1024],
                         self.dm["bq"], [self.last_proj_ev, self.ev_onesrow])
        ev_bk = self.dma("gpsimd", self.brow[0:1, 1024:2048],
                         self.dm["bk"], [ev_bq])
        ev_bv = self.dma("gpsimd", self.brow[0:1, 2048:3072],
                         self.dm["bv"], [ev_bk])

        x1_v = self.x1_d.rearrange("(n p) c -> n p c", p=128)
        qtm_v = self.qtm_d.rearrange("(n p) c -> n p c", p=128)
        ktm_v = self.ktm_d.rearrange("(n p) c -> n p c", p=128)
        vtm_v = self.vtm_d.rearrange("(n p) c -> n p c", p=128)

        ev_acc0 = v(lambda g: g.memset(self.accA[:], 0.0))
        ev_z0 = v(lambda g: g.memset(self.zacc[:], 0.0))

        ev_pfc = [None, None]
        ev_xfree = [self.ev_p1[NT - 2], self.ev_p1[NT - 1]]
        bufev = {}
        ev_accA = ev_acc0
        faA = [None] * (NT + 1)
        pendA = [None] * NT

        xnev = [None] * (NT + 1)
        trwar = [None, None]

        def frontA2_pre(t):
            par = t % 2
            ev_xl = self.dma("sync", self.xbuf[par][:], x1_v[t],
                             [self.ev_p1[t], ev_xfree[par]])
            ev_ln, mu, rstd, _ = self._ln(self.xbuf[par][:], D, 8 + 16 * par,
                                          [ev_xl])
            ev_xn = v(lambda g, pp=par, m=mu, r=rstd: g.tensor_scalar(
                out=self.xnbf[pp][:], in0=self.xbuf[pp][:], scalar1=m,
                scalar2=r, op0=ALU.subtract, op1=ALU.mult),
                [ev_ln, trwar[par]])
            ev_xfree[par] = ev_xn
            xnev[t] = ev_xn

        def frontA2_tr(t):
            par = t % 2
            evt = None
            for b in range(8):
                w0 = [xnev[t], self.ev_id]
                if b == 0:
                    w0.append(self.ev_ptr[par])
                evt = self.tr(self.ptr[par][:, b * 128:(b + 1) * 128],
                              self.xnbf[par][:, b * 128:(b + 1) * 128], w0)
            trwar[par] = evt
            ev_xnT = v(lambda g, pp=par: g.tensor_copy(self.xnT[pp][:],
                                                       self.ptr[pp][:]), [evt])
            self.ev_ptr[par] = ev_xnT
            faA[t] = ev_xnT

        frontA2_pre(0)
        frontA2_tr(0)
        frontA2_pre(1)
        frontA2_tr(1)
        frontA2_pre(2)
        for t in range(NT):
            par = t % 2
            ev_xnT = faA[t]
            evq_mm = evk_mm = evv_mm = None
            for mi, (pb, wf, boff) in enumerate(
                    ((self.pfc[0], self.wqf, 0), (self.pfc[1], self.wkf, 1024),
                     (self.ppj, self.wvf, 2048))):
                wseed = [ev_bv, self.ev_onesrow,
                         (ev_pfc[0], ev_pfc[1], self.ev_ppj)[mi]]
                for half in range(2):
                    self.mm(pb[:, half * 512:(half + 1) * 512],
                            self.onesrow[0:1, :],
                            self.brow[0:1, boff + half * 512:
                                      boff + half * 512 + 512],
                            start=True, stop=False,
                            waits=wseed if half == 0 else [])
                w0 = [ev_xnT] + ev_wqkv[mi]
                evm = None
                for dc in range(8):
                    for half in range(2):
                        evm = self.mm(
                            pb[:, half * 512:(half + 1) * 512],
                            self.xnT[par][:, dc * 128:(dc + 1) * 128],
                            wf[:, dc * 1024 + half * 512:
                               dc * 1024 + half * 512 + 512],
                            start=False, stop=(dc == 7),
                            waits=w0 if (dc == 0 and half == 0) else [])
                if mi == 0:
                    evq_mm = evm
                elif mi == 1:
                    evk_mm = evm
                else:
                    evv_mm = evm
            if t == NT - 1:
                self.last_qkv_ev = evv_mm
            ev_qs = a(lambda g, pp=par: g.activation(
                out=self.qtm_b[pp][:], in_=self.pfc[0][:], func=AF.Sigmoid),
                [evq_mm, bufev.get(("qtm", par))])
            ev_ks = a(lambda g, pp=par: g.activation(
                out=self.ktm_b[pp][:], in_=self.pfc[1][:], func=AF.Sigmoid),
                [evk_mm, bufev.get(("ktm", par))])
            ev_vt = v(lambda g, pp=par: g.tensor_copy(
                self.vtm_b[pp][:], self.ppj[:]),
                [evv_mm, bufev.get(("vtm", par))])
            ev_pfc[0], ev_pfc[1] = ev_qs, ev_ks
            self.ev_ppj = ev_vt
            bufev[("qtm", par)] = self.dma("sync", qtm_v[t],
                                           self.qtm_b[par][:], [ev_qs])
            bufev[("ktm", par)] = self.dma("sync", ktm_v[t],
                                           self.ktm_b[par][:], [ev_ks])
            bufev[("vtm", par)] = self.dma("sync", vtm_v[t],
                                           self.vtm_b[par][:], [ev_vt])
            pendA[t] = (ev_qs, ev_ks)

            tt = t - 1 if t >= 1 else None
            if t == NT - 1:
                backs = [t - 1, t]
            elif t >= 1:
                backs = [t - 1]
            else:
                backs = []
            for tb in backs:
                parb = tb % 2
                ev_qs_b, ev_ks_b = pendA[tb]
                evq = None
                for b in range(8):
                    w0 = [ev_qs_b, self.ev_id]
                    if b == 0:
                        w0.append(self.ev_ptr[parb])
                    evq = self.tr(self.ptr[parb][:, b * 128:(b + 1) * 128],
                                  self.qtm_b[parb][:, b * 128:(b + 1) * 128],
                                  w0)
                ev_qf = a(lambda g, pp=parb: g.copy(self.qfm_b[pp][:],
                                                    self.ptr[pp][:]),
                          [evq, bufev.get(("qfm", parb))])
                self.ev_ptr[parb] = ev_qf
                evk = None
                for b in range(8):
                    w0 = [ev_ks_b, self.ev_id]
                    if b == 0:
                        w0.append(self.ev_ptr[1 - parb])
                    evk = self.tr(self.ptr[1 - parb][:, b * 128:(b + 1) * 128],
                                  self.ktm_b[parb][:, b * 128:(b + 1) * 128],
                                  w0)
                ev_kf = v(lambda g, pp=parb: g.tensor_copy(self.kfm_b[pp][:],
                                                           self.ptr[1 - pp][:]),
                          [evk, bufev.get(("kfm", parb))])
                self.ev_ptr[1 - parb] = ev_kf
                ev_qr = v(lambda g, pp=parb: g.tensor_reduce(
                    out=self.sv[:, 16:24],
                    in_=self.qfm_b[pp][:].rearrange("p (c n) -> p c n", n=128),
                    axis=AX, op=ALU.add), [ev_qf, ev_accA])
                ev_a1 = self.po(lambda g: g.tensor_add(self.accA[:, 8:16],
                                                       self.accA[:, 8:16],
                                                       self.sv[:, 16:24]),
                                [ev_qr])
                ev_kr = v(lambda g, pp=parb: g.tensor_reduce(
                    out=self.sv[:, 24:32],
                    in_=self.kfm_b[pp][:].rearrange("p (c n) -> p c n", n=128),
                    axis=AX, op=ALU.add), [ev_kf, ev_a1])
                ev_accA = self.po(lambda g: g.tensor_add(self.accA[:, 0:8],
                                                         self.accA[:, 0:8],
                                                         self.sv[:, 24:32]),
                                  [ev_kr])
                fmoff = (tb // 4) * 4096 + (tb % 4) * 128
                bufev[("qfm", parb)] = self.dma(
                    "sync",
                    rap(self.qfm_d, fmoff,
                        [[8 * T, 128], [512, 8], [1, 128]]),
                    self.qfm_b[parb][:].rearrange("p (c n) -> p c n", n=128),
                    [ev_qf])
                bufev[("kfm", parb)] = self.dma(
                    "sync",
                    rap(self.kfm_d, fmoff,
                        [[8 * T, 128], [512, 8], [1, 128]]),
                    self.kfm_b[parb][:].rearrange("p (c n) -> p c n", n=128),
                    [ev_kf])
            if t + 2 < NT:
                frontA2_tr(t + 2)
            if t + 3 < NT:
                frontA2_pre(t + 3)

        ev_i = self.dma("gpsimd", self.ar1i[:], self.accA[:], [ev_accA])
        ev_cc1 = self.step("gpsimd",
                           lambda g: g.collective_compute(
                               "AllReduce", ALU.add, replica_groups=self.repl,
                               ins=[self.ar1i.opt()], outs=[self.ar1o.opt()]),
                           [ev_i], [(self.CC, 1)])
        ev_w1 = self.dma("gpsimd", self.warr[:], self.ar1o[:], [ev_cc1])
        ev_we = v(lambda g: g.tensor_scalar(out=self.warr[:], in0=self.warr[:],
                                            scalar1=EPS, scalar2=None,
                                            op0=ALU.add), [ev_w1])
        ev_b1 = self._build_B(self.B1, [ev_we])
        ev_c01 = self._perhead_c(ev_we, 0, 1)

        rb = self.rb
        si_sbf = self.xnT[0].bitcast(F32)
        sa_sbf = self.xnT[1].bitcast(F32)
        sist = [None] * NB
        sast = [None] * NB
        ldev = {}
        sioT_war = None
        ev_dhm_war = None
        ev_sst = None
        pdv = [(self.pd_a, self.pd_b),
               (self.pfc[0][32:48, 0:512], self.pfc[0][32:48, 512:1024])]
        pdfree = [ev_pfc[0], ev_pfc[0]]
        lds = {}
        pdmm = {}
        bfev = {}
        wsst = {"q": None, "k": None}

        def b_loads(u):
            pr = u % 2
            w_st = [self.ev_w2_st] if u < 2 else []
            lds[u] = (
                self.dma("sync", rb[0 + pr],
                         rap(self.qfm_d, u * 4096, [[8 * T, 128], [1, 4096]]),
                         [bufev[("qfm", 0)], bufev[("qfm", 1)],
                          ldev.get(("b0", pr))] + w_st),
                self.dma("sync", rb[2 + pr],
                         rap(self.kfm_d, u * 4096, [[8 * T, 128], [1, 4096]]),
                         [bufev[("kfm", 0)], bufev[("kfm", 1)],
                          ldev.get(("b2", pr))] + w_st),
                self.dma("gpsimd", rb[4 + pr],
                         rap(self.qtm_d, u * BT * D,
                             [[D, 128], [128 * D, 4], [1, D]]),
                         [bufev[("qtm", 0)], bufev[("qtm", 1)],
                          ldev.get(("b4", pr))] + w_st),
                self.dma("gpsimd", rb[6 + pr],
                         rap(self.ktm_d, u * BT * D,
                             [[D, 128], [128 * D, 4], [1, D]]),
                         [bufev[("ktm", 0)], bufev[("ktm", 1)],
                          ldev.get(("b6", pr))] + w_st))

        def b_pd(u):
            pr = u % 2
            pa, pb_ = pdv[u % 2]
            ev_lq, ev_lk = lds[u][0], lds[u][1]
            evm1 = evm2 = None
            for c in range(8):
                evm1 = self.mm(pa, self.B1[:, 32 * c:32 * c + 16],
                               rb[0 + pr][:, c * BT:(c + 1) * BT],
                               start=(c == 0), stop=(c == 7),
                               waits=[ev_lq, ev_b1, pdfree[u % 2]]
                               if c == 0 else [])
            for c in range(8):
                evm2 = self.mm(pb_, self.B1[:, 32 * c + 16:32 * c + 32],
                               rb[2 + pr][:, c * BT:(c + 1) * BT],
                               start=(c == 0), stop=(c == 7),
                               waits=[ev_lk, ev_b1] if c == 0 else [])
            pdmm[u] = (evm1, evm2)
            ldev[("b0", pr)] = evm1
            ldev[("b2", pr)] = evm2

        def b_chain(u):
            nonlocal ev_dhm_war, sioT_war
            evm1, evm2 = pdmm[u]
            pa, pb_ = pdv[u % 2]
            ev_d1 = v(lambda g, p1=pa: g.tensor_scalar(
                out=self.dhm[0:16, 0:512], in0=p1,
                scalar1=self.cvec[0:16, 0:1], scalar2=None, op0=ALU.add),
                [evm1, ev_c01, ev_dhm_war])
            ev_d2 = v(lambda g, p2=pb_: g.tensor_scalar(
                out=self.dhm[0:16, 512:1024], in0=p2,
                scalar1=self.cvec[0:16, 1:2], scalar2=None, op0=ALU.add),
                [evm2, ev_d1])
            pdfree[u % 2] = ev_d2
            evt = None
            for cc in range(4):
                w0 = [ev_d2, self.ev_id]
                if cc == 0:
                    w0.append(self.ev_ptr[0])
                evt = self.tr(self.ptr[0][:, 32 * cc:32 * cc + 16],
                              self.dhm[0:16, 128 * cc:128 * cc + 128], w0)
                evt = self.tr(self.ptr[0][:, 32 * cc + 16:32 * cc + 32],
                              self.dhm[0:16, 512 + 128 * cc:
                                       512 + 128 * cc + 128],
                              [self.ev_id])
            ev_dhm_war = evt
            ev_rec = v(lambda g: g.reciprocal(self.siorec,
                                              self.ptr[0][:, 0:128]),
                       [evt, sioT_war])
            self.ev_ptr[0] = ev_rec
            bfev[u] = v(lambda g: g.tensor_copy(self.sio_tm[:], self.siorec),
                        [ev_rec])
            sist[u] = v(lambda g, uu=u: g.tensor_copy(
                rap(si_sbf, uu * 64, [[512, 128], [16, 4], [1, 16]]),
                rap(self.siorec, 0, [[self.AFW, 128], [32, 4], [1, 16]])),
                [ev_rec, self.last_qkv_ev if u == 0 else None])

        def b_ws(u):
            nonlocal sioT_war
            pr = u % 2
            ev_bf = bfev[u]
            ev_lqt, ev_lkt = lds[u][2], lds[u][3]
            evmq = evmk = None
            for cc in range(4):
                lhs = self.sio_tm[:, 32 * cc:32 * cc + 32]
                for half in range(2):
                    evmq = self.mm(self.pws_q[:, half * 512:(half + 1) * 512],
                                   lhs,
                                   rb[4 + pr][:, cc * 1024 + half * 512:
                                               cc * 1024 + half * 512 + 512],
                                   start=(u == 0 and cc == 0),
                                   stop=(u == NB - 1 and cc == 3),
                                   waits=[ev_bf, ev_lqt, ev_pfc[1]]
                                   if (u == 0 and cc == 0 and half == 0)
                                   else ([ev_bf] if half == 0 else []))
                for half in range(2):
                    evmk = self.mm(self.pws_k[:, half * 512:(half + 1) * 512],
                                   lhs,
                                   rb[6 + pr][:, cc * 1024 + half * 512:
                                               cc * 1024 + half * 512 + 512],
                                   start=(u == 0 and cc == 0),
                                   stop=(u == NB - 1 and cc == 3),
                                   waits=[ev_bf, ev_lkt, self.ev_ppj]
                                   if (u == 0 and cc == 0 and half == 0)
                                   else ([ev_bf] if half == 0 else []))
            sioT_war = evmk
            ldev[("b4", pr)] = evmq
            ldev[("b6", pr)] = evmk
            return evmk

        b_loads(0)
        b_pd(0)
        for u in range(NB):
            if u + 1 < NB:
                b_loads(u + 1)
            b_chain(u)
            if u + 1 < NB:
                b_pd(u + 1)
            ev_last_ws = b_ws(u)

        ev_wo2 = ldw_fm(self.wof, "wo")

        ev_e1 = v(lambda g: g.tensor_copy(self.qsi_sb, self.pws_q[0:16, :]),
                  [ev_last_ws])
        ev_e2 = v(lambda g: g.tensor_copy(self.kso_sb, self.pws_k[:]),
                  [ev_last_ws, ev_e1])
        ev_i1 = self.dma("gpsimd", self.ar2i[:, 0:1024], self.qsi_sb, [ev_e1])
        ev_i2 = self.dma("gpsimd", self.ar2i[:, 1024:2048],
                         rap(self.kso_sb, 16 * self.AFW,
                             [[self.AFW, 16], [1, 1024]]), [ev_e2])
        ev_cc2 = self.step("gpsimd",
                           lambda g: g.collective_compute(
                               "AllReduce", ALU.add, replica_groups=self.repl,
                               ins=[self.ar2i.opt()], outs=[self.ar2o.opt()]),
                           [ev_i1, ev_i2], [(self.CC, 1)])
        ev_bl = [None]
        for side, base in ((0, 1024), (1, 0)):
            ev_t1 = self.dma("gpsimd", self.warr[0:64, side * 8:side * 8 + 8],
                             rap(self.ar2o, base, [[1, 64], [2112, 8]]),
                             [ev_cc2, ev_b1, ev_c01, ev_bl[-1]], slow=True)
            ev_t2 = self.dma("gpsimd", self.warr[64:128, side * 8:side * 8 + 8],
                             rap(self.ar2o, base + 2048 + 64,
                                 [[1, 64], [2112, 8]]),
                             [ev_t1], slow=True)
            ev_bl.append(ev_t2)
        ev_we2 = v(lambda g: g.tensor_scalar(out=self.warr[:], in0=self.warr[:],
                                             scalar1=EPS, scalar2=None,
                                             op0=ALU.add), [ev_bl[-1]])
        ev_b2 = self._build_B(self.B2, [ev_we2])
        ev_c23 = self._perhead_c(ev_we2, 2, 3)
        ev_nc = v(lambda g: g.tensor_scalar(out=self.cvec[0:16, 4:5],
                                            in0=self.cvec[0:16, 2:3],
                                            scalar1=-1.0, scalar2=None,
                                            op0=ALU.mult), [ev_c23])

        ev_dhm_warC = ev_dhm_war
        ev_kv = None
        ev_z = ev_z0
        ev_etm_war = None
        ev_vsc_war = [None, None]
        ev_e = None
        eev = {}
        etmev = {}

        def c_loads(u):
            pr = u % 2
            lds[u] = (
                self.dma("sync", rb[0 + pr],
                         rap(self.qfm_d, u * 4096, [[8 * T, 128], [1, 4096]]),
                         [ldev.get(("b0", pr))]),
                self.dma("sync", rb[2 + pr],
                         rap(self.kfm_d, u * 4096, [[8 * T, 128], [1, 4096]]),
                         [ldev.get(("b2", pr))]),
                self.dma("gpsimd", rb[4 + pr],
                         rap(self.ktm_d, u * BT * D,
                             [[D, 128], [128 * D, 4], [1, D]]),
                         [ldev.get(("b4", pr))]),
                self.dma("gpsimd", rb[6 + pr],
                         rap(self.vtm_d, u * BT * D,
                             [[D, 128], [128 * D, 4], [1, D]]),
                         [bufev[("vtm", 0)], bufev[("vtm", 1)],
                          ldev.get(("b6", pr))]))

        def c_pd(u):
            pr = u % 2
            pa, pb_ = pdv[u % 2]
            ev_lq, ev_lk = lds[u][0], lds[u][1]
            evm1 = evm2 = None
            for c in range(8):
                evm1 = self.mm(pa, self.B2[:, 32 * c:32 * c + 16],
                               rb[0 + pr][:, c * BT:(c + 1) * BT],
                               start=(c == 0), stop=(c == 7),
                               waits=[ev_lq, ev_b2, pdfree[u % 2]]
                               if c == 0 else [])
            for c in range(8):
                evm2 = self.mm(pb_, self.B2[:, 32 * c + 16:32 * c + 32],
                               rb[2 + pr][:, c * BT:(c + 1) * BT],
                               start=(c == 0), stop=(c == 7),
                               waits=[ev_lk, ev_b2] if c == 0 else [])
            pdmm[u] = (evm1, evm2)
            ldev[("b0", pr)] = evm1
            ldev[("b2", pr)] = evm2

        def c_chain(u):
            nonlocal ev_dhm_warC, ev_etm_war, ev_z, ev_e
            evm1, evm2 = pdmm[u]
            pa, pb_ = pdv[u % 2]
            ev_em = a(lambda g, p1=pa: g.activation(
                out=self.dhm[0:16, 512:1024], in_=p1, func=AF.Exp,
                scale=self.cst[0:16, 1:2], bias=self.cvec[0:16, 4:5]),
                [evm1, ev_nc, ev_dhm_warC])
            ev_cs = v(lambda g, p2=pb_: g.tensor_scalar(
                out=self.qsi_sb[0:16, 0:512], in0=p2,
                scalar1=self.cvec[0:16, 3:4], scalar2=1.0,
                op0=ALU.add, op1=ALU.min), [evm2, ev_c23, ev_i1])
            ev_cs2 = v(lambda g: g.tensor_scalar(
                out=self.qsi_sb[0:16, 0:512], in0=self.qsi_sb[0:16, 0:512],
                scalar1=-1.0, scalar2=None, op0=ALU.max), [ev_cs])
            ev_e = a(lambda g: g.activation(
                out=self.dhm[0:16, 0:512], in_=self.qsi_sb[0:16, 0:512],
                func=AF.Exp, accum_out=self.zacc[0:16, 1:2]), [ev_cs2, ev_em])
            eev[u] = ev_e
            pdfree[u % 2] = ev_e
            ev_z = v(lambda g: g.tensor_add(self.zacc[0:16, 0:1],
                                            self.zacc[0:16, 0:1],
                                            self.zacc[0:16, 1:2]),
                     [ev_e, ev_z])
            evt = None
            for cc in range(4):
                w0 = [ev_e, self.ev_id]
                if cc == 0:
                    w0.append(self.ev_ptr[0])
                evt = self.tr(self.ptr[0][:, 32 * cc:32 * cc + 16],
                              self.dhm[0:16, 128 * cc:128 * cc + 128], w0)
                evt = self.tr(self.ptr[0][:, 32 * cc + 16:32 * cc + 32],
                              self.dhm[0:16, 512 + 128 * cc:
                                       512 + 128 * cc + 128],
                              [self.ev_id])
            ev_dhm_warC = evt
            ev_etm = v(lambda g: g.tensor_copy(self.etm[:],
                                               self.ptr[0][:, 0:128]),
                       [evt, ev_etm_war])
            etmev[u] = ev_etm
            self.ev_ptr[0] = ev_etm
            ev_sa = None
            for cc in range(4):
                ev_sa = v(lambda g, c2=cc: g.tensor_scalar(
                    out=rap(self.satm, 16 * c2, [[self.AFW, 128], [1, 16]]),
                    in0=self.etm[:, 32 * c2 + 16:32 * c2 + 32],
                    scalar1=1.0, scalar2=None, op0=ALU.add), [ev_etm, ev_sa])
            ev_sar = v(lambda g: g.reciprocal(self.satm, self.satm), [ev_sa])
            sast[u] = v(lambda g, uu=u: g.tensor_copy(
                rap(sa_sbf, uu * 64, [[512, 128], [16, 4], [1, 16]]),
                rap(self.satm, 0, [[self.AFW, 128], [16, 4], [1, 16]])),
                [ev_sar, self.last_qkv_ev if u == 0 else None])
            ev_etm_war = ev_sar

        def c_kv(u):
            nonlocal ev_kv
            pr = u % 2
            ev_lkt, ev_lvt = lds[u][2], lds[u][3]
            ev_etm = etmev[u]
            ev_vsc = None
            for cc in range(4):
                ebc = rap(self.etm, 32 * cc,
                          [[self.ATW, 128], [1, 16], [0, 64]])
                ev_vsc = v(lambda g, c2=cc, eb=ebc, pp=pr:
                           g.tensor_tensor(
                               out=self.vsc_b[pp][:].rearrange(
                                   "p (h j) -> p h j", j=64),
                               in0=rb[6 + pp][:, c2 * 1024:
                                              (c2 + 1) * 1024].rearrange(
                                   "p (h j) -> p h j", j=64),
                               in1=eb, op=ALU.mult),
                           [ev_etm, ev_lvt,
                            ev_vsc if ev_vsc else ev_vsc_war[pr]])
                for p2 in range(8):
                    w0 = []
                    if p2 == 0:
                        w0 = [ev_vsc, ev_lkt]
                        if u == 0 and cc == 0:
                            w0.append(ev_e2)
                    ev_kv = self.mm(
                        self.pkv[:, 128 * p2:128 * (p2 + 1)],
                        rb[4 + pr][:, cc * 1024 + 128 * p2:
                                   cc * 1024 + 128 * (p2 + 1)],
                        self.vsc_b[pr][:, 128 * p2:128 * (p2 + 1)],
                        start=(u == 0 and cc == 0),
                        stop=(u == NB - 1 and cc == 3),
                        waits=w0)
            ev_vsc_war[pr] = ev_kv
            ldev[("b4", pr)] = ev_kv
            ldev[("b6", pr)] = ev_kv

        c_loads(0)
        c_pd(0)
        for u in range(NB):
            if u + 1 < NB:
                c_loads(u + 1)
            c_chain(u)
            if u + 1 < NB:
                c_pd(u + 1)
            c_kv(u)

        self.pf_pj = []
        for k in range(3):
            self.pf_pj.append(self.dma(
                "gpsimd", self.w2t3[:, k * 8192:(k + 1) * 8192],
                rap(self.w2t_d, k * 8192, [[32 * D, 128], [1, 8192]]),
                [self.last_qkv_ev, self.ev_w2_st]))

        self.pf_fc = []
        w_rb = [ldev[("b2", 0)], ldev[("b2", 1)], ldev[("b4", 0)],
                ldev[("b4", 1)], ldev[("b6", 0)], ldev[("b6", 1)]]
        for k in range(3):
            self.pf_fc.append(self.dma(
                "gpsimd", self.wfcq3[:, 8192 + k * 8192:8192 + (k + 1) * 8192],
                rap(self.wfcq_d, 8192 + k * 8192, [[8 * HF, 128], [1, 8192]]),
                w_rb if k == 0 else []))

        ev_kev1 = v(lambda g: g.tensor_copy(
            rap(self.kvsb, 0, [[self.AFW, 64], [64, 8], [1, 64]]),
            rap(self.pkv, 0, [[1024, 64], [128, 8], [1, 64]])), [ev_kv])
        ev_kev2 = v(lambda g: g.tensor_copy(
            rap(self.kvsb, self.AFW * 64, [[self.AFW, 64], [64, 8], [1, 64]]),
            rap(self.pkv, 1024 * 64 + 64, [[1024, 64], [128, 8], [1, 64]])),
            [ev_kev1])
        ev_i3a = self.dma("gpsimd", self.ar3i[:, 0:512], self.kvsb,
                          [ev_kev2])
        ev_i3b = self.dma("gpsimd", self.ar3i[0:16, 512:513],
                          self.zacc[0:16, 0:1], [ev_z], slow=True)
        ev_i3c = self.dma("gpsimd",
                          rap(self.ar3i, 16 * 513 + 512, [[513, 112], [1, 1]]),
                          self.zeros[0:112, 0:1], [ev_z], slow=True)
        ev_cc3 = self.step("gpsimd",
                           lambda g: g.collective_compute(
                               "AllReduce", ALU.add, replica_groups=self.repl,
                               ins=[self.ar3i.opt()], outs=[self.ar3o.opt()]),
                           [ev_i3a, ev_i3b, ev_i3c], [(self.CC, 1)])
        ev_kvl = self.dma("gpsimd", self.kvr, self.ar3o[:, 0:512], [ev_cc3])
        ev_zl = self.dma("gpsimd", self.zacc[0:16, 2:3],
                         self.ar3o[0:16, 512:513], [ev_cc3], slow=True)
        ev_kb0 = v(lambda g: g.memset(self.kvblocks[:], 0.0),
                   [ev_vsc_war[0], ev_vsc_war[1], ev_kev2, ev_i3a, sast[NB - 1]])
        ev_kb1 = v(lambda g: g.tensor_copy(
            rap(self.kvblocks, 0, [[self.ATW, 64], [128, 8], [1, 64]]),
            rap(self.kvr, 0, [[self.AFW, 64], [64, 8], [1, 64]])),
            [ev_kvl, ev_kb0])
        ev_kb2 = v(lambda g: g.tensor_copy(
            rap(self.kvblocks, self.ATW * 64 + 64,
                [[self.ATW, 64], [128, 8], [1, 64]]),
            rap(self.kvr, self.AFW * 64, [[self.AFW, 64], [64, 8], [1, 64]])),
            [ev_kb1])
        ev_zf = v(lambda g: g.reciprocal(self.zacc[0:16, 3:4],
                                         self.zacc[0:16, 2:3]), [ev_zl])
        ev_zf2 = v(lambda g: g.tensor_scalar(out=self.zacc[0:16, 3:4],
                                             in0=self.zacc[0:16, 3:4],
                                             scalar1=LSEQ, scalar2=None,
                                             op0=ALU.mult), [ev_zf])
        ev_zs = self.dma("gpsimd", self.zf_d[:], self.zacc[0:16, 3:4], [ev_zf2])
        ev_zbc = self.dma("gpsimd", self.zbc[:],
                          rap(self.zf_d, 0, [[0, 128], [1, 16]]), [ev_zs])
        bo16 = self.art[0:1, 12032:13056]
        ev_bo = self.dma("gpsimd", bo16, self.dm["bo"], [ev_kb2, ev_zbc])

        x1_v2 = self.x1_d.rearrange("(n p) c -> n p c", p=128)
        dstD = self.y_out if _STOP_AFTER == 2 else self.x2_d
        x2_v = dstD.rearrange("(n p) c -> n p c", p=128)
        ev_p2 = [None] * NT
        ev_attw = [ev_e, ev_kev2]
        ev_sc_prev = ev_e
        ev_woev = [ev_e2, ev_kev2]
        ev_stot_war = None
        ev_xD = [None, None]
        ev_wo_done = None
        for u in range(NB):
            pr = u % 2
            qfm_t = rb[0 + pr]
            ev_lq = self.dma("sync", qfm_t,
                             rap(self.qfm_d, u * 4096,
                                 [[8 * T, 128], [1, 4096]]),
                             [ldev.get(("b0", pr))])
            ldev[("b0", pr)] = None
            ev_s1 = v(lambda g, uu=u: g.tensor_mul(
                self.stot,
                rap(si_sbf, uu * 64, [[512, 128], [16, 4], [1, 16]]),
                rap(sa_sbf, uu * 64, [[512, 128], [16, 4], [1, 16]])),
                [sist[u], sast[u], ev_stot_war])
            zb = rap(self.zbc, 0, [[16, 128], [0, 4], [1, 16]])
            ev_s2 = v(lambda g, z=zb: g.tensor_tensor(
                out=rap(self.stot, 0, [[self.AFW, 128], [16, 4], [1, 16]]),
                in0=rap(self.stot, 0, [[self.AFW, 128], [16, 4], [1, 16]]),
                in1=z, op=ALU.mult), [ev_s1, ev_zbc])
            ev_sc_last = None
            for tb in range(4):
                t128 = u * 4 + tb
                pD = self.pfc[0]
                wo_pb = self.ppj if tb % 2 == 0 else self.pfc[1]
                evo = None
                for p2 in range(8):
                    evo = self.mm(
                        pD[:, 128 * p2:128 * (p2 + 1)],
                        qfm_t[:, p2 * BT + tb * 128: p2 * BT + (tb + 1) * 128],
                        self.kvblocks[:, 128 * p2:128 * (p2 + 1)],
                        start=True, stop=True,
                        waits=[ev_lq, ev_kb2, ev_sc_prev]
                        if p2 == 0 else [])
                    self.last_outmm_ev = evo
                for half in range(2):
                    self.mm(wo_pb[:, half * 512:(half + 1) * 512],
                            self.onesrow[0:1, :],
                            bo16[0:1, half * 512:half * 512 + 512],
                            start=True, stop=False,
                            waits=[ev_bo, ev_woev[tb % 2]]
                            if half == 0 else [])
                sb_ap = rap(self.stot, 16 * tb,
                            [[self.AFW, 128], [1, 16], [0, 64]])
                ev_sc = v(lambda g, pD2=pD, sbb=sb_ap, t2=tb:
                          g.tensor_tensor(
                              out=self.attntm_b[t2 % 2][:].rearrange(
                                  "p (h j) -> p h j", j=64),
                              in0=pD2[:].rearrange("p (h j) -> p h j", j=64),
                              in1=sbb, op=ALU.mult),
                          [evo, ev_s2, ev_attw[tb % 2]])
                ev_sc_prev = ev_sc
                ev_sc_last = ev_sc
                evt = None
                for b in range(8):
                    w0 = [ev_sc, self.ev_id]
                    if b == 0:
                        w0.append(self.ev_ptr[tb % 2])
                    evt = self.tr(self.ptr[tb % 2][:, b * 128:(b + 1) * 128],
                                  self.attntm_b[tb % 2][:, b * 128:
                                                        (b + 1) * 128], w0)
                ev_attw[tb % 2] = evt
                ev_af = a(lambda g, t2=tb: g.copy(self.attnfm_b[t2 % 2][:],
                                                  self.ptr[t2 % 2][:]), [evt])
                self.ev_ptr[tb % 2] = ev_af
                evw = None
                for c in range(8):
                    for half in range(2):
                        evw = self.mm(
                            wo_pb[:, half * 512:(half + 1) * 512],
                            self.attnfm_b[tb % 2][:, c * 128:(c + 1) * 128],
                            self.wof[:, c * 1024 + half * 512:
                                     c * 1024 + half * 512 + 512],
                            start=False, stop=(c == 7),
                            waits=([ev_af] + (ev_wo2 if t128 == 0 else []))
                            if (c == 0 and half == 0) else [])
                ev_wo_done = evw
                ev_xl = self.dma("sync", self.xbuf[tb % 2][:], x1_v2[t128],
                                 [ev_xD[tb % 2]])
                ev_u = v(lambda g, t2=tb, wp=wo_pb: g.scalar_tensor_tensor(
                    out=self.xbuf[t2 % 2][:], in0=wp[:], scalar=1.0,
                    in1=self.xbuf[t2 % 2][:], op0=ALU.mult, op1=ALU.add),
                    [evw, ev_xl])
                ev_woev[tb % 2] = ev_u
                ev_st = self.dma("sync", x2_v[t128],
                                 self.xbuf[tb % 2][:], [ev_u])
                ev_xD[tb % 2] = ev_st
                ev_p2[t128] = ev_st
            ev_stot_war = ev_sc_last
        self.ev_p2 = ev_p2
        self.ev_wo_done = ev_wo_done
        self.ev_ppj = ev_p2[NT - 1]
        self.pf_fc.append(self.dma(
            "sync", self.wfcq3[:, 0:8192],
            rap(self.wfcq_d, 0, [[8 * HF, 128], [1, 8192]]),
            [self.last_outmm_ev, self.ev_wfc_st]))
        self.pf_pj.append(self.dma(
            "gpsimd", self.w2t3[:, 24576:32768],
            rap(self.w2t_d, 24576, [[32 * D, 128], [1, 8192]]),
            [ev_wo_done]))

    def _build_B(self, B, waits):
        v = self.dv
        ev = v(lambda g: g.memset(B[:], 0.0), list(waits))
        for c in range(8):
            for side in range(2):
                sc = side * 8 + c
                col = 32 * c + 16 * side + 2 * c
                ev = v(lambda g, b=B, s=sc, cl=col: g.tensor_copy(
                    b[0:64, cl:cl + 1], self.warr[0:64, s:s + 1]), [ev])
                ev = v(lambda g, b=B, s=sc, cl=col: g.tensor_copy(
                    b[64:128, cl + 1:cl + 2], self.warr[64:128, s:s + 1]),
                    [ev])
        return ev

    def _perhead_c(self, ev_warr_eps, col_a, col_b):
        v = self.dv
        evm = self.step(
            "tensor",
            lambda g: g.matmul(self.ppj[0:16, 0:2], self.warr[:],
                               self.ones2[:], start=True, stop=True,
                               skip_group_check=True),
            [ev_warr_eps, self.ev_ones2, self.ev_ppj], [(self.PE, 1)])
        ev_ev = v(lambda g: g.tensor_copy(self.hd16[0:16, 0:2],
                                          self.ppj[0:16, 0:2]), [evm])
        self.ev_ppj = ev_ev
        ev_st = self.dma("gpsimd", self.ph_d[:], self.hd16[0:16, 0:2], [ev_ev])
        ev = ev_st
        for i, dstcol in enumerate((col_a, col_b)):
            ev_l = self.dma("gpsimd", self.cvec[0:16, 6:7],
                            rap(self.ph_d, i * 16, [[1, 16], [1, 1]]),
                            [ev, ev_st])
            ev = v(lambda g, dc=dstcol: g.tensor_scalar(
                out=self.cvec[0:16, dc:dc + 1], in0=self.cvec[0:16, 6:7],
                scalar1=EPS, scalar2=None, op0=ALU.mult), [ev_l])
        return ev

    def _emit(self, es):
        block = es.enter_context(self.nc.Block())

        def mkrun(ename):
            def run(eng_obj):
                seen = {}
                for waits, emit, incs in self.q[ename]:
                    for s, vv in waits:
                        if seen.get(s.name, 0) < vv:
                            eng_obj.wait_ge(s.h, vv)
                            seen[s.name] = vv
                    inst = emit(eng_obj)
                    for s, k in incs:
                        inst.then_inc(s.h, k)
            return run

        block.sync(mkrun("sync"))
        block.gpsimd(mkrun("gpsimd"))
        block.tensor(mkrun("tensor"))
        block.vector(mkrun("vector"))
        block.scalar(mkrun("scalar"))


_CACHE = {}


def _build():
    if "nc" not in _CACHE:
        _CACHE["nc"] = Builder().build()
    return _CACHE["nc"]


def kernel(**inputs):
    xs = np.ascontiguousarray(
        np.asarray(inputs["L0_attn_x"], dtype=np.float32).reshape(8 * T, D))
    com = {
        "wq": np.ascontiguousarray(np.asarray(inputs["L0_attn_Wq"], np.float32)),
        "wk": np.ascontiguousarray(np.asarray(inputs["L0_attn_Wk"], np.float32)),
        "wv": np.ascontiguousarray(np.asarray(inputs["L0_attn_Wv"], np.float32)),
        "wo": np.ascontiguousarray(np.asarray(inputs["L0_attn_Wo"], np.float32)),
        "bq": np.ascontiguousarray(np.asarray(inputs["L0_attn_bq"], np.float32).reshape(1, D)),
        "bk": np.ascontiguousarray(np.asarray(inputs["L0_attn_bk"], np.float32).reshape(1, D)),
        "bv": np.ascontiguousarray(np.asarray(inputs["L0_attn_bv"], np.float32).reshape(1, D)),
        "bo": np.ascontiguousarray(np.asarray(inputs["L0_attn_bo"], np.float32).reshape(1, D)),
        "wfc": np.ascontiguousarray(np.asarray(inputs["L0_ffn_W_fc"], np.float32)),
        "lnw": np.ascontiguousarray(np.asarray(inputs["L0_ffn_ln_w"], np.float32).reshape(1, HF)),
        "lnb": np.ascontiguousarray(np.asarray(inputs["L0_ffn_ln_b"], np.float32).reshape(1, HF)),
        "wproj": np.ascontiguousarray(np.asarray(inputs["L0_ffn_W_proj"], np.float32)),
    }
    nc = _build()
    in_maps = [dict(x=np.ascontiguousarray(xs[i * T:(i + 1) * T]),
                    gwfc=np.ascontiguousarray(com["wfc"][i * 512:(i + 1) * 512]),
                    gwproj=np.ascontiguousarray(com["wproj"][i * 128:(i + 1) * 128]),
                    **com)
               for i in range(8)]
    res = run_bass_kernel_spmd(nc, in_maps, core_ids=list(range(8)))
    out = np.concatenate([np.asarray(res.results[i]["y"]) for i in range(8)],
                         axis=0)
    return out.reshape(2, 16384, D).astype(np.float32)


# version 31
# speedup vs baseline: 1.1303x; 1.1303x over previous
import numpy as np
from contextlib import ExitStack

import concourse.bass as bass
import concourse.mybir as mybir
from concourse.bass_utils import run_bass_kernel_spmd

F32 = mybir.dt.float32
BF16 = mybir.dt.bfloat16
AF = mybir.ActivationFunctionType
ALU = mybir.AluOpType
AX = mybir.AxisListType.X

T = 4096
NT = T // 128
D = 1024
HF = 4096
EPS = 1e-6
LNEPS = 1e-5
LSEQ = 16384.0
NB = 8
BT = 512

ENGINES = ("sync", "gpsimd", "tensor", "vector", "scalar")

_STOP_AFTER = 0


class Sem:
    def __init__(self, name):
        self.name, self.n, self.h = name, 0, None

    def ev(self):
        return (self, self.n)


def rap(view, extra_off, dims):
    if isinstance(view, bass.AP):
        return bass.AP(view.tensor, view.offset + extra_off, dims)
    return bass.AP(view, extra_off, dims)


class Builder:
    def __init__(self):
        self.nc = bass.Bass("TRN2", target_bir_lowering=False)
        self.q = {e: [] for e in ENGINES}
        self.LD = Sem("LD")
        self.GD = Sem("GD")
        self.CC = Sem("CC")
        self.PE = Sem("PE")
        self.DV = Sem("DV")
        self.AC = Sem("AC")
        self.PO = Sem("PO")
        self.sems = [self.LD, self.GD, self.CC, self.PE, self.DV, self.AC,
                     self.PO]

    def step(self, eng, emit, waits=(), incs=()):
        waits = [w for w in waits if w is not None]
        for s, v in waits:
            assert v <= s.n, f"wait on future event {s.name}:{v} > {s.n}"
        self.q[eng].append((waits, emit, list(incs)))
        for s, k in incs:
            s.n += k
        return incs[0][0].ev() if incs else None

    def dma(self, eng, out, in_, waits=(), slow=False):
        sem = self.GD if eng == "gpsimd" else self.LD
        return self.step(
            eng,
            lambda g, o=out, i=in_, sl=slow: g.dma_start(
                out=o, in_=i, allow_slow_non_contiguous=sl),
            waits, [(sem, 16)])

    def mm(self, out, lhsT, rhs, start, stop, waits=()):
        return self.step(
            "tensor",
            lambda g, o=out, l=lhsT, r=rhs, s0=start, s1=stop:
            g.matmul(o, l, r, start=s0, stop=s1, skip_group_check=True),
            waits, [(self.PE, 1)])

    def tr(self, out, in_, waits=()):
        k = in_.shape[0]
        return self.step(
            "tensor",
            lambda g, o=out, i=in_, kk=k:
            g.transpose(o, i, self.ident[0:kk, 0:kk]),
            waits, [(self.PE, 1)])

    def dv(self, emit, waits=()):
        return self.step("vector", emit, waits, [(self.DV, 1)])

    def ac(self, emit, waits=()):
        return self.step("scalar", emit, waits, [(self.AC, 1)])

    def po(self, emit, waits=()):
        return self.step("gpsimd", emit, waits, [(self.PO, 1)])

    def build(self):
        nc = self.nc
        dm = {}
        self.x_in = nc.dram_tensor("x", [T, D], F32, kind="ExternalInput").ap()
        for nm in ("wq", "wk", "wv", "wo"):
            dm[nm] = nc.dram_tensor(nm, [D, D], F32, kind="ExternalInput").ap()
        for nm in ("bq", "bk", "bv", "bo"):
            dm[nm] = nc.dram_tensor(nm, [1, D], F32, kind="ExternalInput").ap()
        dm["wfc"] = nc.dram_tensor("wfc", [HF, D], F32, kind="ExternalInput").ap()
        dm["gwfc"] = nc.dram_tensor("gwfc", [HF // 8, D], F32,
                                    kind="ExternalInput").ap()
        dm["gwproj"] = nc.dram_tensor("gwproj", [D // 8, HF], F32,
                                      kind="ExternalInput").ap()
        dm["lnw"] = nc.dram_tensor("lnw", [1, HF], F32, kind="ExternalInput").ap()
        dm["lnb"] = nc.dram_tensor("lnb", [1, HF], F32, kind="ExternalInput").ap()
        dm["wproj"] = nc.dram_tensor("wproj", [D, HF], F32,
                                     kind="ExternalInput").ap()
        self.dm = dm
        self.y_out = nc.dram_tensor("y", [T, D], F32, kind="ExternalOutput").ap()

        self.x1_d = nc.dram_tensor("x1d", [T, D], F32, kind="Internal").ap()
        self.x2_d = nc.dram_tensor("x2d", [T, D], F32, kind="Internal").ap()
        self.qfm_d = nc.dram_tensor("qfmd", [128, 8 * T], BF16, kind="Internal").ap()
        self.kfm_d = nc.dram_tensor("kfmd", [128, 8 * T], BF16, kind="Internal").ap()
        self.qtm_d = nc.dram_tensor("qtmd", [T, D], BF16, kind="Internal").ap()
        self.ktm_d = nc.dram_tensor("ktmd", [T, D], BF16, kind="Internal").ap()
        self.vtm_d = nc.dram_tensor("vtmd", [T, D], BF16, kind="Internal").ap()
        self.si_d = nc.dram_tensor("sid", [T, 16], F32, kind="Internal").ap()
        self.sa_d = nc.dram_tensor("sad", [T, 16], F32, kind="Internal").ap()
        self.wfcq_d = nc.dram_tensor("wfcqd", [128, 8 * HF], BF16,
                                     kind="Internal").ap()
        self.watt_d = nc.dram_tensor("wattd", [4 * D, D], BF16,
                                     kind="Internal").ap()
        self.w2t_d = nc.dram_tensor("w2td", [128, 32 * D], BF16,
                                    kind="Internal").ap()
        self.gred_d = nc.dram_tensor("gredd", [128, 1], F32, kind="Internal").ap()
        self.gred2 = nc.dram_tensor("gred2", [1, 512], F32, kind="Internal").ap()
        self.arg_i = nc.dram_tensor("argi", [1, 2], F32, kind="Internal").ap()
        self.arg_o = nc.dram_tensor("argo", [1, 2], F32, kind="Internal").ap()
        self.c2_d = nc.dram_tensor("c2d", [128, 16], F32, kind="Internal").ap()
        self.zf_d = nc.dram_tensor("zfd", [16, 1], F32, kind="Internal").ap()
        self.ph_d = nc.dram_tensor("phd", [16, 2], F32, kind="Internal").ap()
        self.ar1i = nc.dram_tensor("ar1i", [128, 16], F32, kind="Internal").ap()
        self.ar1o = nc.dram_tensor("ar1o", [128, 16], F32, kind="Internal").ap()
        self.ar2i = nc.dram_tensor("ar2i", [16, 2048], F32, kind="Internal").ap()
        self.ar2o = nc.dram_tensor("ar2o", [16, 2048], F32, kind="Internal").ap()
        self.ar3i = nc.dram_tensor("ar3i", [128, 513], F32, kind="Internal").ap()
        self.ar3o = nc.dram_tensor("ar3o", [128, 513], F32, kind="Internal").ap()
        self.repl = [[0, 1, 2, 3], [4, 5, 6, 7]]

        with ExitStack() as es:
            self._alloc(es)
            self._program()
            self._emit(es)
        return nc

    def _alloc(self, es):
        nc = self.nc
        e = es.enter_context

        def sb(name, shape, dt):
            return e(nc.sbuf_tensor(name, shape, dt))

        self.arw = sb("arw", [128, 65536], BF16)
        arw = self.arw
        self.wfcq = arw[:, 0:32768]
        self.w2t = arw[:, 32768:65536]
        self.wqf = arw[:, 0:8192]
        self.wkf = arw[:, 8192:16384]
        self.wvf = arw[:, 16384:24576]
        self.wof = arw[:, 24576:32768]
        self.rb = [arw[:, 32768 + i * 4096: 32768 + (i + 1) * 4096]
                   for i in range(8)]
        self.wfcq3 = arw[:, 32768:65536]
        self.w2t3 = arw[:, 0:32768]

        self.art = sb("art", [128, 18432], BF16)
        art = self.art
        artf_h = art.bitcast(F32)
        self.artf = artf_h
        self.AFW = 9216
        self.ATW = 18432
        self.h_b = [art[:, 0:4096], art[:, 4096:8192]]
        self.sqb = art[:, 8192:10240]
        self.gT_b = [art[:, 10240:14336], art[:, 14336:18432]]
        self.wnat = rap(artf_h, 0, [[self.AFW, 128], [1, 4096]])
        self.wnat_half = [rap(artf_h, s * 2048, [[self.AFW, 128], [1, 2048]])
                          for s in range(2)]
        self.qsc = rap(artf_h, 4096, [[self.AFW, 128], [1, 2048]])
        self.wqnat = art[:, 12288:16384]
        self.qtm_b = [art[:, 0:1024], art[:, 1024:2048]]
        self.ktm_b = [art[:, 2048:3072], art[:, 3072:4096]]
        self.vtm_b = [art[:, 4096:5120], art[:, 5120:6144]]
        self.qfm_b = [art[:, 6144:7168], art[:, 7168:8192]]
        self.kfm_b = [art[:, 8192:9216], art[:, 9216:10240]]
        self.bqb = art[:, 10240:11264]
        self.bkb = art[:, 11264:12288]
        self.bvb = art[:, 12288:13312]
        self.dhm = art[:, 0:1024]
        self.sio_tm = art[:, 1024:1152]
        self.etm = art[:, 1152:1280]
        self.vsc_b = [art[:, 2048:3072], art[:, 3072:4096]]
        self.attntm_b = [art[:, 4096:5120], art[:, 5120:6144]]
        self.attnfm_b = [art[:, 6144:7168], art[:, 7168:8192]]
        self.kvblocks = art[:, 9216:10240]
        def fview(c0, n):
            return rap(artf_h, c0, [[self.AFW, 128], [1, n]])
        self.siorec = fview(5120, 128)
        self.satm = fview(5248, 64)
        self.stot = fview(5312, 64)
        self.sitm_ld = fview(5376, 64)
        self.satm_ld = fview(5440, 64)
        self.kvr = fview(5504, 512)
        self.kvsb = self.kvr
        self.bob = fview(6016, 1024)
        self.qsi_sb = rap(artf_h, 7040, [[self.AFW, 16], [1, 1024]])
        self.kso_sb = rap(artf_h, 8064, [[self.AFW, 32], [1, 1024]])

        self.xbuf = [sb(f"xbuf{i}", [128, D], F32) for i in range(4)]
        self.xnbf = [sb(f"xnbf{i}", [128, D], BF16) for i in range(2)]
        self.xnT = [sb(f"xnT{i}", [128, D], BF16) for i in range(2)]
        self.c2h = sb("c2h", [128, D], F32)
        self.wbc = sb("wbc", [128, HF], BF16)
        self.ident = sb("ident", [128, 128], BF16)
        self.identf = sb("identf", [128, 128], F32)
        self.ones2 = sb("ones2", [128, 2], F32)
        self.c2row = sb("c2row", [16, 128], F32)
        self.zeros = sb("zeros", [128, 1], F32)
        self.stat = sb("stat", [128, 64], F32)
        self.sv = sb("sv", [128, 48], F32)
        self.svrow = sb("svrow", [1, 132], F32)
        self.c2sb = sb("c2sb", [128, 16], F32)
        self.cs2b = sb("cs2b", [128, D], F32)
        self.B1 = sb("B1", [128, 256], BF16)
        self.B2 = sb("B2", [128, 256], BF16)
        self.warr = sb("warr", [128, 16], F32)
        self.accA = sb("accA", [128, 16], F32)
        self.hd16 = sb("hd16", [16, 2], F32)
        self.cvec = sb("cvec", [16, 8], F32)
        self.zacc = sb("zacc", [16, 4], F32)
        self.zbc = sb("zbc", [128, 16], F32)
        self.cst = sb("cst", [128, 3], F32)
        self.onesrow = self.wbc[0:1, 0:128]
        self.brow = self.wbc[0:1, 128:3328]

        self.pfc = [e(nc.psum_tensor(f"pfc{i}", [128, 1024], F32))
                    for i in range(2)]
        self.ppj = e(nc.psum_tensor("ppj", [128, 1024], F32))
        self.ptr = [e(nc.psum_tensor(f"ptr{i}", [128, 1024], BF16))
                    for i in range(2)]

        for s in self.sems:
            s.h = e(nc.semaphore(s.name))

        self.pfcb = [p.bitcast(BF16) for p in self.pfc]
        self.pd_a = self.pfc[0][0:16, 0:512]
        self.pd_b = self.pfc[0][0:16, 512:1024]
        self.pws_q = self.pfc[1][0:32, :]
        self.pws_k = self.ppj[0:32, :]
        self.pkv = self.pfc[1][:, :]

    def _program(self):
        self._phase0()
        dst1 = self.y_out if _STOP_AFTER == 1 else self.x1_d
        wready = [self.ev_wfc, self.ev_w2, self.ev_c2h]
        self.ev_p1 = self._ffn_phase(self.x_in, dst1, self.wfcq, self.w2t,
                                     wready, init_war=None)
        if _STOP_AFTER == 1:
            self._final()
            return
        self._attention()
        if _STOP_AFTER == 2:
            self._final()
            return
        barrier = self.ev_p2[NT - 1]
        self._ffn_phase(self.x2_d, self.y_out, self.wfcq3, self.w2t3,
                        self.pf_fc, wready_pj=self.pf_pj,
                        src_ready=self.ev_p2, init_war=barrier)
        self._final()

    def _final(self):
        self.step("sync", lambda g: g.nop(), [self.LD.ev()])
        self.step("gpsimd", lambda g: g.nop(), [self.GD.ev()])

    def _phase0(self):
        v = self.dv
        e0 = v(lambda g: g.memset(self.zeros[:], 0.0))
        e0 = v(lambda g: g.memset(self.cst[:, 0:1], LNEPS), [e0])
        e0 = v(lambda g: g.memset(self.cst[:, 1:2], -1.0), [e0])
        e0 = v(lambda g: g.memset(self.cst[:, 2:3], 4.0 * LNEPS), [e0])
        p0 = self.po(lambda g: g.memset(self.ident[:], 0.0))
        self.ev_id = self.po(
            lambda g: g.affine_select(out=self.ident[:], in_=self.ident[:],
                                      compare_op=ALU.not_equal, fill=1.0,
                                      base=0, pattern=[[-1, 128]],
                                      channel_multiplier=1), [p0])
        p1 = self.po(lambda g: g.memset(self.identf[:], 0.0), [self.ev_id])
        self.ev_idf = self.po(
            lambda g: g.affine_select(out=self.identf[:], in_=self.identf[:],
                                      compare_op=ALU.not_equal, fill=1.0,
                                      base=0, pattern=[[-1, 128]],
                                      channel_multiplier=1), [p1])
        e1 = v(lambda g: g.memset(self.ones2[:], 0.0), [e0])
        e1 = v(lambda g: g.memset(self.ones2[0:64, 0:1], 1.0), [e1])
        self.ev_ones2 = v(lambda g: g.memset(self.ones2[64:128, 1:2], 1.0),
                          [e1])
        self.ev_wbcl = self.dma("gpsimd", self.wbc[:],
                                rap(self.dm["lnw"], 0, [[0, 128], [1, HF]]))
        cs2b16 = self.cs2b.bitcast(BF16)
        c2h16 = self.c2h.bitcast(BF16)
        self.bbcq = [cs2b16[:, 0:1024], cs2b16[:, 1024:2048],
                     c2h16[:, 0:1024], c2h16[:, 1024:2048]]
        ev_b = None
        for q in range(4):
            ev_b = self.dma("gpsimd", self.bbcq[q],
                            rap(self.dm["lnb"], q * 1024,
                                [[0, 128], [1, 1024]]))
        self.ev_bbcl = ev_b

        self.ev_ppj = None
        self.ev_ptr = [None, None]
        self._trbatch = 0

        ntot = float(HF * D)
        arwf0 = self.arw.bitcast(F32)
        gB = rap(arwf0, 16384, [[32768, 128], [1, 4096]])
        ev_lA = self.dma("sync", self.wnat,
                         rap(self.dm["gwfc"], 0,
                             [[D, 128], [128 * D, 4], [1, D]]))
        ev_lB = self.dma("gpsimd", gB,
                         rap(self.dm["gwproj"], 0, [[HF, 128], [1, HF]]))
        ev_rA = v(lambda g: g.tensor_reduce(out=self.sv[:, 0:1],
                                            in_=self.wnat, axis=AX, op=ALU.add,
                                            apply_absolute_value=True), [ev_lA])
        ev_rB = v(lambda g: g.tensor_reduce(out=self.sv[:, 1:2],
                                            in_=gB, axis=AX, op=ALU.add,
                                            apply_absolute_value=True), [ev_lB])
        ev_s1 = self.dma("gpsimd", rap(self.gred2, 0, [[2, 128], [1, 2]]),
                         self.sv[:, 0:2], [ev_rA, ev_rB])
        qrow = rap(self.artf, 4096, [[self.AFW, 1], [1, 256]])
        ev_l1 = self.dma("gpsimd", qrow,
                         rap(self.gred2, 0, [[0, 1], [1, 256]]), [ev_s1])
        vAB = rap(self.artf, 4096, [[self.AFW, 1], [1, 2], [2, 128]])
        orow = rap(self.artf, 4096 + 256, [[self.AFW, 1], [1, 2]])
        ev_r2 = v(lambda g: g.tensor_reduce(out=orow, in_=vAB, axis=AX,
                                            op=ALU.add), [ev_l1])
        ev_si = self.dma("gpsimd", self.arg_i, orow, [ev_r2])
        ev_cc = self.step("gpsimd",
                          lambda g: g.collective_compute(
                              "AllReduce", ALU.add,
                              replica_groups=[[0, 1, 2, 3, 4, 5, 6, 7]],
                              ins=[self.arg_i.opt()], outs=[self.arg_o.opt()]),
                          [ev_si], [(self.CC, 1)])
        grow = rap(self.artf, 4096 + 260, [[self.AFW, 1], [1, 2]])
        trow = rap(self.artf, 4096 + 262, [[self.AFW, 1], [1, 2]])
        ev_lg = self.dma("gpsimd", grow, self.arg_o, [ev_cc])
        ev_t = v(lambda g: g.tensor_scalar(out=trow, in0=grow,
                                           scalar1=0.5 / ntot, scalar2=0.5e-6,
                                           op0=ALU.mult, op1=ALU.add), [ev_lg])
        ev_ts = self.dma("gpsimd", rap(self.gred2, 256, [[0, 1], [1, 2]]),
                         trow, [ev_t])
        ev_tl = self.dma("gpsimd", self.sv[:, 2:4],
                         rap(self.gred2, 256, [[0, 128], [1, 2]]), [ev_ts])
        ev_nthr = v(lambda g: g.tensor_scalar(out=self.sv[:, 4:6],
                                              in0=self.sv[:, 2:4],
                                              scalar1=-1.0, scalar2=None,
                                              op0=ALU.mult), [ev_tl])

        x_v0 = self.x_in.rearrange("(n p) c -> n p c", p=128)
        self.ev_xpre = [self.dma("sync", self.xbuf[t][:], x_v0[t], [])
                        for t in range(4)]

        self.ev_wfc = self._quantize(
            self.dm["wfc"], HF, D, self.wfcq, False,
            thr=self.sv[:, 2:3], nthr=self.sv[:, 4:5], ev_thr=ev_nthr,
            bufB_off=16384, prev_state=(ev_rB, ev_rB, None, None, None))
        fc_state = self._q_last
        self.ev_w2 = self._quantize(
            self.dm["wproj"], D, HF, self.w2t, True,
            thr=self.sv[:, 3:4], nthr=self.sv[:, 5:6], ev_thr=ev_nthr,
            prev_state=fc_state)
        self.ev_sqb_list = [self._q_last[2], self._q_last[3]]

        ev_c2t = self.step(
            "tensor",
            lambda g: g.transpose(self.ppj[0:16, 0:128], self.c2sb[:],
                                  self.identf[:]),
            self.evs_c2 + [self.ev_idf, self.ev_ppj], [(self.PE, 1)])
        ev_c2r = v(lambda g: g.tensor_copy(self.c2row[:],
                                           self.ppj[0:16, 0:128]),
                   [ev_c2t])
        self.ev_ppj = ev_c2r
        ev_c2s = self.dma("gpsimd",
                          rap(self.c2_d, 0, [[128, 16], [1, 128]]),
                          self.c2row[:], [ev_c2r])
        ev_c2l = self.dma("gpsimd", self.c2h[:],
                          rap(self.c2_d, 0, [[0, 128], [1, 1024]]),
                          [ev_c2s])
        ev_cs2l = self.dma("gpsimd", self.cs2b[:],
                           rap(self.c2_d, 1024, [[0, 128], [1, 1024]]),
                           [ev_c2s])
        self.ev_cs2b = ev_cs2l
        self.ev_c2h = v(lambda g: g.tensor_scalar(
            out=self.c2h[:], in0=self.c2h[:], scalar1=0.25, scalar2=None,
            op0=ALU.mult), [ev_c2l])
        self.ev_wfc_st = self.dma("gpsimd", self.wfcq_d[:], self.wfcq,
                                  [self.ev_wfc])
        self.ev_w2_st = self.dma("gpsimd", self.w2t_d[:], self.w2t,
                                 [self.ev_w2])
        self.ev_cast = [self.dma("gpsimd",
                                 rap(self.watt_d, w * D * D, [[D, D], [1, D]]),
                                 self.dm[nm], [])
                        for w, nm in enumerate(("wq", "wk", "wv", "wo"))]

    def _quantize(self, w_dram, nrow, ncol, fm_dst, is_proj, thr, nthr,
                  ev_thr, bufB_off=None, prev_state=None):
        v, a = self.dv, self.ac
        nchunk = nrow // 128
        per_load = 4096 // ncol
        nloads = nchunk // per_load
        arwf = self.arw.bitcast(F32)
        nbuf = 2 if bufB_off is not None else 1
        if prev_state is not None:
            prev_subA, prev_subB, prev_sg2, prev_sg3, prev_tr = prev_state
        else:
            prev_subA = prev_subB = prev_sg2 = prev_sg3 = prev_tr = None

        def buf_q(j, s2):
            if bufB_off is not None and j % 2 == 1:
                return rap(arwf, bufB_off + s2 * 1024,
                           [[32768, 128], [1, 1024]])
            return rap(self.artf, s2 * 1024, [[self.AFW, 128], [1, 1024]])

        def buf_half(j, h):
            if bufB_off is not None and j % 2 == 1:
                return rap(arwf, bufB_off + h * 2048,
                           [[32768, 128], [1, 2048]])
            return rap(self.artf, h * 2048, [[self.AFW, 128], [1, 2048]])

        def load_half(j, h):
            if per_load > 1:
                return rap(w_dram,
                           j * per_load * 128 * ncol + h * 2 * 128 * ncol,
                           [[ncol, 128], [128 * ncol, 2], [1, ncol]])
            return rap(w_dram, j * 128 * ncol + h * 2048,
                       [[ncol, 128], [1, 2048]])

        if is_proj:
            self.evs_c2 = []
        ev_chunk_done = prev_tr
        ev_fm = None
        prev_cs = None
        subs = {}
        signs = {}
        c2last = {}
        sgb = [self.art[:, 8192 + k * 1024:8192 + (k + 1) * 1024]
               for k in range(4)]
        wq_quarter = lambda qq: self.wqnat[:, qq * 1024:(qq + 1) * 1024]
        for j in range(nloads):
            w_ld = []
            if j - nbuf >= 0:
                w_ld.append(subs[(j - nbuf, 3)])
            elif bufB_off is None or j % 2 == 0:
                w_ld.append(prev_subA)
            else:
                w_ld.append(prev_subB)
            ldh = [self.dma("sync", buf_half(j, 0), load_half(j, 0), w_ld),
                   self.dma("gpsimd", buf_half(j, 1), load_half(j, 1), w_ld)]
            sub = []
            for s2 in range(4):
                wsl = buf_q(j, s2)
                s1v = sgb[2 * (s2 % 2)]
                s2v = sgb[2 * (s2 % 2) + 1]
                if s2 >= 2:
                    w_sc = [signs[(j, s2 - 2)]]
                elif j > 0:
                    w_sc = [signs[(j - 1, s2 + 2)]]
                else:
                    w_sc = [prev_sg2 if s2 == 0 else prev_sg3]
                a(lambda g, aa=wsl, o=s1v, nn=nthr:
                  g.activation(out=o, in_=aa, func=AF.Sign, bias=nn,
                               scale=1.0),
                  [ldh[s2 // 2]] + w_sc
                  + ([ev_thr] if (j == 0 and s2 == 0) else []))
                ev_s2 = a(lambda g, aa=wsl, o=s2v, tt=thr:
                          g.activation(out=o, in_=aa, func=AF.Sign, bias=tt,
                                       scale=1.0), [])
                subs[(j, s2)] = ev_s2
                ev_ad = self.po(lambda g, o=wq_quarter(s2), i1=s1v, i2=s2v:
                                g.tensor_tensor(out=o, in0=i1, in1=i2,
                                                op=ALU.add),
                                [ev_s2] + ([ev_chunk_done, prev_cs]
                                           if s2 == 0 else []))
                signs[(j, s2)] = ev_ad
                sub.append(ev_ad)
            ev_q = sub[-1]

            evq_r = list(sub)
            if is_proj:
                ev_cq = None
                ev_mul = None
                for q2 in range(4):
                    ev_cq = v(lambda g, qq2=q2: g.scalar_tensor_tensor(
                        out=rap(self.artf, 8192, [[self.AFW, 128], [1, 1024]]),
                        in0=self.wqnat[:, qq2 * 1024:(qq2 + 1) * 1024],
                        scalar=1.0, in1=self.bbcq[qq2],
                        op0=ALU.mult, op1=ALU.mult,
                        accum_out=self.sv[:, 8 + qq2:9 + qq2]),
                        [sub[q2], self.ev_bbcl, ev_cq])
                    ev_mul = v(lambda g, qq2=q2: g.scalar_tensor_tensor(
                        out=self.wqnat[:, qq2 * 1024:(qq2 + 1) * 1024],
                        in0=self.wqnat[:, qq2 * 1024:(qq2 + 1) * 1024],
                        scalar=1.0,
                        in1=self.wbc[:, qq2 * 1024:(qq2 + 1) * 1024],
                        op0=ALU.mult, op1=ALU.mult,
                        accum_out=self.sv[:, 16 + qq2:17 + qq2]),
                        [ev_cq, self.ev_wbcl])
                    evq_r[q2] = ev_mul
                c2last[j] = ev_cq
                self._c2scratch_last = ev_cq
                ev_cb = self.po(lambda g: g.tensor_add(
                    self.sv[:, 12:13], self.sv[:, 8:9], self.sv[:, 9:10]),
                    [ev_cq])
                ev_cb2 = self.po(lambda g: g.tensor_add(
                    self.sv[:, 13:14], self.sv[:, 10:11], self.sv[:, 11:12]),
                    [ev_cb])
                ev_cc = self.po(lambda g, jj=j: g.tensor_add(
                    self.c2sb[:, jj:jj + 1], self.sv[:, 12:13],
                    self.sv[:, 13:14]), [ev_cb2])
                self.evs_c2.append(ev_cc)
                ev_cs1 = self.po(lambda g: g.tensor_add(
                    self.sv[:, 20:21], self.sv[:, 16:17], self.sv[:, 17:18]),
                    [ev_mul])
                ev_cs2b = self.po(lambda g: g.tensor_add(
                    self.sv[:, 21:22], self.sv[:, 18:19], self.sv[:, 19:20]),
                    [ev_cs1])
                ev_cs = self.po(lambda g, jj=j: g.tensor_add(
                    self.c2sb[:, 8 + jj:9 + jj], self.sv[:, 20:21],
                    self.sv[:, 21:22]), [ev_cs2b])
                self.evs_c2.append(ev_cs)
                prev_cs = ev_cs
                ev_q = ev_mul

            percol = nrow
            last_tr = None
            for cc in range(per_load):
                rowchunk = j * per_load + cc
                nblk = ncol // 128
                for b0 in range(0, nblk, 8):
                    nb_in = min(8, nblk - b0)
                    par = self._trbatch % 2
                    self._trbatch += 1
                    pb = self.ptr[par]
                    quarter = cc if per_load > 1 else b0 // 8
                    for b in range(nb_in):
                        w0 = [evq_r[3], self.ev_id]
                        if b == 0:
                            w0.append(self.ev_ptr[par])
                        last_tr = self.tr(
                            pb[:, b * 128:(b + 1) * 128],
                            self.wqnat[:, cc * ncol + (b0 + b) * 128:
                                       cc * ncol + (b0 + b + 1) * 128], w0)
                    dst = rap(fm_dst, b0 * percol + rowchunk * 128,
                              [[65536, 128], [percol, nb_in], [1, 128]])
                    src = rap(pb[:, 0:1], 0, [[1024, 128], [128, nb_in],
                                              [1, 128]])
                    w_fm = [last_tr]
                    if j == 0 and cc == 0 and b0 == 0:
                        w_fm.append(prev_subB)
                    if par == 0:
                        ev_fm = self.ac(lambda g, d=dst, s=src: g.copy(d, s),
                                        w_fm)
                    else:
                        ev_fm = self.dv(lambda g, d=dst, s=src:
                                        g.tensor_copy(d, s), w_fm)
                    self.ev_ptr[par] = ev_fm
            ev_chunk_done = last_tr
        jA = nloads - nbuf
        self._q_last = (subs[(jA, 3)], subs[(nloads - 1, 3)],
                        signs[(nloads - 1, 2)], signs[(nloads - 1, 3)],
                        ev_chunk_done)
        return ev_fm

    def _ln(self, xb, dcols, slot, waits):
        st = self.stat
        mu = st[:, slot:slot + 1]
        rstd = st[:, slot + 1:slot + 2]
        t1 = st[:, slot + 2:slot + 3]
        t2 = st[:, slot + 3:slot + 4]
        q1 = st[:, slot + 4:slot + 5]
        q2 = st[:, slot + 5:slot + 6]
        p, v, a = self.po, self.dv, self.ac
        ev_s = v(lambda g: g.tensor_reduce(out=t1, in_=xb, axis=AX, op=ALU.add),
                 list(waits))
        ha = dcols // 2
        ev_q1 = a(lambda g: g.activation(out=self.sqb[:, 0:ha], in_=xb[:, 0:ha],
                                         func=AF.Square, accum_out=q1),
                  list(waits))
        ev_q2 = a(lambda g: g.activation(out=self.sqb[:, 0:ha],
                                         in_=xb[:, ha:dcols],
                                         func=AF.Square, accum_out=q2), [ev_q1])
        ev_ss = p(lambda g: g.tensor_add(t2, q1, q2), [ev_q2])
        inv = 1.0 / dcols
        ev_mu = p(lambda g: g.tensor_scalar(out=mu, in0=t1, scalar1=inv,
                                            scalar2=None, op0=ALU.mult), [ev_s])
        ev_e2 = p(lambda g: g.tensor_scalar(out=t2, in0=t2, scalar1=inv,
                                            scalar2=None, op0=ALU.mult), [ev_ss])
        ev_m2 = p(lambda g: g.tensor_mul(t1, mu, mu), [ev_mu])
        ev_var = p(lambda g: g.tensor_sub(t2, t2, t1), [ev_e2, ev_m2])
        ev_sd = a(lambda g: g.activation(out=t1, in_=t2, func=AF.Sqrt,
                                         bias=self.cst[:, 0:1], scale=1.0),
                  [ev_var])
        ev_rs = v(lambda g: g.reciprocal(rstd, t1), [ev_sd])
        return ev_rs, mu, rstd, ev_q2

    def _ffn_phase(self, src_ap, dst_ap, wfc, w2t, wready,
                   src_ready=None, init_war=None, wready_pj=None):
        if wready_pj is None:
            wready_pj = wready
        x_v = src_ap.rearrange("(n p) c -> n p c", p=128)
        d_v = dst_ap.rearrange("(n p) c -> n p c", p=128)
        ev_done = [None] * NT
        ev_hfree = [None, None]
        ev_gTfree = [None, None]
        ev_pfc = [None, None]
        fa = [None] * (NT + 1)
        fx = [None] * (NT + 1)
        hl = [None] * NT
        v, a = self.dv, self.ac
        st = self.stat

        def xslot(t):
            return 16 * (t % 2)

        def hslot(t):
            return 32 + 16 * (t % 2)

        flx = [None] * (NT + 1)

        def frontA(t):
            par = t % 4
            if t < 4 and src_ready is None and init_war is None:
                flx[t] = self.ev_xpre[t]
                return
            w = [ev_done[t - 4] if t >= 4 else init_war]
            if src_ready is not None:
                w.append(src_ready[t])
            flx[t] = self.dma("sync", self.xbuf[par][:], x_v[t], w)

        fsum = [None] * (NT + 1)

        def frontA_sum(t):
            par = t % 4
            ev_xld = flx[t]
            s = xslot(t)
            t1 = st[:, s + 2:s + 3]
            mu = st[:, s:s + 1]
            p = self.po
            ev_s = v(lambda g: g.tensor_reduce(out=t1, in_=self.xbuf[par][:],
                                               axis=AX, op=ALU.add), [ev_xld])
            ev_mu = p(lambda g: g.tensor_scalar(out=mu, in0=t1,
                                                scalar1=1.0 / D, scalar2=None,
                                                op0=ALU.mult), [ev_s])
            ev_xc = v(lambda g, pp=par: g.tensor_scalar(
                out=self.xnbf[t % 2][:], in0=self.xbuf[pp][:], scalar1=mu,
                scalar2=None, op0=ALU.subtract),
                [ev_mu, fx_war[t % 2]])
            ev_m2 = p(lambda g: g.tensor_mul(t1, mu, mu), [ev_mu])
            fsum[t] = (ev_m2, ev_xld)
            xcev[t] = ev_xc

        def frontA_stats(t):
            par = t % 4
            ev_m2, ev_xld = fsum[t]
            s = xslot(t)
            t1 = st[:, s + 2:s + 3]
            var = st[:, s + 3:s + 4]
            q1 = st[:, s + 4:s + 5]
            q2 = st[:, s + 5:s + 6]
            p = self.po
            ev_q1 = a(lambda g: g.activation(out=self.sqb[:, 0:512],
                                             in_=self.xbuf[par][:, 0:512],
                                             func=AF.Square, accum_out=q1),
                      [ev_xld] + getattr(self, "ev_sqb_list", []))
            ev_q2 = a(lambda g: g.activation(out=self.sqb[:, 0:512],
                                             in_=self.xbuf[par][:, 512:1024],
                                             func=AF.Square, accum_out=q2),
                      [ev_q1])
            ev_ss = p(lambda g: g.tensor_add(var, q1, q2),
                      [ev_q2, sdev[t - 2] if t >= 2 else None])
            ev_e2 = p(lambda g: g.tensor_scalar(out=var, in0=var,
                                                scalar1=1.0 / D, scalar2=None,
                                                op0=ALU.mult), [ev_ss])
            ev_var = p(lambda g: g.tensor_sub(var, var, t1),
                       [ev_e2, ev_m2])
            ev_pre = v(lambda g, pp=par: g.tensor_add(
                self.xbuf[pp][:], self.xbuf[pp][:], self.c2h[:]),
                [xcev[t], ev_q2, self.ev_c2h])
            preev[t] = ev_pre
            fa[t] = (ev_var, ev_q2, ev_xld)

        def finish_x(t, extra=()):
            s = xslot(t)
            rstd = st[:, s + 1:s + 2]
            var = st[:, s + 3:s + 4]
            sd = st[:, s + 6:s + 7]
            ev_sd = a(lambda g: g.activation(out=sd, in_=var, func=AF.Sqrt,
                                             bias=self.cst[:, 2:3], scale=4.0),
                      [fa[t][0]] + list(extra))
            sdev[t] = ev_sd
            ev_rs = v(lambda g: g.reciprocal(rstd, sd), [ev_sd])
            rsev[t] = ev_rs
            return ev_rs

        fx_war = [None, None]
        preev = [None] * NT
        bk = [None] * NT
        rsev = [None] * NT
        sdev = [None] * NT
        xcev = [None] * NT

        def xnT_tr(t, ev_xn):
            par2 = t % 2
            evt = None
            for b in range(8):
                w0 = [ev_xn, self.ev_id]
                if b == 0:
                    w0.append(self.ev_ptr[par2])
                evt = self.tr(self.ptr[par2][:, b * 128:(b + 1) * 128],
                              self.xnbf[par2][:, b * 128:(b + 1) * 128], w0)
            ev_xnT = v(lambda g, pp=par2: g.tensor_copy(
                self.xnT[pp][:], self.ptr[pp][:]), [evt])
            self.ev_ptr[par2] = ev_xnT
            fx_war[par2] = evt
            fx[t] = ev_xnT

        def fc_quarters(t, qlist):
            par2 = t % 2
            ev_xnT = fx[t]
            hs = st[:, hslot(t): hslot(t) + 4]
            ev_gelu = None
            for qq in qlist:
                pb = self.pfc[qq % 2]
                w0 = [ev_xnT, ev_pfc[qq % 2]] + list(wready)
                evm = None
                for dc in range(8):
                    for half in range(2):
                        col0 = dc * 4096 + qq * 1024 + half * 512
                        evm = self.mm(
                            pb[:, half * 512:(half + 1) * 512],
                            self.xnT[par2][:, dc * 128:(dc + 1) * 128],
                            wfc[:, col0:col0 + 512],
                            start=(dc == 0), stop=(dc == 7),
                            waits=w0 if (dc == 0 and half == 0) else [])
                rsx = st[:, xslot(t) + 1:xslot(t) + 2]
                ev_gelu = a(lambda g, pp=t % 2, q2=qq, pb2=pb, h2=hs, r3=rsx:
                            g.activation(
                                out=self.h_b[pp][:, q2 * 1024:(q2 + 1) * 1024],
                                in_=pb2[:], func=AF.Gelu, scale=r3,
                                accum_out=h2[:, q2:q2 + 1]),
                            [evm, rsev[t],
                             ev_hfree[t % 2] if qq == 0 else None])
                ev_pfc[qq % 2] = ev_gelu
            return ev_gelu

        def hln(t, ev_gelu, last):
            par2 = t % 2
            s = hslot(t)
            hs = st[:, s:s + 4]
            t1 = st[:, s + 4:s + 5]
            var = st[:, s + 5:s + 6]
            muh = st[:, s + 6:s + 7]
            rsh = st[:, s + 7:s + 8]
            q1 = st[:, s + 8:s + 9]
            q2s = st[:, s + 9:s + 10]
            sd = st[:, s + 10:s + 11]
            nr2 = st[:, s + 11:s + 12]
            p = self.po
            ev_hsum = v(lambda g: g.tensor_reduce(out=t1, in_=hs, axis=AX,
                                                  op=ALU.add), [ev_gelu])
            ev_sq1 = a(lambda g: g.activation(
                out=self.sqb[:], in_=self.h_b[par2][:, 0:2048],
                func=AF.Square, accum_out=q1), [ev_gelu])
            ev_sq2 = a(lambda g: g.activation(
                out=self.sqb[:], in_=self.h_b[par2][:, 2048:4096],
                func=AF.Square, accum_out=q2s), [ev_sq1])
            ev_ss = p(lambda g: g.tensor_add(var, q1, q2s), [ev_sq2])
            ev_mu = p(lambda g: g.tensor_scalar(out=muh, in0=t1,
                                                scalar1=1.0 / HF, scalar2=None,
                                                op0=ALU.mult), [ev_hsum])
            ev_e2 = p(lambda g: g.tensor_scalar(out=var, in0=var,
                                                scalar1=1.0 / HF, scalar2=None,
                                                op0=ALU.mult), [ev_ss])
            ev_m2 = p(lambda g: g.tensor_mul(t1, muh, muh), [ev_mu])
            ev_var = p(lambda g: g.tensor_sub(var, var, t1), [ev_e2, ev_m2])
            ev_sd = a(lambda g: g.activation(out=sd, in_=var, func=AF.Sqrt,
                                             bias=self.cst[:, 0:1], scale=1.0),
                      [ev_var])
            if not last:
                finish_x(t + 1, extra=[ev_sd])
            ev_rs = v(lambda g: g.reciprocal(rsh, sd), [ev_sd])
            ev_nr2 = v(lambda g: g.tensor_scalar(out=nr2, in0=rsh,
                                                 scalar1=-0.25, scalar2=None,
                                                 op0=ALU.mult), [ev_rs])
            hl[t] = (ev_gelu, ev_mu, ev_nr2)
            return None

        def backA(t):
            par2 = t % 2
            ev_h, ev_muh, ev_nr2 = hl[t]
            ev_last_t = None
            evs = []
            for rnd in range(4):
                pb = self.ptr[par2 if rnd % 2 == 0 else 1 - par2][:, 0:1024]
                for b in range(8):
                    blk = rnd * 8 + b
                    w0 = [ev_h, self.ev_id]
                    if b == 0:
                        if rnd < 2:
                            w0.append(self.ev_ptr[par2] if rnd == 0
                                      else self.ev_ptr[1 - par2])
                        else:
                            w0.append(evs[rnd - 2])
                    ev_last_t = self.tr(
                        pb[:, b * 128:(b + 1) * 128],
                        self.h_b[par2][:, blk * 128:(blk + 1) * 128], w0)
                w1 = [ev_last_t, ev_gTfree[par2] if rnd == 0 else None]
                if rnd % 2 == 0:
                    ev_ev = a(lambda g, pp=par2, r2=rnd, pb2=pb: g.copy(
                        self.gT_b[pp][:, r2 * 1024:(r2 + 1) * 1024], pb2[:]),
                        w1)
                else:
                    ev_ev = v(lambda g, pp=par2, r2=rnd, pb2=pb: g.tensor_copy(
                        self.gT_b[pp][:, r2 * 1024:(r2 + 1) * 1024], pb2[:]),
                        w1)
                evs.append(ev_ev)
            self.ev_ptr[par2] = evs[2]
            self.ev_ptr[1 - par2] = evs[3]
            ev_hfree[par2] = ev_last_t
            bk[t] = (evs, ev_muh, ev_nr2)
            dbg = getattr(self, "_dbg", None)
            if dbg is not None and t == 0:
                ev_gTfree[par2] = self.dma("sync", dbg["gt"],
                                           self.gT_b[par2][:], [evs[3]])

        def back(t):
            par2 = t % 2
            evs_gT, ev_muh, ev_nr2 = bk[t]
            s = hslot(t)
            muh = st[:, s + 6:s + 7]
            nr2 = st[:, s + 11:s + 12]
            evm = None
            for hc in range(32):
                for half in range(2):
                    if hc % 8 == 0 and half == 0:
                        w0 = [evs_gT[hc // 8]]
                        if hc == 0:
                            w0 += [self.ev_ppj] + list(wready_pj)
                    else:
                        w0 = []
                    evm = self.mm(
                        self.ppj[:, half * 512:(half + 1) * 512],
                        self.gT_b[par2][:, hc * 128:(hc + 1) * 128],
                        w2t[:, hc * 1024 + half * 512:
                            hc * 1024 + half * 512 + 512],
                        start=(hc == 0), stop=(hc == 31),
                        waits=w0)
            ev_gTfree[par2] = evm
            self.last_proj_ev = evm
            ev_u1 = v(lambda g: g.scalar_tensor_tensor(
                out=self.ppj[:], in0=self.cs2b[:], scalar=muh,
                in1=self.ppj[:], op0=ALU.mult, op1=ALU.subtract),
                [evm, ev_muh, self.ev_cs2b])
            xb = self.xbuf[t % 4]
            ev_x2 = v(lambda g: g.scalar_tensor_tensor(
                out=xb[:], in0=self.ppj[:], scalar=nr2,
                in1=xb[:], op0=ALU.mult, op1=ALU.add),
                [ev_u1, ev_nr2, preev[t]])
            self.ev_ppj = ev_x2
            ev_done[t] = self.dma("sync", d_v[t], xb[:], [ev_x2])

        dbg = getattr(self, "_dbg", None)
        frontA(0)
        frontA_sum(0)
        frontA_stats(0)
        xnT_tr(0, xcev[0])
        ev_r0 = finish_x(0)
        frontA(1)
        frontA_sum(1)
        frontA_stats(1)
        xnT_tr(1, xcev[1])
        finish_x(1, extra=[ev_r0])
        for t in range(NT):
            if t + 2 < NT:
                frontA(t + 2)
                frontA_sum(t + 2)
            ev_g01 = fc_quarters(t, (0, 1))
            ev_g23 = fc_quarters(t, (2, 3))
            if dbg is not None and t == 0:
                ev_hfree[0] = self.dma("sync", dbg["h"], self.h_b[0][:],
                                       [ev_g23])
            if t >= 1:
                backA(t - 1)
            if t >= 1:
                back(t - 1)
            if t + 2 < NT:
                frontA_stats(t + 2)
            hln(t, ev_g23, last=(t == 0 or t + 1 >= NT))
            if t + 2 < NT:
                xnT_tr(t + 2, xcev[t + 2])
            if t == NT - 1:
                self.last_fc_ev = ev_g23
        backA(NT - 1)
        back(NT - 1)
        return ev_done

    def _attention(self):
        v, a = self.dv, self.ac
        ev_cast = self.ev_cast

        wmap = {"wq": 0, "wk": 1, "wv": 2, "wo": 3}

        def ldw_fm(dst, name):
            return [self.dma(
                "sync" if h == 0 else "gpsimd",
                dst[:, h * 4096:(h + 1) * 4096].rearrange(
                    "p (c m) -> p c m", m=D),
                rap(self.watt_d, wmap[name] * D * D + h * 4 * 128 * D,
                    [[D, 128], [128 * D, 4], [1, D]]),
                [self.last_fc_ev, self.ev_wfc_st, ev_cast[wmap[name]]])
                for h in range(2)]
        ev_wqkv = [ldw_fm(self.wqf, "wq"), ldw_fm(self.wkf, "wk"),
                   ldw_fm(self.wvf, "wv")]
        self.ev_onesrow = v(lambda g: g.memset(self.onesrow, 1.0))
        ev_bq = self.dma("gpsimd", self.brow[0:1, 0:1024],
                         self.dm["bq"], [self.last_proj_ev, self.ev_onesrow])
        ev_bk = self.dma("gpsimd", self.brow[0:1, 1024:2048],
                         self.dm["bk"], [ev_bq])
        ev_bv = self.dma("gpsimd", self.brow[0:1, 2048:3072],
                         self.dm["bv"], [ev_bk])

        x1_v = self.x1_d.rearrange("(n p) c -> n p c", p=128)
        qtm_v = self.qtm_d.rearrange("(n p) c -> n p c", p=128)
        ktm_v = self.ktm_d.rearrange("(n p) c -> n p c", p=128)
        vtm_v = self.vtm_d.rearrange("(n p) c -> n p c", p=128)

        ev_acc0 = v(lambda g: g.memset(self.accA[:], 0.0))
        ev_z0 = v(lambda g: g.memset(self.zacc[:], 0.0))

        ev_pfc = [None, None]
        ev_xfree = [self.ev_p1[NT - 2], self.ev_p1[NT - 1]]
        bufev = {}
        ev_accA = ev_acc0
        faA = [None] * (NT + 1)
        pendA = [None] * NT

        xnev = [None] * (NT + 1)
        trwar = [None, None]

        def frontA2_pre(t):
            par = t % 2
            ev_xl = self.dma("sync", self.xbuf[par][:], x1_v[t],
                             [self.ev_p1[t], ev_xfree[par]])
            ev_ln, mu, rstd, _ = self._ln(self.xbuf[par][:], D, 8 + 16 * par,
                                          [ev_xl])
            ev_xn = v(lambda g, pp=par, m=mu, r=rstd: g.tensor_scalar(
                out=self.xnbf[pp][:], in0=self.xbuf[pp][:], scalar1=m,
                scalar2=r, op0=ALU.subtract, op1=ALU.mult),
                [ev_ln, trwar[par]])
            ev_xfree[par] = ev_xn
            xnev[t] = ev_xn

        def frontA2_tr(t):
            par = t % 2
            evt = None
            for b in range(8):
                w0 = [xnev[t], self.ev_id]
                if b == 0:
                    w0.append(self.ev_ptr[par])
                evt = self.tr(self.ptr[par][:, b * 128:(b + 1) * 128],
                              self.xnbf[par][:, b * 128:(b + 1) * 128], w0)
            trwar[par] = evt
            ev_xnT = v(lambda g, pp=par: g.tensor_copy(self.xnT[pp][:],
                                                       self.ptr[pp][:]), [evt])
            self.ev_ptr[par] = ev_xnT
            faA[t] = ev_xnT

        frontA2_pre(0)
        frontA2_tr(0)
        frontA2_pre(1)
        frontA2_tr(1)
        frontA2_pre(2)
        for t in range(NT):
            par = t % 2
            ev_xnT = faA[t]
            evq_mm = evk_mm = evv_mm = None
            for mi, (pb, wf, boff) in enumerate(
                    ((self.pfc[0], self.wqf, 0), (self.pfc[1], self.wkf, 1024),
                     (self.ppj, self.wvf, 2048))):
                wseed = [ev_bv, self.ev_onesrow,
                         (ev_pfc[0], ev_pfc[1], self.ev_ppj)[mi]]
                for half in range(2):
                    self.mm(pb[:, half * 512:(half + 1) * 512],
                            self.onesrow[0:1, :],
                            self.brow[0:1, boff + half * 512:
                                      boff + half * 512 + 512],
                            start=True, stop=False,
                            waits=wseed if half == 0 else [])
                w0 = [ev_xnT] + ev_wqkv[mi]
                evm = None
                for dc in range(8):
                    for half in range(2):
                        evm = self.mm(
                            pb[:, half * 512:(half + 1) * 512],
                            self.xnT[par][:, dc * 128:(dc + 1) * 128],
                            wf[:, dc * 1024 + half * 512:
                               dc * 1024 + half * 512 + 512],
                            start=False, stop=(dc == 7),
                            waits=w0 if (dc == 0 and half == 0) else [])
                if mi == 0:
                    evq_mm = evm
                elif mi == 1:
                    evk_mm = evm
                else:
                    evv_mm = evm
            if t == NT - 1:
                self.last_qkv_ev = evv_mm
            ev_qs = a(lambda g, pp=par: g.activation(
                out=self.qtm_b[pp][:], in_=self.pfc[0][:], func=AF.Sigmoid),
                [evq_mm, bufev.get(("qtm", par))])
            ev_ks = a(lambda g, pp=par: g.activation(
                out=self.ktm_b[pp][:], in_=self.pfc[1][:], func=AF.Sigmoid),
                [evk_mm, bufev.get(("ktm", par))])
            ev_vt = v(lambda g, pp=par: g.tensor_copy(
                self.vtm_b[pp][:], self.ppj[:]),
                [evv_mm, bufev.get(("vtm", par))])
            ev_pfc[0], ev_pfc[1] = ev_qs, ev_ks
            self.ev_ppj = ev_vt
            bufev[("qtm", par)] = self.dma("sync", qtm_v[t],
                                           self.qtm_b[par][:], [ev_qs])
            bufev[("ktm", par)] = self.dma("sync", ktm_v[t],
                                           self.ktm_b[par][:], [ev_ks])
            bufev[("vtm", par)] = self.dma("sync", vtm_v[t],
                                           self.vtm_b[par][:], [ev_vt])
            pendA[t] = (ev_qs, ev_ks)

            tt = t - 1 if t >= 1 else None
            if t == NT - 1:
                backs = [t - 1, t]
            elif t >= 1:
                backs = [t - 1]
            else:
                backs = []
            for tb in backs:
                parb = tb % 2
                ev_qs_b, ev_ks_b = pendA[tb]
                evq = None
                for b in range(8):
                    w0 = [ev_qs_b, self.ev_id]
                    if b == 0:
                        w0.append(self.ev_ptr[parb])
                    evq = self.tr(self.ptr[parb][:, b * 128:(b + 1) * 128],
                                  self.qtm_b[parb][:, b * 128:(b + 1) * 128],
                                  w0)
                ev_qf = a(lambda g, pp=parb: g.copy(self.qfm_b[pp][:],
                                                    self.ptr[pp][:]),
                          [evq, bufev.get(("qfm", parb))])
                self.ev_ptr[parb] = ev_qf
                evk = None
                for b in range(8):
                    w0 = [ev_ks_b, self.ev_id]
                    if b == 0:
                        w0.append(self.ev_ptr[1 - parb])
                    evk = self.tr(self.ptr[1 - parb][:, b * 128:(b + 1) * 128],
                                  self.ktm_b[parb][:, b * 128:(b + 1) * 128],
                                  w0)
                ev_kf = v(lambda g, pp=parb: g.tensor_copy(self.kfm_b[pp][:],
                                                           self.ptr[1 - pp][:]),
                          [evk, bufev.get(("kfm", parb))])
                self.ev_ptr[1 - parb] = ev_kf
                ev_qr = v(lambda g, pp=parb: g.tensor_reduce(
                    out=self.sv[:, 16:24],
                    in_=self.qfm_b[pp][:].rearrange("p (c n) -> p c n", n=128),
                    axis=AX, op=ALU.add), [ev_qf, ev_accA])
                ev_a1 = self.po(lambda g: g.tensor_add(self.accA[:, 8:16],
                                                       self.accA[:, 8:16],
                                                       self.sv[:, 16:24]),
                                [ev_qr])
                ev_kr = v(lambda g, pp=parb: g.tensor_reduce(
                    out=self.sv[:, 24:32],
                    in_=self.kfm_b[pp][:].rearrange("p (c n) -> p c n", n=128),
                    axis=AX, op=ALU.add), [ev_kf, ev_a1])
                ev_accA = self.po(lambda g: g.tensor_add(self.accA[:, 0:8],
                                                         self.accA[:, 0:8],
                                                         self.sv[:, 24:32]),
                                  [ev_kr])
                fmoff = (tb // 4) * 4096 + (tb % 4) * 128
                bufev[("qfm", parb)] = self.dma(
                    "sync",
                    rap(self.qfm_d, fmoff,
                        [[8 * T, 128], [512, 8], [1, 128]]),
                    self.qfm_b[parb][:].rearrange("p (c n) -> p c n", n=128),
                    [ev_qf])
                bufev[("kfm", parb)] = self.dma(
                    "sync",
                    rap(self.kfm_d, fmoff,
                        [[8 * T, 128], [512, 8], [1, 128]]),
                    self.kfm_b[parb][:].rearrange("p (c n) -> p c n", n=128),
                    [ev_kf])
            if t + 2 < NT:
                frontA2_tr(t + 2)
            if t + 3 < NT:
                frontA2_pre(t + 3)

        ev_i = self.dma("gpsimd", self.ar1i[:], self.accA[:], [ev_accA])
        ev_cc1 = self.step("gpsimd",
                           lambda g: g.collective_compute(
                               "AllReduce", ALU.add, replica_groups=self.repl,
                               ins=[self.ar1i.opt()], outs=[self.ar1o.opt()]),
                           [ev_i], [(self.CC, 1)])
        ev_w1 = self.dma("gpsimd", self.warr[:], self.ar1o[:], [ev_cc1])
        ev_we = v(lambda g: g.tensor_scalar(out=self.warr[:], in0=self.warr[:],
                                            scalar1=EPS, scalar2=None,
                                            op0=ALU.add), [ev_w1])
        ev_b1 = self._build_B(self.B1, [ev_we])
        ev_c01 = self._perhead_c(ev_we, 0, 1)

        rb = self.rb
        si_sbf = self.xnT[0].bitcast(F32)
        sa_sbf = self.xnT[1].bitcast(F32)
        sist = [None] * NB
        sast = [None] * NB
        ldev = {}
        sioT_war = None
        ev_dhm_war = None
        ev_sst = None
        pdv = [(self.pd_a, self.pd_b),
               (self.pfc[0][32:48, 0:512], self.pfc[0][32:48, 512:1024])]
        pdfree = [ev_pfc[0], ev_pfc[0]]
        lds = {}
        pdmm = {}
        bfev = {}
        wsst = {"q": None, "k": None}

        def b_loads(u):
            pr = u % 2
            w_st = [self.ev_w2_st] if u < 2 else []
            lds[u] = (
                self.dma("sync", rb[0 + pr],
                         rap(self.qfm_d, u * 4096, [[8 * T, 128], [1, 4096]]),
                         [bufev[("qfm", 0)], bufev[("qfm", 1)],
                          ldev.get(("b0", pr))] + w_st),
                self.dma("sync", rb[2 + pr],
                         rap(self.kfm_d, u * 4096, [[8 * T, 128], [1, 4096]]),
                         [bufev[("kfm", 0)], bufev[("kfm", 1)],
                          ldev.get(("b2", pr))] + w_st),
                self.dma("gpsimd", rb[4 + pr],
                         rap(self.qtm_d, u * BT * D,
                             [[D, 128], [128 * D, 4], [1, D]]),
                         [bufev[("qtm", 0)], bufev[("qtm", 1)],
                          ldev.get(("b4", pr))] + w_st),
                self.dma("gpsimd", rb[6 + pr],
                         rap(self.ktm_d, u * BT * D,
                             [[D, 128], [128 * D, 4], [1, D]]),
                         [bufev[("ktm", 0)], bufev[("ktm", 1)],
                          ldev.get(("b6", pr))] + w_st))

        def b_pd(u):
            pr = u % 2
            pa, pb_ = pdv[u % 2]
            ev_lq, ev_lk = lds[u][0], lds[u][1]
            evm1 = evm2 = None
            for c in range(8):
                evm1 = self.mm(pa, self.B1[:, 32 * c:32 * c + 16],
                               rb[0 + pr][:, c * BT:(c + 1) * BT],
                               start=(c == 0), stop=(c == 7),
                               waits=[ev_lq, ev_b1, pdfree[u % 2]]
                               if c == 0 else [])
            for c in range(8):
                evm2 = self.mm(pb_, self.B1[:, 32 * c + 16:32 * c + 32],
                               rb[2 + pr][:, c * BT:(c + 1) * BT],
                               start=(c == 0), stop=(c == 7),
                               waits=[ev_lk, ev_b1] if c == 0 else [])
            pdmm[u] = (evm1, evm2)
            ldev[("b0", pr)] = evm1
            ldev[("b2", pr)] = evm2

        def b_chain(u):
            nonlocal ev_dhm_war, sioT_war
            evm1, evm2 = pdmm[u]
            pa, pb_ = pdv[u % 2]
            ev_d1 = v(lambda g, p1=pa: g.tensor_scalar(
                out=self.dhm[0:16, 0:512], in0=p1,
                scalar1=self.cvec[0:16, 0:1], scalar2=None, op0=ALU.add),
                [evm1, ev_c01, ev_dhm_war])
            ev_d2 = v(lambda g, p2=pb_: g.tensor_scalar(
                out=self.dhm[0:16, 512:1024], in0=p2,
                scalar1=self.cvec[0:16, 1:2], scalar2=None, op0=ALU.add),
                [evm2, ev_d1])
            pdfree[u % 2] = ev_d2
            evt = None
            for cc in range(4):
                w0 = [ev_d2, self.ev_id]
                if cc == 0:
                    w0.append(self.ev_ptr[0])
                evt = self.tr(self.ptr[0][:, 32 * cc:32 * cc + 16],
                              self.dhm[0:16, 128 * cc:128 * cc + 128], w0)
                evt = self.tr(self.ptr[0][:, 32 * cc + 16:32 * cc + 32],
                              self.dhm[0:16, 512 + 128 * cc:
                                       512 + 128 * cc + 128],
                              [self.ev_id])
            ev_dhm_war = evt
            ev_rec = v(lambda g: g.reciprocal(self.siorec,
                                              self.ptr[0][:, 0:128]),
                       [evt, sioT_war])
            self.ev_ptr[0] = ev_rec
            bfev[u] = v(lambda g: g.tensor_copy(self.sio_tm[:], self.siorec),
                        [ev_rec])
            sist[u] = v(lambda g, uu=u: g.tensor_copy(
                rap(si_sbf, uu * 64, [[512, 128], [16, 4], [1, 16]]),
                rap(self.siorec, 0, [[self.AFW, 128], [32, 4], [1, 16]])),
                [ev_rec, self.last_qkv_ev if u == 0 else None])

        def b_ws(u):
            nonlocal sioT_war
            pr = u % 2
            ev_bf = bfev[u]
            ev_lqt, ev_lkt = lds[u][2], lds[u][3]
            evmq = evmk = None
            for cc in range(4):
                lhs = self.sio_tm[:, 32 * cc:32 * cc + 32]
                for half in range(2):
                    evmq = self.mm(self.pws_q[:, half * 512:(half + 1) * 512],
                                   lhs,
                                   rb[4 + pr][:, cc * 1024 + half * 512:
                                               cc * 1024 + half * 512 + 512],
                                   start=(u == 0 and cc == 0),
                                   stop=(u == NB - 1 and cc == 3),
                                   waits=[ev_bf, ev_lqt, ev_pfc[1]]
                                   if (u == 0 and cc == 0 and half == 0)
                                   else ([ev_bf] if half == 0 else []))
                for half in range(2):
                    evmk = self.mm(self.pws_k[:, half * 512:(half + 1) * 512],
                                   lhs,
                                   rb[6 + pr][:, cc * 1024 + half * 512:
                                               cc * 1024 + half * 512 + 512],
                                   start=(u == 0 and cc == 0),
                                   stop=(u == NB - 1 and cc == 3),
                                   waits=[ev_bf, ev_lkt, self.ev_ppj]
                                   if (u == 0 and cc == 0 and half == 0)
                                   else ([ev_bf] if half == 0 else []))
            sioT_war = evmk
            ldev[("b4", pr)] = evmq
            ldev[("b6", pr)] = evmk
            return evmk

        b_loads(0)
        b_pd(0)
        for u in range(NB):
            if u + 1 < NB:
                b_loads(u + 1)
            b_chain(u)
            if u + 1 < NB:
                b_pd(u + 1)
            ev_last_ws = b_ws(u)

        ev_wo2 = ldw_fm(self.wof, "wo")

        ev_e1 = v(lambda g: g.tensor_copy(self.qsi_sb, self.pws_q[0:16, :]),
                  [ev_last_ws])
        ev_e2 = v(lambda g: g.tensor_copy(self.kso_sb, self.pws_k[:]),
                  [ev_last_ws, ev_e1])
        ev_i1 = self.dma("gpsimd", self.ar2i[:, 0:1024], self.qsi_sb, [ev_e1])
        ev_i2 = self.dma("gpsimd", self.ar2i[:, 1024:2048],
                         rap(self.kso_sb, 16 * self.AFW,
                             [[self.AFW, 16], [1, 1024]]), [ev_e2])
        ev_cc2 = self.step("gpsimd",
                           lambda g: g.collective_compute(
                               "AllReduce", ALU.add, replica_groups=self.repl,
                               ins=[self.ar2i.opt()], outs=[self.ar2o.opt()]),
                           [ev_i1, ev_i2], [(self.CC, 1)])
        ev_bl = [None]
        for side, base in ((0, 1024), (1, 0)):
            ev_t1 = self.dma("gpsimd", self.warr[0:64, side * 8:side * 8 + 8],
                             rap(self.ar2o, base, [[1, 64], [2112, 8]]),
                             [ev_cc2, ev_b1, ev_c01, ev_bl[-1]], slow=True)
            ev_t2 = self.dma("gpsimd", self.warr[64:128, side * 8:side * 8 + 8],
                             rap(self.ar2o, base + 2048 + 64,
                                 [[1, 64], [2112, 8]]),
                             [ev_t1], slow=True)
            ev_bl.append(ev_t2)
        ev_we2 = v(lambda g: g.tensor_scalar(out=self.warr[:], in0=self.warr[:],
                                             scalar1=EPS, scalar2=None,
                                             op0=ALU.add), [ev_bl[-1]])
        ev_b2 = self._build_B(self.B2, [ev_we2])
        ev_c23 = self._perhead_c(ev_we2, 2, 3)
        ev_nc = v(lambda g: g.tensor_scalar(out=self.cvec[0:16, 4:5],
                                            in0=self.cvec[0:16, 2:3],
                                            scalar1=-1.0, scalar2=None,
                                            op0=ALU.mult), [ev_c23])

        ev_dhm_warC = ev_dhm_war
        ev_kv = None
        ev_z = ev_z0
        ev_etm_war = None
        ev_vsc_war = [None, None]
        ev_e = None
        eev = {}
        etmev = {}

        def c_loads(u):
            pr = u % 2
            lds[u] = (
                self.dma("sync", rb[0 + pr],
                         rap(self.qfm_d, u * 4096, [[8 * T, 128], [1, 4096]]),
                         [ldev.get(("b0", pr))]),
                self.dma("sync", rb[2 + pr],
                         rap(self.kfm_d, u * 4096, [[8 * T, 128], [1, 4096]]),
                         [ldev.get(("b2", pr))]),
                self.dma("gpsimd", rb[4 + pr],
                         rap(self.ktm_d, u * BT * D,
                             [[D, 128], [128 * D, 4], [1, D]]),
                         [ldev.get(("b4", pr))]),
                self.dma("gpsimd", rb[6 + pr],
                         rap(self.vtm_d, u * BT * D,
                             [[D, 128], [128 * D, 4], [1, D]]),
                         [bufev[("vtm", 0)], bufev[("vtm", 1)],
                          ldev.get(("b6", pr))]))

        def c_pd(u):
            pr = u % 2
            pa, pb_ = pdv[u % 2]
            ev_lq, ev_lk = lds[u][0], lds[u][1]
            evm1 = evm2 = None
            for c in range(8):
                evm1 = self.mm(pa, self.B2[:, 32 * c:32 * c + 16],
                               rb[0 + pr][:, c * BT:(c + 1) * BT],
                               start=(c == 0), stop=(c == 7),
                               waits=[ev_lq, ev_b2, pdfree[u % 2]]
                               if c == 0 else [])
            for c in range(8):
                evm2 = self.mm(pb_, self.B2[:, 32 * c + 16:32 * c + 32],
                               rb[2 + pr][:, c * BT:(c + 1) * BT],
                               start=(c == 0), stop=(c == 7),
                               waits=[ev_lk, ev_b2] if c == 0 else [])
            pdmm[u] = (evm1, evm2)
            ldev[("b0", pr)] = evm1
            ldev[("b2", pr)] = evm2

        def c_chain(u):
            nonlocal ev_dhm_warC, ev_etm_war, ev_z, ev_e
            evm1, evm2 = pdmm[u]
            pa, pb_ = pdv[u % 2]
            ev_em = a(lambda g, p1=pa: g.activation(
                out=self.dhm[0:16, 512:1024], in_=p1, func=AF.Exp,
                scale=self.cst[0:16, 1:2], bias=self.cvec[0:16, 4:5]),
                [evm1, ev_nc, ev_dhm_warC])
            ev_cs = v(lambda g, p2=pb_: g.tensor_scalar(
                out=self.qsi_sb[0:16, 0:512], in0=p2,
                scalar1=self.cvec[0:16, 3:4], scalar2=1.0,
                op0=ALU.add, op1=ALU.min), [evm2, ev_c23, ev_i1])
            ev_cs2 = v(lambda g: g.tensor_scalar(
                out=self.qsi_sb[0:16, 0:512], in0=self.qsi_sb[0:16, 0:512],
                scalar1=-1.0, scalar2=None, op0=ALU.max), [ev_cs])
            ev_e = a(lambda g: g.activation(
                out=self.dhm[0:16, 0:512], in_=self.qsi_sb[0:16, 0:512],
                func=AF.Exp, accum_out=self.zacc[0:16, 1:2]), [ev_cs2, ev_em])
            eev[u] = ev_e
            pdfree[u % 2] = ev_e
            ev_z = v(lambda g: g.tensor_add(self.zacc[0:16, 0:1],
                                            self.zacc[0:16, 0:1],
                                            self.zacc[0:16, 1:2]),
                     [ev_e, ev_z])
            evt = None
            for cc in range(4):
                w0 = [ev_e, self.ev_id]
                if cc == 0:
                    w0.append(self.ev_ptr[0])
                evt = self.tr(self.ptr[0][:, 32 * cc:32 * cc + 16],
                              self.dhm[0:16, 128 * cc:128 * cc + 128], w0)
                evt = self.tr(self.ptr[0][:, 32 * cc + 16:32 * cc + 32],
                              self.dhm[0:16, 512 + 128 * cc:
                                       512 + 128 * cc + 128],
                              [self.ev_id])
            ev_dhm_warC = evt
            ev_etm = v(lambda g: g.tensor_copy(self.etm[:],
                                               self.ptr[0][:, 0:128]),
                       [evt, ev_etm_war])
            etmev[u] = ev_etm
            self.ev_ptr[0] = ev_etm
            ev_sa = None
            for cc in range(4):
                ev_sa = v(lambda g, c2=cc: g.tensor_scalar(
                    out=rap(self.satm, 16 * c2, [[self.AFW, 128], [1, 16]]),
                    in0=self.etm[:, 32 * c2 + 16:32 * c2 + 32],
                    scalar1=1.0, scalar2=None, op0=ALU.add), [ev_etm, ev_sa])
            ev_sar = v(lambda g: g.reciprocal(self.satm, self.satm), [ev_sa])
            sast[u] = v(lambda g, uu=u: g.tensor_copy(
                rap(sa_sbf, uu * 64, [[512, 128], [16, 4], [1, 16]]),
                rap(self.satm, 0, [[self.AFW, 128], [16, 4], [1, 16]])),
                [ev_sar, self.last_qkv_ev if u == 0 else None])
            ev_etm_war = ev_sar

        def c_kv(u):
            nonlocal ev_kv
            pr = u % 2
            ev_lkt, ev_lvt = lds[u][2], lds[u][3]
            ev_etm = etmev[u]
            ev_vsc = None
            for cc in range(4):
                ebc = rap(self.etm, 32 * cc,
                          [[self.ATW, 128], [1, 16], [0, 64]])
                ev_vsc = v(lambda g, c2=cc, eb=ebc, pp=pr:
                           g.tensor_tensor(
                               out=self.vsc_b[pp][:].rearrange(
                                   "p (h j) -> p h j", j=64),
                               in0=rb[6 + pp][:, c2 * 1024:
                                              (c2 + 1) * 1024].rearrange(
                                   "p (h j) -> p h j", j=64),
                               in1=eb, op=ALU.mult),
                           [ev_etm, ev_lvt,
                            ev_vsc if ev_vsc else ev_vsc_war[pr]])
                for p2 in range(8):
                    w0 = []
                    if p2 == 0:
                        w0 = [ev_vsc, ev_lkt]
                        if u == 0 and cc == 0:
                            w0.append(ev_e2)
                    ev_kv = self.mm(
                        self.pkv[:, 128 * p2:128 * (p2 + 1)],
                        rb[4 + pr][:, cc * 1024 + 128 * p2:
                                   cc * 1024 + 128 * (p2 + 1)],
                        self.vsc_b[pr][:, 128 * p2:128 * (p2 + 1)],
                        start=(u == 0 and cc == 0),
                        stop=(u == NB - 1 and cc == 3),
                        waits=w0)
            ev_vsc_war[pr] = ev_kv
            ldev[("b4", pr)] = ev_kv
            ldev[("b6", pr)] = ev_kv

        c_loads(0)
        c_pd(0)
        for u in range(NB):
            if u + 1 < NB:
                c_loads(u + 1)
            c_chain(u)
            if u + 1 < NB:
                c_pd(u + 1)
            c_kv(u)

        self.pf_pj = []
        for k in range(3):
            self.pf_pj.append(self.dma(
                "gpsimd", self.w2t3[:, k * 8192:(k + 1) * 8192],
                rap(self.w2t_d, k * 8192, [[32 * D, 128], [1, 8192]]),
                [self.last_qkv_ev, self.ev_w2_st]))

        self.pf_fc = []
        w_rb = [ldev[("b2", 0)], ldev[("b2", 1)], ldev[("b4", 0)],
                ldev[("b4", 1)], ldev[("b6", 0)], ldev[("b6", 1)]]
        for k in range(3):
            self.pf_fc.append(self.dma(
                "gpsimd", self.wfcq3[:, 8192 + k * 8192:8192 + (k + 1) * 8192],
                rap(self.wfcq_d, 8192 + k * 8192, [[8 * HF, 128], [1, 8192]]),
                w_rb if k == 0 else []))

        ev_kev1 = v(lambda g: g.tensor_copy(
            rap(self.kvsb, 0, [[self.AFW, 64], [64, 8], [1, 64]]),
            rap(self.pkv, 0, [[1024, 64], [128, 8], [1, 64]])), [ev_kv])
        ev_kev2 = v(lambda g: g.tensor_copy(
            rap(self.kvsb, self.AFW * 64, [[self.AFW, 64], [64, 8], [1, 64]]),
            rap(self.pkv, 1024 * 64 + 64, [[1024, 64], [128, 8], [1, 64]])),
            [ev_kev1])
        ev_i3a = self.dma("gpsimd", self.ar3i[:, 0:512], self.kvsb,
                          [ev_kev2])
        ev_i3b = self.dma("gpsimd", self.ar3i[0:16, 512:513],
                          self.zacc[0:16, 0:1], [ev_z], slow=True)
        ev_i3c = self.dma("gpsimd",
                          rap(self.ar3i, 16 * 513 + 512, [[513, 112], [1, 1]]),
                          self.zeros[0:112, 0:1], [ev_z], slow=True)
        ev_cc3 = self.step("gpsimd",
                           lambda g: g.collective_compute(
                               "AllReduce", ALU.add, replica_groups=self.repl,
                               ins=[self.ar3i.opt()], outs=[self.ar3o.opt()]),
                           [ev_i3a, ev_i3b, ev_i3c], [(self.CC, 1)])
        ev_kvl = self.dma("gpsimd", self.kvr, self.ar3o[:, 0:512], [ev_cc3])
        ev_zl = self.dma("gpsimd", self.zacc[0:16, 2:3],
                         self.ar3o[0:16, 512:513], [ev_cc3], slow=True)
        ev_kb0 = v(lambda g: g.memset(self.kvblocks[:], 0.0),
                   [ev_vsc_war[0], ev_vsc_war[1], ev_kev2, ev_i3a, sast[NB - 1]])
        ev_kb1 = v(lambda g: g.tensor_copy(
            rap(self.kvblocks, 0, [[self.ATW, 64], [128, 8], [1, 64]]),
            rap(self.kvr, 0, [[self.AFW, 64], [64, 8], [1, 64]])),
            [ev_kvl, ev_kb0])
        ev_kb2 = v(lambda g: g.tensor_copy(
            rap(self.kvblocks, self.ATW * 64 + 64,
                [[self.ATW, 64], [128, 8], [1, 64]]),
            rap(self.kvr, self.AFW * 64, [[self.AFW, 64], [64, 8], [1, 64]])),
            [ev_kb1])
        ev_zf = v(lambda g: g.reciprocal(self.zacc[0:16, 3:4],
                                         self.zacc[0:16, 2:3]), [ev_zl])
        ev_zf2 = v(lambda g: g.tensor_scalar(out=self.zacc[0:16, 3:4],
                                             in0=self.zacc[0:16, 3:4],
                                             scalar1=LSEQ, scalar2=None,
                                             op0=ALU.mult), [ev_zf])
        ev_zs = self.dma("gpsimd", self.zf_d[:], self.zacc[0:16, 3:4], [ev_zf2])
        ev_zbc = self.dma("gpsimd", self.zbc[:],
                          rap(self.zf_d, 0, [[0, 128], [1, 16]]), [ev_zs])
        bo16 = self.art[0:1, 12032:13056]
        ev_bo = self.dma("gpsimd", bo16, self.dm["bo"], [ev_kb2, ev_zbc])

        x1_v2 = self.x1_d.rearrange("(n p) c -> n p c", p=128)
        dstD = self.y_out if _STOP_AFTER == 2 else self.x2_d
        x2_v = dstD.rearrange("(n p) c -> n p c", p=128)
        ev_p2 = [None] * NT
        ev_attw = [ev_e, ev_kev2]
        ev_sc_prev = ev_e
        ev_woev = [ev_e2, ev_kev2]
        ev_stot_war = None
        ev_xD = [None, None]
        ev_wo_done = None
        for u in range(NB):
            pr = u % 2
            qfm_t = rb[0 + pr]
            ev_lq = self.dma("sync", qfm_t,
                             rap(self.qfm_d, u * 4096,
                                 [[8 * T, 128], [1, 4096]]),
                             [ldev.get(("b0", pr))])
            ldev[("b0", pr)] = None
            ev_s1 = v(lambda g, uu=u: g.tensor_mul(
                self.stot,
                rap(si_sbf, uu * 64, [[512, 128], [16, 4], [1, 16]]),
                rap(sa_sbf, uu * 64, [[512, 128], [16, 4], [1, 16]])),
                [sist[u], sast[u], ev_stot_war])
            zb = rap(self.zbc, 0, [[16, 128], [0, 4], [1, 16]])
            ev_s2 = v(lambda g, z=zb: g.tensor_tensor(
                out=rap(self.stot, 0, [[self.AFW, 128], [16, 4], [1, 16]]),
                in0=rap(self.stot, 0, [[self.AFW, 128], [16, 4], [1, 16]]),
                in1=z, op=ALU.mult), [ev_s1, ev_zbc])
            ev_sc_last = None
            for tb in range(4):
                t128 = u * 4 + tb
                pD = self.pfc[0]
                wo_pb = self.ppj if tb % 2 == 0 else self.pfc[1]
                evo = None
                for p2 in range(8):
                    evo = self.mm(
                        pD[:, 128 * p2:128 * (p2 + 1)],
                        qfm_t[:, p2 * BT + tb * 128: p2 * BT + (tb + 1) * 128],
                        self.kvblocks[:, 128 * p2:128 * (p2 + 1)],
                        start=True, stop=True,
                        waits=[ev_lq, ev_kb2, ev_sc_prev]
                        if p2 == 0 else [])
                    self.last_outmm_ev = evo
                for half in range(2):
                    self.mm(wo_pb[:, half * 512:(half + 1) * 512],
                            self.onesrow[0:1, :],
                            bo16[0:1, half * 512:half * 512 + 512],
                            start=True, stop=False,
                            waits=[ev_bo, ev_woev[tb % 2]]
                            if half == 0 else [])
                sb_ap = rap(self.stot, 16 * tb,
                            [[self.AFW, 128], [1, 16], [0, 64]])
                ev_sc = v(lambda g, pD2=pD, sbb=sb_ap, t2=tb:
                          g.tensor_tensor(
                              out=self.attntm_b[t2 % 2][:].rearrange(
                                  "p (h j) -> p h j", j=64),
                              in0=pD2[:].rearrange("p (h j) -> p h j", j=64),
                              in1=sbb, op=ALU.mult),
                          [evo, ev_s2, ev_attw[tb % 2]])
                ev_sc_prev = ev_sc
                ev_sc_last = ev_sc
                evt = None
                for b in range(8):
                    w0 = [ev_sc, self.ev_id]
                    if b == 0:
                        w0.append(self.ev_ptr[tb % 2])
                    evt = self.tr(self.ptr[tb % 2][:, b * 128:(b + 1) * 128],
                                  self.attntm_b[tb % 2][:, b * 128:
                                                        (b + 1) * 128], w0)
                ev_attw[tb % 2] = evt
                ev_af = a(lambda g, t2=tb: g.copy(self.attnfm_b[t2 % 2][:],
                                                  self.ptr[t2 % 2][:]), [evt])
                self.ev_ptr[tb % 2] = ev_af
                evw = None
                for c in range(8):
                    for half in range(2):
                        evw = self.mm(
                            wo_pb[:, half * 512:(half + 1) * 512],
                            self.attnfm_b[tb % 2][:, c * 128:(c + 1) * 128],
                            self.wof[:, c * 1024 + half * 512:
                                     c * 1024 + half * 512 + 512],
                            start=False, stop=(c == 7),
                            waits=([ev_af] + (ev_wo2 if t128 == 0 else []))
                            if (c == 0 and half == 0) else [])
                ev_wo_done = evw
                ev_xl = self.dma("sync", self.xbuf[tb % 2][:], x1_v2[t128],
                                 [ev_xD[tb % 2]])
                ev_u = v(lambda g, t2=tb, wp=wo_pb: g.scalar_tensor_tensor(
                    out=self.xbuf[t2 % 2][:], in0=wp[:], scalar=1.0,
                    in1=self.xbuf[t2 % 2][:], op0=ALU.mult, op1=ALU.add),
                    [evw, ev_xl])
                ev_woev[tb % 2] = ev_u
                ev_st = self.dma("sync", x2_v[t128],
                                 self.xbuf[tb % 2][:], [ev_u])
                ev_xD[tb % 2] = ev_st
                ev_p2[t128] = ev_st
            ev_stot_war = ev_sc_last
        self.ev_p2 = ev_p2
        self.ev_wo_done = ev_wo_done
        self.ev_ppj = ev_p2[NT - 1]
        self.pf_fc.append(self.dma(
            "sync", self.wfcq3[:, 0:8192],
            rap(self.wfcq_d, 0, [[8 * HF, 128], [1, 8192]]),
            [self.last_outmm_ev, self.ev_wfc_st]))
        self.pf_pj.append(self.dma(
            "gpsimd", self.w2t3[:, 24576:32768],
            rap(self.w2t_d, 24576, [[32 * D, 128], [1, 8192]]),
            [ev_wo_done]))

    def _build_B(self, B, waits):
        v = self.dv
        ev = v(lambda g: g.memset(B[:], 0.0), list(waits))
        for c in range(8):
            for side in range(2):
                sc = side * 8 + c
                col = 32 * c + 16 * side + 2 * c
                ev = v(lambda g, b=B, s=sc, cl=col: g.tensor_copy(
                    b[0:64, cl:cl + 1], self.warr[0:64, s:s + 1]), [ev])
                ev = v(lambda g, b=B, s=sc, cl=col: g.tensor_copy(
                    b[64:128, cl + 1:cl + 2], self.warr[64:128, s:s + 1]),
                    [ev])
        return ev

    def _perhead_c(self, ev_warr_eps, col_a, col_b):
        v = self.dv
        evm = self.step(
            "tensor",
            lambda g: g.matmul(self.ppj[0:16, 0:2], self.warr[:],
                               self.ones2[:], start=True, stop=True,
                               skip_group_check=True),
            [ev_warr_eps, self.ev_ones2, self.ev_ppj], [(self.PE, 1)])
        ev_ev = v(lambda g: g.tensor_copy(self.hd16[0:16, 0:2],
                                          self.ppj[0:16, 0:2]), [evm])
        self.ev_ppj = ev_ev
        ev_st = self.dma("gpsimd", self.ph_d[:], self.hd16[0:16, 0:2], [ev_ev])
        ev = ev_st
        for i, dstcol in enumerate((col_a, col_b)):
            ev_l = self.dma("gpsimd", self.cvec[0:16, 6:7],
                            rap(self.ph_d, i * 16, [[1, 16], [1, 1]]),
                            [ev, ev_st])
            ev = v(lambda g, dc=dstcol: g.tensor_scalar(
                out=self.cvec[0:16, dc:dc + 1], in0=self.cvec[0:16, 6:7],
                scalar1=EPS, scalar2=None, op0=ALU.mult), [ev_l])
        return ev

    def _emit(self, es):
        block = es.enter_context(self.nc.Block())

        def mkrun(ename):
            def run(eng_obj):
                seen = {}
                for waits, emit, incs in self.q[ename]:
                    for s, vv in waits:
                        if seen.get(s.name, 0) < vv:
                            eng_obj.wait_ge(s.h, vv)
                            seen[s.name] = vv
                    inst = emit(eng_obj)
                    for s, k in incs:
                        inst.then_inc(s.h, k)
            return run

        block.sync(mkrun("sync"))
        block.gpsimd(mkrun("gpsimd"))
        block.tensor(mkrun("tensor"))
        block.vector(mkrun("vector"))
        block.scalar(mkrun("scalar"))


_CACHE = {}


def _build():
    if "nc" not in _CACHE:
        _CACHE["nc"] = Builder().build()
    return _CACHE["nc"]


def kernel(**inputs):
    xs = np.ascontiguousarray(
        np.asarray(inputs["L0_attn_x"], dtype=np.float32).reshape(8 * T, D))
    com = {
        "wq": np.ascontiguousarray(np.asarray(inputs["L0_attn_Wq"], np.float32)),
        "wk": np.ascontiguousarray(np.asarray(inputs["L0_attn_Wk"], np.float32)),
        "wv": np.ascontiguousarray(np.asarray(inputs["L0_attn_Wv"], np.float32)),
        "wo": np.ascontiguousarray(np.asarray(inputs["L0_attn_Wo"], np.float32)),
        "bq": np.ascontiguousarray(np.asarray(inputs["L0_attn_bq"], np.float32).reshape(1, D)),
        "bk": np.ascontiguousarray(np.asarray(inputs["L0_attn_bk"], np.float32).reshape(1, D)),
        "bv": np.ascontiguousarray(np.asarray(inputs["L0_attn_bv"], np.float32).reshape(1, D)),
        "bo": np.ascontiguousarray(np.asarray(inputs["L0_attn_bo"], np.float32).reshape(1, D)),
        "wfc": np.ascontiguousarray(np.asarray(inputs["L0_ffn_W_fc"], np.float32)),
        "lnw": np.ascontiguousarray(np.asarray(inputs["L0_ffn_ln_w"], np.float32).reshape(1, HF)),
        "lnb": np.ascontiguousarray(np.asarray(inputs["L0_ffn_ln_b"], np.float32).reshape(1, HF)),
        "wproj": np.ascontiguousarray(np.asarray(inputs["L0_ffn_W_proj"], np.float32)),
    }
    nc = _build()
    in_maps = [dict(x=np.ascontiguousarray(xs[i * T:(i + 1) * T]),
                    gwfc=np.ascontiguousarray(com["wfc"][i * 512:(i + 1) * 512]),
                    gwproj=np.ascontiguousarray(com["wproj"][i * 128:(i + 1) * 128]),
                    **com)
               for i in range(8)]
    res = run_bass_kernel_spmd(nc, in_maps, core_ids=list(range(8)))
    out = np.concatenate([np.asarray(res.results[i]["y"]) for i in range(8)],
                         axis=0)
    return out.reshape(2, 16384, D).astype(np.float32)


# version 32
# speedup vs baseline: 1.1545x; 1.0214x over previous
import numpy as np
from contextlib import ExitStack

import concourse.bass as bass
import concourse.mybir as mybir
from concourse.bass_utils import run_bass_kernel_spmd

F32 = mybir.dt.float32
BF16 = mybir.dt.bfloat16
AF = mybir.ActivationFunctionType
ALU = mybir.AluOpType
AX = mybir.AxisListType.X

T = 4096
NT = T // 128
D = 1024
HF = 4096
EPS = 1e-6
LNEPS = 1e-5
LSEQ = 16384.0
NB = 8
BT = 512

ENGINES = ("sync", "gpsimd", "tensor", "vector", "scalar")

_STOP_AFTER = 0


class Sem:
    def __init__(self, name):
        self.name, self.n, self.h = name, 0, None

    def ev(self):
        return (self, self.n)


def rap(view, extra_off, dims):
    if isinstance(view, bass.AP):
        return bass.AP(view.tensor, view.offset + extra_off, dims)
    return bass.AP(view, extra_off, dims)


class Builder:
    def __init__(self):
        self.nc = bass.Bass("TRN2", target_bir_lowering=False)
        self.q = {e: [] for e in ENGINES}
        self.LD = Sem("LD")
        self.GD = Sem("GD")
        self.CC = Sem("CC")
        self.PE = Sem("PE")
        self.DV = Sem("DV")
        self.AC = Sem("AC")
        self.PO = Sem("PO")
        self.sems = [self.LD, self.GD, self.CC, self.PE, self.DV, self.AC,
                     self.PO]

    def step(self, eng, emit, waits=(), incs=()):
        waits = [w for w in waits if w is not None]
        for s, v in waits:
            assert v <= s.n, f"wait on future event {s.name}:{v} > {s.n}"
        self.q[eng].append((waits, emit, list(incs)))
        for s, k in incs:
            s.n += k
        return incs[0][0].ev() if incs else None

    def dma(self, eng, out, in_, waits=(), slow=False):
        sem = self.GD if eng == "gpsimd" else self.LD
        return self.step(
            eng,
            lambda g, o=out, i=in_, sl=slow: g.dma_start(
                out=o, in_=i, allow_slow_non_contiguous=sl),
            waits, [(sem, 16)])

    def mm(self, out, lhsT, rhs, start, stop, waits=()):
        return self.step(
            "tensor",
            lambda g, o=out, l=lhsT, r=rhs, s0=start, s1=stop:
            g.matmul(o, l, r, start=s0, stop=s1, skip_group_check=True),
            waits, [(self.PE, 1)])

    def tr(self, out, in_, waits=()):
        k = in_.shape[0]
        return self.step(
            "tensor",
            lambda g, o=out, i=in_, kk=k:
            g.transpose(o, i, self.ident[0:kk, 0:kk]),
            waits, [(self.PE, 1)])

    def dv(self, emit, waits=()):
        return self.step("vector", emit, waits, [(self.DV, 1)])

    def ac(self, emit, waits=()):
        return self.step("scalar", emit, waits, [(self.AC, 1)])

    def po(self, emit, waits=()):
        return self.step("gpsimd", emit, waits, [(self.PO, 1)])

    def build(self):
        nc = self.nc
        dm = {}
        self.x_in = nc.dram_tensor("x", [T, D], F32, kind="ExternalInput").ap()
        for nm in ("wq", "wk", "wv", "wo"):
            dm[nm] = nc.dram_tensor(nm, [D, D], F32, kind="ExternalInput").ap()
        for nm in ("bq", "bk", "bv", "bo"):
            dm[nm] = nc.dram_tensor(nm, [1, D], F32, kind="ExternalInput").ap()
        dm["wfc"] = nc.dram_tensor("wfc", [HF, D], F32, kind="ExternalInput").ap()
        dm["gwfc"] = nc.dram_tensor("gwfc", [HF // 8, D], F32,
                                    kind="ExternalInput").ap()
        dm["gwproj"] = nc.dram_tensor("gwproj", [D // 8, HF], F32,
                                      kind="ExternalInput").ap()
        dm["lnw"] = nc.dram_tensor("lnw", [1, HF], F32, kind="ExternalInput").ap()
        dm["lnb"] = nc.dram_tensor("lnb", [1, HF], F32, kind="ExternalInput").ap()
        dm["wproj"] = nc.dram_tensor("wproj", [D, HF], F32,
                                     kind="ExternalInput").ap()
        self.dm = dm
        self.y_out = nc.dram_tensor("y", [T, D], F32, kind="ExternalOutput").ap()

        self.x1_d = nc.dram_tensor("x1d", [T, D], F32, kind="Internal").ap()
        self.x2_d = nc.dram_tensor("x2d", [T, D], F32, kind="Internal").ap()
        self.qfm_d = nc.dram_tensor("qfmd", [128, 8 * T], BF16, kind="Internal").ap()
        self.kfm_d = nc.dram_tensor("kfmd", [128, 8 * T], BF16, kind="Internal").ap()
        self.qtm_d = nc.dram_tensor("qtmd", [T, D], BF16, kind="Internal").ap()
        self.ktm_d = nc.dram_tensor("ktmd", [T, D], BF16, kind="Internal").ap()
        self.vtm_d = nc.dram_tensor("vtmd", [T, D], BF16, kind="Internal").ap()
        self.si_d = nc.dram_tensor("sid", [T, 16], F32, kind="Internal").ap()
        self.sa_d = nc.dram_tensor("sad", [T, 16], F32, kind="Internal").ap()
        self.wfcq_d = nc.dram_tensor("wfcqd", [128, 8 * HF], BF16,
                                     kind="Internal").ap()
        self.watt_d = nc.dram_tensor("wattd", [4 * D, D], BF16,
                                     kind="Internal").ap()
        self.w2t_d = nc.dram_tensor("w2td", [128, 32 * D], BF16,
                                    kind="Internal").ap()
        self.gred_d = nc.dram_tensor("gredd", [128, 1], F32, kind="Internal").ap()
        self.gred2 = nc.dram_tensor("gred2", [1, 512], F32, kind="Internal").ap()
        self.arg_i = nc.dram_tensor("argi", [1, 2], F32, kind="Internal").ap()
        self.arg_o = nc.dram_tensor("argo", [1, 2], F32, kind="Internal").ap()
        self.c2_d = nc.dram_tensor("c2d", [128, 16], F32, kind="Internal").ap()
        self.zf_d = nc.dram_tensor("zfd", [16, 1], F32, kind="Internal").ap()
        self.ph_d = nc.dram_tensor("phd", [16, 2], F32, kind="Internal").ap()
        self.ar1i = nc.dram_tensor("ar1i", [128, 16], F32, kind="Internal").ap()
        self.ar1o = nc.dram_tensor("ar1o", [128, 16], F32, kind="Internal").ap()
        self.ar2i = nc.dram_tensor("ar2i", [16, 2048], F32, kind="Internal").ap()
        self.ar2o = nc.dram_tensor("ar2o", [16, 2048], F32, kind="Internal").ap()
        self.ar3i = nc.dram_tensor("ar3i", [128, 513], F32, kind="Internal").ap()
        self.ar3o = nc.dram_tensor("ar3o", [128, 513], F32, kind="Internal").ap()
        self.repl = [[0, 1, 2, 3], [4, 5, 6, 7]]

        with ExitStack() as es:
            self._alloc(es)
            self._program()
            self._emit(es)
        return nc

    def _alloc(self, es):
        nc = self.nc
        e = es.enter_context

        def sb(name, shape, dt):
            return e(nc.sbuf_tensor(name, shape, dt))

        self.arw = sb("arw", [128, 65536], BF16)
        arw = self.arw
        self.wfcq = arw[:, 0:32768]
        self.w2t = arw[:, 32768:65536]
        self.wqf = arw[:, 0:8192]
        self.wkf = arw[:, 8192:16384]
        self.wvf = arw[:, 16384:24576]
        self.wof = arw[:, 24576:32768]
        self.rb = [arw[:, 32768 + i * 4096: 32768 + (i + 1) * 4096]
                   for i in range(8)]
        self.wfcq3 = arw[:, 32768:65536]
        self.w2t3 = arw[:, 0:32768]

        self.art = sb("art", [128, 18432], BF16)
        art = self.art
        artf_h = art.bitcast(F32)
        self.artf = artf_h
        self.AFW = 9216
        self.ATW = 18432
        self.h_b = [art[:, 0:4096], art[:, 4096:8192]]
        self.sqb = art[:, 8192:10240]
        self.gT_b = [art[:, 10240:14336], art[:, 14336:18432]]
        self.wnat = rap(artf_h, 0, [[self.AFW, 128], [1, 4096]])
        self.wnat_half = [rap(artf_h, s * 2048, [[self.AFW, 128], [1, 2048]])
                          for s in range(2)]
        self.qsc = rap(artf_h, 4096, [[self.AFW, 128], [1, 2048]])
        self.wqnat = art[:, 12288:16384]
        self.qtm_b = [art[:, 0:1024], art[:, 1024:2048]]
        self.ktm_b = [art[:, 2048:3072], art[:, 3072:4096]]
        self.vtm_b = [art[:, 4096:5120], art[:, 5120:6144]]
        self.qfm_b = [art[:, 6144:7168], art[:, 7168:8192]]
        self.kfm_b = [art[:, 8192:9216], art[:, 9216:10240]]
        self.bqb = art[:, 10240:11264]
        self.bkb = art[:, 11264:12288]
        self.bvb = art[:, 12288:13312]
        self.dhm = art[:, 0:1024]
        self.sio_tm = art[:, 1024:1152]
        self.etm = art[:, 1152:1280]
        self.vsc_b = [art[:, 2048:3072], art[:, 3072:4096]]
        self.attntm_b = [art[:, 4096:5120], art[:, 5120:6144]]
        self.attnfm_b = [art[:, 6144:7168], art[:, 7168:8192]]
        self.kvblocks = art[:, 9216:10240]
        def fview(c0, n):
            return rap(artf_h, c0, [[self.AFW, 128], [1, n]])
        self.siorec = fview(5120, 128)
        self.satm = fview(5248, 64)
        self.stot = fview(5312, 64)
        self.sitm_ld = fview(5376, 64)
        self.satm_ld = fview(5440, 64)
        self.kvr = fview(5504, 512)
        self.kvsb = self.kvr
        self.bob = fview(6016, 1024)
        self.qsi_sb = rap(artf_h, 7040, [[self.AFW, 16], [1, 1024]])
        self.kso_sb = rap(artf_h, 8064, [[self.AFW, 32], [1, 1024]])

        self.xbuf = [sb(f"xbuf{i}", [128, D], F32) for i in range(4)]
        self.xnbf = [sb(f"xnbf{i}", [128, D], BF16) for i in range(2)]
        self.xnT = [sb(f"xnT{i}", [128, D], BF16) for i in range(2)]
        self.c2h = sb("c2h", [128, D], F32)
        self.wbc = sb("wbc", [128, HF], BF16)
        self.ident = sb("ident", [128, 128], BF16)
        self.identf = sb("identf", [128, 128], F32)
        self.ones2 = sb("ones2", [128, 2], F32)
        self.c2row = sb("c2row", [16, 128], F32)
        self.zeros = sb("zeros", [128, 1], F32)
        self.stat = sb("stat", [128, 64], F32)
        self.sv = sb("sv", [128, 48], F32)
        self.svrow = sb("svrow", [1, 132], F32)
        self.c2sb = sb("c2sb", [128, 16], F32)
        self.cs2b = sb("cs2b", [128, D], F32)
        self.B1 = sb("B1", [128, 256], BF16)
        self.B2 = sb("B2", [128, 256], BF16)
        self.warr = sb("warr", [128, 16], F32)
        self.accA = sb("accA", [128, 16], F32)
        self.hd16 = sb("hd16", [16, 2], F32)
        self.cvec = sb("cvec", [16, 8], F32)
        self.zacc = sb("zacc", [16, 4], F32)
        self.zbc = sb("zbc", [128, 16], F32)
        self.cst = sb("cst", [128, 3], F32)
        self.onesrow = self.wbc[0:1, 0:128]
        self.brow = self.wbc[0:1, 128:3328]

        self.pfc = [e(nc.psum_tensor(f"pfc{i}", [128, 1024], F32))
                    for i in range(2)]
        self.ppj = e(nc.psum_tensor("ppj", [128, 1024], F32))
        self.ptr = [e(nc.psum_tensor(f"ptr{i}", [128, 1024], BF16))
                    for i in range(2)]

        for s in self.sems:
            s.h = e(nc.semaphore(s.name))

        self.pfcb = [p.bitcast(BF16) for p in self.pfc]
        self.pd_a = self.pfc[0][0:16, 0:512]
        self.pd_b = self.pfc[0][0:16, 512:1024]
        self.pws_q = self.pfc[1][0:32, :]
        self.pws_k = self.ppj[0:32, :]
        self.pkv = self.pfc[1][:, :]

    def _program(self):
        self._phase0()
        dst1 = self.y_out if _STOP_AFTER == 1 else self.x1_d
        wready = [self.ev_wfc, self.ev_w2, self.ev_c2h]
        self.ev_p1 = self._ffn_phase(self.x_in, dst1, self.wfcq, self.w2t,
                                     wready, init_war=None)
        if _STOP_AFTER == 1:
            self._final()
            return
        self._attention()
        if _STOP_AFTER == 2:
            self._final()
            return
        barrier = self.ev_p2[NT - 1]
        self._ffn_phase(self.x2_d, self.y_out, self.wfcq3, self.w2t3,
                        self.pf_fc, wready_pj=self.pf_pj,
                        src_ready=self.ev_p2, init_war=barrier)
        self._final()

    def _final(self):
        self.step("sync", lambda g: g.nop(), [self.LD.ev()])
        self.step("gpsimd", lambda g: g.nop(), [self.GD.ev()])

    def _phase0(self):
        v = self.dv
        e0 = v(lambda g: g.memset(self.zeros[:], 0.0))
        e0 = v(lambda g: g.memset(self.cst[:, 0:1], LNEPS), [e0])
        e0 = v(lambda g: g.memset(self.cst[:, 1:2], -1.0), [e0])
        e0 = v(lambda g: g.memset(self.cst[:, 2:3], 4.0 * LNEPS), [e0])
        p0 = self.po(lambda g: g.memset(self.ident[:], 0.0))
        self.ev_id = self.po(
            lambda g: g.affine_select(out=self.ident[:], in_=self.ident[:],
                                      compare_op=ALU.not_equal, fill=1.0,
                                      base=0, pattern=[[-1, 128]],
                                      channel_multiplier=1), [p0])
        p1 = self.po(lambda g: g.memset(self.identf[:], 0.0), [self.ev_id])
        self.ev_idf = self.po(
            lambda g: g.affine_select(out=self.identf[:], in_=self.identf[:],
                                      compare_op=ALU.not_equal, fill=1.0,
                                      base=0, pattern=[[-1, 128]],
                                      channel_multiplier=1), [p1])
        e1 = v(lambda g: g.memset(self.ones2[:], 0.0), [e0])
        e1 = v(lambda g: g.memset(self.ones2[0:64, 0:1], 1.0), [e1])
        self.ev_ones2 = v(lambda g: g.memset(self.ones2[64:128, 1:2], 1.0),
                          [e1])
        self.ev_wbcl = self.dma("gpsimd", self.wbc[:],
                                rap(self.dm["lnw"], 0, [[0, 128], [1, HF]]))
        cs2b16 = self.cs2b.bitcast(BF16)
        c2h16 = self.c2h.bitcast(BF16)
        self.bbcq = [cs2b16[:, 0:1024], cs2b16[:, 1024:2048],
                     c2h16[:, 0:1024], c2h16[:, 1024:2048]]
        ev_b = None
        for q in range(4):
            ev_b = self.dma("gpsimd", self.bbcq[q],
                            rap(self.dm["lnb"], q * 1024,
                                [[0, 128], [1, 1024]]))
        self.ev_bbcl = ev_b

        self.ev_ppj = None
        self.ev_ptr = [None, None]
        self._trbatch = 0

        ntot = float(HF * D)
        arwf0 = self.arw.bitcast(F32)
        gB = rap(arwf0, 16384, [[32768, 128], [1, 4096]])
        ev_lA = self.dma("sync", self.wnat,
                         rap(self.dm["gwfc"], 0,
                             [[D, 128], [128 * D, 4], [1, D]]))
        ev_lB = self.dma("gpsimd", gB,
                         rap(self.dm["gwproj"], 0, [[HF, 128], [1, HF]]))
        ev_rA = v(lambda g: g.tensor_reduce(out=self.sv[:, 0:1],
                                            in_=self.wnat, axis=AX, op=ALU.add,
                                            apply_absolute_value=True), [ev_lA])
        ev_rB = v(lambda g: g.tensor_reduce(out=self.sv[:, 1:2],
                                            in_=gB, axis=AX, op=ALU.add,
                                            apply_absolute_value=True), [ev_lB])
        ev_s1 = self.dma("gpsimd", rap(self.gred2, 0, [[2, 128], [1, 2]]),
                         self.sv[:, 0:2], [ev_rA, ev_rB])
        qrow = rap(self.artf, 4096, [[self.AFW, 1], [1, 256]])
        ev_l1 = self.dma("gpsimd", qrow,
                         rap(self.gred2, 0, [[0, 1], [1, 256]]), [ev_s1])
        vAB = rap(self.artf, 4096, [[self.AFW, 1], [1, 2], [2, 128]])
        orow = rap(self.artf, 4096 + 256, [[self.AFW, 1], [1, 2]])
        ev_r2 = v(lambda g: g.tensor_reduce(out=orow, in_=vAB, axis=AX,
                                            op=ALU.add), [ev_l1])
        ev_si = self.dma("gpsimd", self.arg_i, orow, [ev_r2])
        ev_cc = self.step("gpsimd",
                          lambda g: g.collective_compute(
                              "AllReduce", ALU.add,
                              replica_groups=[[0, 1, 2, 3, 4, 5, 6, 7]],
                              ins=[self.arg_i.opt()], outs=[self.arg_o.opt()]),
                          [ev_si], [(self.CC, 1)])
        grow = rap(self.artf, 4096 + 260, [[self.AFW, 1], [1, 2]])
        trow = rap(self.artf, 4096 + 262, [[self.AFW, 1], [1, 2]])
        ev_lg = self.dma("gpsimd", grow, self.arg_o, [ev_cc])
        ev_t = v(lambda g: g.tensor_scalar(out=trow, in0=grow,
                                           scalar1=0.5 / ntot, scalar2=0.5e-6,
                                           op0=ALU.mult, op1=ALU.add), [ev_lg])
        ev_ts = self.dma("gpsimd", rap(self.gred2, 256, [[0, 1], [1, 2]]),
                         trow, [ev_t])
        ev_tl = self.dma("gpsimd", self.sv[:, 2:4],
                         rap(self.gred2, 256, [[0, 128], [1, 2]]), [ev_ts])
        ev_nthr = v(lambda g: g.tensor_scalar(out=self.sv[:, 4:6],
                                              in0=self.sv[:, 2:4],
                                              scalar1=-1.0, scalar2=None,
                                              op0=ALU.mult), [ev_tl])

        x_v0 = self.x_in.rearrange("(n p) c -> n p c", p=128)
        self.ev_xpre = []
        evp = ev_lA
        for t in range(4):
            evp = self.dma("sync", self.xbuf[t][:], x_v0[t], [evp])
            self.ev_xpre.append(evp)

        self.ev_wfc = self._quantize(
            self.dm["wfc"], HF, D, self.wfcq, False,
            thr=self.sv[:, 2:3], nthr=self.sv[:, 4:5], ev_thr=ev_nthr,
            bufB_off=16384, prev_state=(ev_rB, ev_rB, None, None, None))
        fc_state = self._q_last
        self.ev_w2 = self._quantize(
            self.dm["wproj"], D, HF, self.w2t, True,
            thr=self.sv[:, 3:4], nthr=self.sv[:, 5:6], ev_thr=ev_nthr,
            prev_state=fc_state)
        self.ev_sqb_list = [self._q_last[2], self._q_last[3]]

        ev_c2t = self.step(
            "tensor",
            lambda g: g.transpose(self.ppj[0:16, 0:128], self.c2sb[:],
                                  self.identf[:]),
            self.evs_c2 + [self.ev_idf, self.ev_ppj], [(self.PE, 1)])
        ev_c2r = v(lambda g: g.tensor_copy(self.c2row[:],
                                           self.ppj[0:16, 0:128]),
                   [ev_c2t])
        self.ev_ppj = ev_c2r
        ev_c2s = self.dma("gpsimd",
                          rap(self.c2_d, 0, [[128, 16], [1, 128]]),
                          self.c2row[:], [ev_c2r])
        ev_c2l = self.dma("gpsimd", self.c2h[:],
                          rap(self.c2_d, 0, [[0, 128], [1, 1024]]),
                          [ev_c2s])
        ev_cs2l = self.dma("gpsimd", self.cs2b[:],
                           rap(self.c2_d, 1024, [[0, 128], [1, 1024]]),
                           [ev_c2s])
        self.ev_cs2b = ev_cs2l
        self.ev_c2h = v(lambda g: g.tensor_scalar(
            out=self.c2h[:], in0=self.c2h[:], scalar1=0.25, scalar2=None,
            op0=ALU.mult), [ev_c2l])
        self.ev_wfc_st = self.dma("gpsimd", self.wfcq_d[:], self.wfcq,
                                  [self.ev_wfc])
        self.ev_w2_st = self.dma("gpsimd", self.w2t_d[:], self.w2t,
                                 [self.ev_w2])
        self.ev_cast = [self.dma("gpsimd",
                                 rap(self.watt_d, w * D * D, [[D, D], [1, D]]),
                                 self.dm[nm], [])
                        for w, nm in enumerate(("wq", "wk", "wv", "wo"))]

    def _quantize(self, w_dram, nrow, ncol, fm_dst, is_proj, thr, nthr,
                  ev_thr, bufB_off=None, prev_state=None):
        v, a = self.dv, self.ac
        nchunk = nrow // 128
        per_load = 4096 // ncol
        nloads = nchunk // per_load
        arwf = self.arw.bitcast(F32)
        nbuf = 2 if bufB_off is not None else 1
        if prev_state is not None:
            prev_subA, prev_subB, prev_sg2, prev_sg3, prev_tr = prev_state
        else:
            prev_subA = prev_subB = prev_sg2 = prev_sg3 = prev_tr = None

        def buf_q(j, s2):
            if bufB_off is not None and j % 2 == 1:
                return rap(arwf, bufB_off + s2 * 1024,
                           [[32768, 128], [1, 1024]])
            return rap(self.artf, s2 * 1024, [[self.AFW, 128], [1, 1024]])

        def buf_half(j, h):
            if bufB_off is not None and j % 2 == 1:
                return rap(arwf, bufB_off + h * 2048,
                           [[32768, 128], [1, 2048]])
            return rap(self.artf, h * 2048, [[self.AFW, 128], [1, 2048]])

        def load_half(j, h):
            if per_load > 1:
                return rap(w_dram,
                           j * per_load * 128 * ncol + h * 2 * 128 * ncol,
                           [[ncol, 128], [128 * ncol, 2], [1, ncol]])
            return rap(w_dram, j * 128 * ncol + h * 2048,
                       [[ncol, 128], [1, 2048]])

        if is_proj:
            self.evs_c2 = []
        ev_chunk_done = prev_tr
        ev_fm = None
        prev_cs = None
        subs = {}
        signs = {}
        c2last = {}
        sgb = [self.art[:, 8192 + k * 1024:8192 + (k + 1) * 1024]
               for k in range(4)]
        wq_quarter = lambda qq: self.wqnat[:, qq * 1024:(qq + 1) * 1024]
        for j in range(nloads):
            w_ld = []
            if j - nbuf >= 0:
                w_ld.append(subs[(j - nbuf, 3)])
            elif bufB_off is None or j % 2 == 0:
                w_ld.append(prev_subA)
            else:
                w_ld.append(prev_subB)
            ldh = [self.dma("sync", buf_half(j, 0), load_half(j, 0), w_ld),
                   self.dma("gpsimd", buf_half(j, 1), load_half(j, 1), w_ld)]
            sub = []
            for s2 in range(4):
                wsl = buf_q(j, s2)
                s1v = sgb[2 * (s2 % 2)]
                s2v = sgb[2 * (s2 % 2) + 1]
                if s2 >= 2:
                    w_sc = [signs[(j, s2 - 2)]]
                elif j > 0:
                    w_sc = [signs[(j - 1, s2 + 2)]]
                else:
                    w_sc = [prev_sg2 if s2 == 0 else prev_sg3]
                a(lambda g, aa=wsl, o=s1v, nn=nthr:
                  g.activation(out=o, in_=aa, func=AF.Sign, bias=nn,
                               scale=1.0),
                  [ldh[s2 // 2]] + w_sc
                  + ([ev_thr] if (j == 0 and s2 == 0) else []))
                ev_s2 = a(lambda g, aa=wsl, o=s2v, tt=thr:
                          g.activation(out=o, in_=aa, func=AF.Sign, bias=tt,
                                       scale=1.0), [])
                subs[(j, s2)] = ev_s2
                ev_ad = self.po(lambda g, o=wq_quarter(s2), i1=s1v, i2=s2v:
                                g.tensor_tensor(out=o, in0=i1, in1=i2,
                                                op=ALU.add),
                                [ev_s2] + ([ev_chunk_done, prev_cs]
                                           if s2 == 0 else []))
                signs[(j, s2)] = ev_ad
                sub.append(ev_ad)
            ev_q = sub[-1]

            evq_r = list(sub)
            if is_proj:
                ev_cq = None
                ev_mul = None
                for q2 in range(4):
                    ev_cq = v(lambda g, qq2=q2: g.scalar_tensor_tensor(
                        out=rap(self.artf, 8192, [[self.AFW, 128], [1, 1024]]),
                        in0=self.wqnat[:, qq2 * 1024:(qq2 + 1) * 1024],
                        scalar=1.0, in1=self.bbcq[qq2],
                        op0=ALU.mult, op1=ALU.mult,
                        accum_out=self.sv[:, 8 + qq2:9 + qq2]),
                        [sub[q2], self.ev_bbcl, ev_cq])
                    ev_mul = v(lambda g, qq2=q2: g.scalar_tensor_tensor(
                        out=self.wqnat[:, qq2 * 1024:(qq2 + 1) * 1024],
                        in0=self.wqnat[:, qq2 * 1024:(qq2 + 1) * 1024],
                        scalar=1.0,
                        in1=self.wbc[:, qq2 * 1024:(qq2 + 1) * 1024],
                        op0=ALU.mult, op1=ALU.mult,
                        accum_out=self.sv[:, 16 + qq2:17 + qq2]),
                        [ev_cq, self.ev_wbcl])
                    evq_r[q2] = ev_mul
                c2last[j] = ev_cq
                self._c2scratch_last = ev_cq
                ev_cb = self.po(lambda g: g.tensor_add(
                    self.sv[:, 12:13], self.sv[:, 8:9], self.sv[:, 9:10]),
                    [ev_cq])
                ev_cb2 = self.po(lambda g: g.tensor_add(
                    self.sv[:, 13:14], self.sv[:, 10:11], self.sv[:, 11:12]),
                    [ev_cb])
                ev_cc = self.po(lambda g, jj=j: g.tensor_add(
                    self.c2sb[:, jj:jj + 1], self.sv[:, 12:13],
                    self.sv[:, 13:14]), [ev_cb2])
                self.evs_c2.append(ev_cc)
                ev_cs1 = self.po(lambda g: g.tensor_add(
                    self.sv[:, 20:21], self.sv[:, 16:17], self.sv[:, 17:18]),
                    [ev_mul])
                ev_cs2b = self.po(lambda g: g.tensor_add(
                    self.sv[:, 21:22], self.sv[:, 18:19], self.sv[:, 19:20]),
                    [ev_cs1])
                ev_cs = self.po(lambda g, jj=j: g.tensor_add(
                    self.c2sb[:, 8 + jj:9 + jj], self.sv[:, 20:21],
                    self.sv[:, 21:22]), [ev_cs2b])
                self.evs_c2.append(ev_cs)
                prev_cs = ev_cs
                ev_q = ev_mul

            percol = nrow
            last_tr = None
            for cc in range(per_load):
                rowchunk = j * per_load + cc
                nblk = ncol // 128
                for b0 in range(0, nblk, 8):
                    nb_in = min(8, nblk - b0)
                    par = self._trbatch % 2
                    self._trbatch += 1
                    pb = self.ptr[par]
                    quarter = cc if per_load > 1 else b0 // 8
                    for b in range(nb_in):
                        w0 = [evq_r[3], self.ev_id]
                        if b == 0:
                            w0.append(self.ev_ptr[par])
                        last_tr = self.tr(
                            pb[:, b * 128:(b + 1) * 128],
                            self.wqnat[:, cc * ncol + (b0 + b) * 128:
                                       cc * ncol + (b0 + b + 1) * 128], w0)
                    dst = rap(fm_dst, b0 * percol + rowchunk * 128,
                              [[65536, 128], [percol, nb_in], [1, 128]])
                    src = rap(pb[:, 0:1], 0, [[1024, 128], [128, nb_in],
                                              [1, 128]])
                    w_fm = [last_tr]
                    if j == 0 and cc == 0 and b0 == 0:
                        w_fm.append(prev_subB)
                    if par == 0:
                        ev_fm = self.ac(lambda g, d=dst, s=src: g.copy(d, s),
                                        w_fm)
                    else:
                        ev_fm = self.dv(lambda g, d=dst, s=src:
                                        g.tensor_copy(d, s), w_fm)
                    self.ev_ptr[par] = ev_fm
            ev_chunk_done = last_tr
        jA = nloads - nbuf
        self._q_last = (subs[(jA, 3)], subs[(nloads - 1, 3)],
                        signs[(nloads - 1, 2)], signs[(nloads - 1, 3)],
                        ev_chunk_done)
        return ev_fm

    def _ln(self, xb, dcols, slot, waits):
        st = self.stat
        mu = st[:, slot:slot + 1]
        rstd = st[:, slot + 1:slot + 2]
        t1 = st[:, slot + 2:slot + 3]
        t2 = st[:, slot + 3:slot + 4]
        q1 = st[:, slot + 4:slot + 5]
        q2 = st[:, slot + 5:slot + 6]
        p, v, a = self.po, self.dv, self.ac
        ev_s = v(lambda g: g.tensor_reduce(out=t1, in_=xb, axis=AX, op=ALU.add),
                 list(waits))
        ha = dcols // 2
        ev_q1 = a(lambda g: g.activation(out=self.sqb[:, 0:ha], in_=xb[:, 0:ha],
                                         func=AF.Square, accum_out=q1),
                  list(waits))
        ev_q2 = a(lambda g: g.activation(out=self.sqb[:, 0:ha],
                                         in_=xb[:, ha:dcols],
                                         func=AF.Square, accum_out=q2), [ev_q1])
        ev_ss = p(lambda g: g.tensor_add(t2, q1, q2), [ev_q2])
        inv = 1.0 / dcols
        ev_mu = p(lambda g: g.tensor_scalar(out=mu, in0=t1, scalar1=inv,
                                            scalar2=None, op0=ALU.mult), [ev_s])
        ev_e2 = p(lambda g: g.tensor_scalar(out=t2, in0=t2, scalar1=inv,
                                            scalar2=None, op0=ALU.mult), [ev_ss])
        ev_m2 = p(lambda g: g.tensor_mul(t1, mu, mu), [ev_mu])
        ev_var = p(lambda g: g.tensor_sub(t2, t2, t1), [ev_e2, ev_m2])
        ev_sd = a(lambda g: g.activation(out=t1, in_=t2, func=AF.Sqrt,
                                         bias=self.cst[:, 0:1], scale=1.0),
                  [ev_var])
        ev_rs = v(lambda g: g.reciprocal(rstd, t1), [ev_sd])
        return ev_rs, mu, rstd, ev_q2

    def _ffn_phase(self, src_ap, dst_ap, wfc, w2t, wready,
                   src_ready=None, init_war=None, wready_pj=None):
        if wready_pj is None:
            wready_pj = wready
        x_v = src_ap.rearrange("(n p) c -> n p c", p=128)
        d_v = dst_ap.rearrange("(n p) c -> n p c", p=128)
        ev_done = [None] * NT
        ev_hfree = [None, None]
        ev_gTfree = [None, None]
        ev_pfc = [None, None]
        fa = [None] * (NT + 1)
        fx = [None] * (NT + 1)
        hl = [None] * NT
        v, a = self.dv, self.ac
        st = self.stat

        def xslot(t):
            return 16 * (t % 2)

        def hslot(t):
            return 32 + 16 * (t % 2)

        flx = [None] * (NT + 1)

        def frontA(t):
            par = t % 4
            if t < 4 and src_ready is None and init_war is None:
                flx[t] = self.ev_xpre[t]
                return
            w = [ev_done[t - 4] if t >= 4 else init_war]
            if src_ready is not None:
                w.append(src_ready[t])
            flx[t] = self.dma("sync", self.xbuf[par][:], x_v[t], w)

        fsum = [None] * (NT + 1)

        def frontA_sum(t):
            par = t % 4
            ev_xld = flx[t]
            s = xslot(t)
            t1 = st[:, s + 2:s + 3]
            mu = st[:, s:s + 1]
            p = self.po
            ev_s = v(lambda g: g.tensor_reduce(out=t1, in_=self.xbuf[par][:],
                                               axis=AX, op=ALU.add), [ev_xld])
            ev_mu = p(lambda g: g.tensor_scalar(out=mu, in0=t1,
                                                scalar1=1.0 / D, scalar2=None,
                                                op0=ALU.mult), [ev_s])
            ev_xc = v(lambda g, pp=par: g.tensor_scalar(
                out=self.xnbf[t % 2][:], in0=self.xbuf[pp][:], scalar1=mu,
                scalar2=None, op0=ALU.subtract),
                [ev_mu, fx_war[t % 2]])
            ev_m2 = p(lambda g: g.tensor_mul(t1, mu, mu), [ev_mu])
            fsum[t] = (ev_m2, ev_xld)
            xcev[t] = ev_xc

        def frontA_stats(t):
            par = t % 4
            ev_m2, ev_xld = fsum[t]
            s = xslot(t)
            t1 = st[:, s + 2:s + 3]
            var = st[:, s + 3:s + 4]
            q1 = st[:, s + 4:s + 5]
            q2 = st[:, s + 5:s + 6]
            p = self.po
            ev_q1 = a(lambda g: g.activation(out=self.sqb[:, 0:512],
                                             in_=self.xbuf[par][:, 0:512],
                                             func=AF.Square, accum_out=q1),
                      [ev_xld] + getattr(self, "ev_sqb_list", []))
            ev_q2 = a(lambda g: g.activation(out=self.sqb[:, 0:512],
                                             in_=self.xbuf[par][:, 512:1024],
                                             func=AF.Square, accum_out=q2),
                      [ev_q1])
            ev_ss = p(lambda g: g.tensor_add(var, q1, q2),
                      [ev_q2, sdev[t - 2] if t >= 2 else None])
            ev_e2 = p(lambda g: g.tensor_scalar(out=var, in0=var,
                                                scalar1=1.0 / D, scalar2=None,
                                                op0=ALU.mult), [ev_ss])
            ev_var = p(lambda g: g.tensor_sub(var, var, t1),
                       [ev_e2, ev_m2])
            ev_pre = v(lambda g, pp=par: g.tensor_add(
                self.xbuf[pp][:], self.xbuf[pp][:], self.c2h[:]),
                [xcev[t], ev_q2, self.ev_c2h])
            preev[t] = ev_pre
            fa[t] = (ev_var, ev_q2, ev_xld)

        def finish_x(t, extra=()):
            s = xslot(t)
            rstd = st[:, s + 1:s + 2]
            var = st[:, s + 3:s + 4]
            sd = st[:, s + 6:s + 7]
            ev_sd = a(lambda g: g.activation(out=sd, in_=var, func=AF.Sqrt,
                                             bias=self.cst[:, 2:3], scale=4.0),
                      [fa[t][0]] + list(extra))
            sdev[t] = ev_sd
            ev_rs = v(lambda g: g.reciprocal(rstd, sd), [ev_sd])
            rsev[t] = ev_rs
            return ev_rs

        fx_war = [None, None]
        preev = [None] * NT
        bk = [None] * NT
        rsev = [None] * NT
        sdev = [None] * NT
        xcev = [None] * NT

        def xnT_tr(t, ev_xn):
            par2 = t % 2
            evt = None
            for b in range(8):
                w0 = [ev_xn, self.ev_id]
                if b == 0:
                    w0.append(self.ev_ptr[par2])
                evt = self.tr(self.ptr[par2][:, b * 128:(b + 1) * 128],
                              self.xnbf[par2][:, b * 128:(b + 1) * 128], w0)
            ev_xnT = v(lambda g, pp=par2: g.tensor_copy(
                self.xnT[pp][:], self.ptr[pp][:]), [evt])
            self.ev_ptr[par2] = ev_xnT
            fx_war[par2] = evt
            fx[t] = ev_xnT

        def fc_quarters(t, qlist):
            par2 = t % 2
            ev_xnT = fx[t]
            hs = st[:, hslot(t): hslot(t) + 4]
            ev_gelu = None
            for qq in qlist:
                pb = self.pfc[qq % 2]
                w0 = [ev_xnT, ev_pfc[qq % 2]] + list(wready)
                evm = None
                for dc in range(8):
                    for half in range(2):
                        col0 = dc * 4096 + qq * 1024 + half * 512
                        evm = self.mm(
                            pb[:, half * 512:(half + 1) * 512],
                            self.xnT[par2][:, dc * 128:(dc + 1) * 128],
                            wfc[:, col0:col0 + 512],
                            start=(dc == 0), stop=(dc == 7),
                            waits=w0 if (dc == 0 and half == 0) else [])
                rsx = st[:, xslot(t) + 1:xslot(t) + 2]
                ev_gelu = a(lambda g, pp=t % 2, q2=qq, pb2=pb, h2=hs, r3=rsx:
                            g.activation(
                                out=self.h_b[pp][:, q2 * 1024:(q2 + 1) * 1024],
                                in_=pb2[:], func=AF.Gelu, scale=r3,
                                accum_out=h2[:, q2:q2 + 1]),
                            [evm, rsev[t],
                             ev_hfree[t % 2] if qq == 0 else None])
                ev_pfc[qq % 2] = ev_gelu
            return ev_gelu

        def hln(t, ev_gelu, last):
            par2 = t % 2
            s = hslot(t)
            hs = st[:, s:s + 4]
            t1 = st[:, s + 4:s + 5]
            var = st[:, s + 5:s + 6]
            muh = st[:, s + 6:s + 7]
            rsh = st[:, s + 7:s + 8]
            q1 = st[:, s + 8:s + 9]
            q2s = st[:, s + 9:s + 10]
            sd = st[:, s + 10:s + 11]
            nr2 = st[:, s + 11:s + 12]
            p = self.po
            ev_hsum = v(lambda g: g.tensor_reduce(out=t1, in_=hs, axis=AX,
                                                  op=ALU.add), [ev_gelu])
            ev_sq1 = a(lambda g: g.activation(
                out=self.sqb[:], in_=self.h_b[par2][:, 0:2048],
                func=AF.Square, accum_out=q1), [ev_gelu])
            ev_sq2 = a(lambda g: g.activation(
                out=self.sqb[:], in_=self.h_b[par2][:, 2048:4096],
                func=AF.Square, accum_out=q2s), [ev_sq1])
            ev_ss = p(lambda g: g.tensor_add(var, q1, q2s), [ev_sq2])
            ev_mu = p(lambda g: g.tensor_scalar(out=muh, in0=t1,
                                                scalar1=1.0 / HF, scalar2=None,
                                                op0=ALU.mult), [ev_hsum])
            ev_e2 = p(lambda g: g.tensor_scalar(out=var, in0=var,
                                                scalar1=1.0 / HF, scalar2=None,
                                                op0=ALU.mult), [ev_ss])
            ev_m2 = p(lambda g: g.tensor_mul(t1, muh, muh), [ev_mu])
            ev_var = p(lambda g: g.tensor_sub(var, var, t1), [ev_e2, ev_m2])
            ev_sd = a(lambda g: g.activation(out=sd, in_=var, func=AF.Sqrt,
                                             bias=self.cst[:, 0:1], scale=1.0),
                      [ev_var])
            if not last:
                finish_x(t + 1, extra=[ev_sd])
            ev_rs = v(lambda g: g.reciprocal(rsh, sd), [ev_sd])
            ev_nr2 = v(lambda g: g.tensor_scalar(out=nr2, in0=rsh,
                                                 scalar1=-0.25, scalar2=None,
                                                 op0=ALU.mult), [ev_rs])
            hl[t] = (ev_gelu, ev_mu, ev_nr2)
            return None

        def backA(t):
            par2 = t % 2
            ev_h, ev_muh, ev_nr2 = hl[t]
            ev_last_t = None
            evs = []
            for rnd in range(4):
                pb = self.ptr[par2 if rnd % 2 == 0 else 1 - par2][:, 0:1024]
                for b in range(8):
                    blk = rnd * 8 + b
                    w0 = [ev_h, self.ev_id]
                    if b == 0:
                        if rnd < 2:
                            w0.append(self.ev_ptr[par2] if rnd == 0
                                      else self.ev_ptr[1 - par2])
                        else:
                            w0.append(evs[rnd - 2])
                    ev_last_t = self.tr(
                        pb[:, b * 128:(b + 1) * 128],
                        self.h_b[par2][:, blk * 128:(blk + 1) * 128], w0)
                w1 = [ev_last_t, ev_gTfree[par2] if rnd == 0 else None]
                if rnd % 2 == 0:
                    ev_ev = a(lambda g, pp=par2, r2=rnd, pb2=pb: g.copy(
                        self.gT_b[pp][:, r2 * 1024:(r2 + 1) * 1024], pb2[:]),
                        w1)
                else:
                    ev_ev = v(lambda g, pp=par2, r2=rnd, pb2=pb: g.tensor_copy(
                        self.gT_b[pp][:, r2 * 1024:(r2 + 1) * 1024], pb2[:]),
                        w1)
                evs.append(ev_ev)
            self.ev_ptr[par2] = evs[2]
            self.ev_ptr[1 - par2] = evs[3]
            ev_hfree[par2] = ev_last_t
            bk[t] = (evs, ev_muh, ev_nr2)
            dbg = getattr(self, "_dbg", None)
            if dbg is not None and t == 0:
                ev_gTfree[par2] = self.dma("sync", dbg["gt"],
                                           self.gT_b[par2][:], [evs[3]])

        def back(t):
            par2 = t % 2
            evs_gT, ev_muh, ev_nr2 = bk[t]
            s = hslot(t)
            muh = st[:, s + 6:s + 7]
            nr2 = st[:, s + 11:s + 12]
            evm = None
            for hc in range(32):
                for half in range(2):
                    if hc % 8 == 0 and half == 0:
                        w0 = [evs_gT[hc // 8]]
                        if hc == 0:
                            w0 += [self.ev_ppj] + list(wready_pj)
                    else:
                        w0 = []
                    evm = self.mm(
                        self.ppj[:, half * 512:(half + 1) * 512],
                        self.gT_b[par2][:, hc * 128:(hc + 1) * 128],
                        w2t[:, hc * 1024 + half * 512:
                            hc * 1024 + half * 512 + 512],
                        start=(hc == 0), stop=(hc == 31),
                        waits=w0)
            ev_gTfree[par2] = evm
            self.last_proj_ev = evm
            ev_u1 = v(lambda g: g.scalar_tensor_tensor(
                out=self.ppj[:], in0=self.cs2b[:], scalar=muh,
                in1=self.ppj[:], op0=ALU.mult, op1=ALU.subtract),
                [evm, ev_muh, self.ev_cs2b])
            xb = self.xbuf[t % 4]
            ev_x2 = v(lambda g: g.scalar_tensor_tensor(
                out=xb[:], in0=self.ppj[:], scalar=nr2,
                in1=xb[:], op0=ALU.mult, op1=ALU.add),
                [ev_u1, ev_nr2, preev[t]])
            self.ev_ppj = ev_x2
            ev_done[t] = self.dma("sync", d_v[t], xb[:], [ev_x2])

        dbg = getattr(self, "_dbg", None)
        frontA(0)
        frontA_sum(0)
        frontA_stats(0)
        xnT_tr(0, xcev[0])
        ev_r0 = finish_x(0)
        frontA(1)
        frontA_sum(1)
        frontA_stats(1)
        xnT_tr(1, xcev[1])
        finish_x(1, extra=[ev_r0])
        for t in range(NT):
            if t + 2 < NT:
                frontA(t + 2)
                frontA_sum(t + 2)
            ev_g01 = fc_quarters(t, (0, 1))
            ev_g23 = fc_quarters(t, (2, 3))
            if dbg is not None and t == 0:
                ev_hfree[0] = self.dma("sync", dbg["h"], self.h_b[0][:],
                                       [ev_g23])
            if t >= 1:
                backA(t - 1)
            if t >= 1:
                back(t - 1)
            if t + 2 < NT:
                frontA_stats(t + 2)
            hln(t, ev_g23, last=(t == 0 or t + 1 >= NT))
            if t + 2 < NT:
                xnT_tr(t + 2, xcev[t + 2])
            if t == NT - 1:
                self.last_fc_ev = ev_g23
        backA(NT - 1)
        back(NT - 1)
        return ev_done

    def _attention(self):
        v, a = self.dv, self.ac
        ev_cast = self.ev_cast

        wmap = {"wq": 0, "wk": 1, "wv": 2, "wo": 3}

        def ldw_fm(dst, name):
            return [self.dma(
                "sync" if h == 0 else "gpsimd",
                dst[:, h * 4096:(h + 1) * 4096].rearrange(
                    "p (c m) -> p c m", m=D),
                rap(self.watt_d, wmap[name] * D * D + h * 4 * 128 * D,
                    [[D, 128], [128 * D, 4], [1, D]]),
                [self.last_fc_ev, self.ev_wfc_st, ev_cast[wmap[name]]])
                for h in range(2)]
        ev_wqkv = [ldw_fm(self.wqf, "wq"), ldw_fm(self.wkf, "wk"),
                   ldw_fm(self.wvf, "wv")]
        self.ev_onesrow = v(lambda g: g.memset(self.onesrow, 1.0))
        ev_bq = self.dma("gpsimd", self.brow[0:1, 0:1024],
                         self.dm["bq"], [self.last_proj_ev, self.ev_onesrow])
        ev_bk = self.dma("gpsimd", self.brow[0:1, 1024:2048],
                         self.dm["bk"], [ev_bq])
        ev_bv = self.dma("gpsimd", self.brow[0:1, 2048:3072],
                         self.dm["bv"], [ev_bk])

        x1_v = self.x1_d.rearrange("(n p) c -> n p c", p=128)
        qtm_v = self.qtm_d.rearrange("(n p) c -> n p c", p=128)
        ktm_v = self.ktm_d.rearrange("(n p) c -> n p c", p=128)
        vtm_v = self.vtm_d.rearrange("(n p) c -> n p c", p=128)

        ev_acc0 = v(lambda g: g.memset(self.accA[:], 0.0))
        ev_z0 = v(lambda g: g.memset(self.zacc[:], 0.0))

        ev_pfc = [None, None]
        ev_xfree = [self.ev_p1[NT - 2], self.ev_p1[NT - 1]]
        bufev = {}
        ev_accA = ev_acc0
        faA = [None] * (NT + 1)
        pendA = [None] * NT

        xnev = [None] * (NT + 1)
        trwar = [None, None]

        def frontA2_pre(t):
            par = t % 2
            ev_xl = self.dma("sync", self.xbuf[par][:], x1_v[t],
                             [self.ev_p1[t], ev_xfree[par]])
            ev_ln, mu, rstd, _ = self._ln(self.xbuf[par][:], D, 8 + 16 * par,
                                          [ev_xl])
            ev_xn = v(lambda g, pp=par, m=mu, r=rstd: g.tensor_scalar(
                out=self.xnbf[pp][:], in0=self.xbuf[pp][:], scalar1=m,
                scalar2=r, op0=ALU.subtract, op1=ALU.mult),
                [ev_ln, trwar[par]])
            ev_xfree[par] = ev_xn
            xnev[t] = ev_xn

        def frontA2_tr(t):
            par = t % 2
            evt = None
            for b in range(8):
                w0 = [xnev[t], self.ev_id]
                if b == 0:
                    w0.append(self.ev_ptr[par])
                evt = self.tr(self.ptr[par][:, b * 128:(b + 1) * 128],
                              self.xnbf[par][:, b * 128:(b + 1) * 128], w0)
            trwar[par] = evt
            ev_xnT = v(lambda g, pp=par: g.tensor_copy(self.xnT[pp][:],
                                                       self.ptr[pp][:]), [evt])
            self.ev_ptr[par] = ev_xnT
            faA[t] = ev_xnT

        frontA2_pre(0)
        frontA2_tr(0)
        frontA2_pre(1)
        frontA2_tr(1)
        frontA2_pre(2)
        for t in range(NT):
            par = t % 2
            ev_xnT = faA[t]
            evq_mm = evk_mm = evv_mm = None
            for mi, (pb, wf, boff) in enumerate(
                    ((self.pfc[0], self.wqf, 0), (self.pfc[1], self.wkf, 1024),
                     (self.ppj, self.wvf, 2048))):
                wseed = [ev_bv, self.ev_onesrow,
                         (ev_pfc[0], ev_pfc[1], self.ev_ppj)[mi]]
                for half in range(2):
                    self.mm(pb[:, half * 512:(half + 1) * 512],
                            self.onesrow[0:1, :],
                            self.brow[0:1, boff + half * 512:
                                      boff + half * 512 + 512],
                            start=True, stop=False,
                            waits=wseed if half == 0 else [])
                w0 = [ev_xnT] + ev_wqkv[mi]
                evm = None
                for dc in range(8):
                    for half in range(2):
                        evm = self.mm(
                            pb[:, half * 512:(half + 1) * 512],
                            self.xnT[par][:, dc * 128:(dc + 1) * 128],
                            wf[:, dc * 1024 + half * 512:
                               dc * 1024 + half * 512 + 512],
                            start=False, stop=(dc == 7),
                            waits=w0 if (dc == 0 and half == 0) else [])
                if mi == 0:
                    evq_mm = evm
                elif mi == 1:
                    evk_mm = evm
                else:
                    evv_mm = evm
            if t == NT - 1:
                self.last_qkv_ev = evv_mm
            ev_qs = a(lambda g, pp=par: g.activation(
                out=self.qtm_b[pp][:], in_=self.pfc[0][:], func=AF.Sigmoid),
                [evq_mm, bufev.get(("qtm", par))])
            ev_ks = a(lambda g, pp=par: g.activation(
                out=self.ktm_b[pp][:], in_=self.pfc[1][:], func=AF.Sigmoid),
                [evk_mm, bufev.get(("ktm", par))])
            ev_vt = v(lambda g, pp=par: g.tensor_copy(
                self.vtm_b[pp][:], self.ppj[:]),
                [evv_mm, bufev.get(("vtm", par))])
            ev_pfc[0], ev_pfc[1] = ev_qs, ev_ks
            self.ev_ppj = ev_vt
            bufev[("qtm", par)] = self.dma("sync", qtm_v[t],
                                           self.qtm_b[par][:], [ev_qs])
            bufev[("ktm", par)] = self.dma("sync", ktm_v[t],
                                           self.ktm_b[par][:], [ev_ks])
            bufev[("vtm", par)] = self.dma("sync", vtm_v[t],
                                           self.vtm_b[par][:], [ev_vt])
            pendA[t] = (ev_qs, ev_ks)

            tt = t - 1 if t >= 1 else None
            if t == NT - 1:
                backs = [t - 1, t]
            elif t >= 1:
                backs = [t - 1]
            else:
                backs = []
            for tb in backs:
                parb = tb % 2
                ev_qs_b, ev_ks_b = pendA[tb]
                evq = None
                for b in range(8):
                    w0 = [ev_qs_b, self.ev_id]
                    if b == 0:
                        w0.append(self.ev_ptr[parb])
                    evq = self.tr(self.ptr[parb][:, b * 128:(b + 1) * 128],
                                  self.qtm_b[parb][:, b * 128:(b + 1) * 128],
                                  w0)
                ev_qf = a(lambda g, pp=parb: g.copy(self.qfm_b[pp][:],
                                                    self.ptr[pp][:]),
                          [evq, bufev.get(("qfm", parb))])
                self.ev_ptr[parb] = ev_qf
                evk = None
                for b in range(8):
                    w0 = [ev_ks_b, self.ev_id]
                    if b == 0:
                        w0.append(self.ev_ptr[1 - parb])
                    evk = self.tr(self.ptr[1 - parb][:, b * 128:(b + 1) * 128],
                                  self.ktm_b[parb][:, b * 128:(b + 1) * 128],
                                  w0)
                ev_kf = v(lambda g, pp=parb: g.tensor_copy(self.kfm_b[pp][:],
                                                           self.ptr[1 - pp][:]),
                          [evk, bufev.get(("kfm", parb))])
                self.ev_ptr[1 - parb] = ev_kf
                ev_qr = v(lambda g, pp=parb: g.tensor_reduce(
                    out=self.sv[:, 16:24],
                    in_=self.qfm_b[pp][:].rearrange("p (c n) -> p c n", n=128),
                    axis=AX, op=ALU.add), [ev_qf, ev_accA])
                ev_a1 = self.po(lambda g: g.tensor_add(self.accA[:, 8:16],
                                                       self.accA[:, 8:16],
                                                       self.sv[:, 16:24]),
                                [ev_qr])
                ev_kr = v(lambda g, pp=parb: g.tensor_reduce(
                    out=self.sv[:, 24:32],
                    in_=self.kfm_b[pp][:].rearrange("p (c n) -> p c n", n=128),
                    axis=AX, op=ALU.add), [ev_kf, ev_a1])
                ev_accA = self.po(lambda g: g.tensor_add(self.accA[:, 0:8],
                                                         self.accA[:, 0:8],
                                                         self.sv[:, 24:32]),
                                  [ev_kr])
                fmoff = (tb // 4) * 4096 + (tb % 4) * 128
                bufev[("qfm", parb)] = self.dma(
                    "sync",
                    rap(self.qfm_d, fmoff,
                        [[8 * T, 128], [512, 8], [1, 128]]),
                    self.qfm_b[parb][:].rearrange("p (c n) -> p c n", n=128),
                    [ev_qf])
                bufev[("kfm", parb)] = self.dma(
                    "sync",
                    rap(self.kfm_d, fmoff,
                        [[8 * T, 128], [512, 8], [1, 128]]),
                    self.kfm_b[parb][:].rearrange("p (c n) -> p c n", n=128),
                    [ev_kf])
            if t + 2 < NT:
                frontA2_tr(t + 2)
            if t + 3 < NT:
                frontA2_pre(t + 3)

        ev_i = self.dma("gpsimd", self.ar1i[:], self.accA[:], [ev_accA])
        ev_cc1 = self.step("gpsimd",
                           lambda g: g.collective_compute(
                               "AllReduce", ALU.add, replica_groups=self.repl,
                               ins=[self.ar1i.opt()], outs=[self.ar1o.opt()]),
                           [ev_i], [(self.CC, 1)])
        ev_w1 = self.dma("gpsimd", self.warr[:], self.ar1o[:], [ev_cc1])
        ev_we = v(lambda g: g.tensor_scalar(out=self.warr[:], in0=self.warr[:],
                                            scalar1=EPS, scalar2=None,
                                            op0=ALU.add), [ev_w1])
        ev_b1 = self._build_B(self.B1, [ev_we])
        ev_c01 = self._perhead_c(ev_we, 0, 1)

        rb = self.rb
        si_sbf = self.xnT[0].bitcast(F32)
        sa_sbf = self.xnT[1].bitcast(F32)
        sist = [None] * NB
        sast = [None] * NB
        ldev = {}
        sioT_war = None
        ev_dhm_war = None
        ev_sst = None
        pdv = [(self.pd_a, self.pd_b),
               (self.pfc[0][32:48, 0:512], self.pfc[0][32:48, 512:1024])]
        pdfree = [ev_pfc[0], ev_pfc[0]]
        lds = {}
        pdmm = {}
        bfev = {}
        wsst = {"q": None, "k": None}

        def b_loads(u):
            pr = u % 2
            w_st = [self.ev_w2_st] if u < 2 else []
            lds[u] = (
                self.dma("sync", rb[0 + pr],
                         rap(self.qfm_d, u * 4096, [[8 * T, 128], [1, 4096]]),
                         [bufev[("qfm", 0)], bufev[("qfm", 1)],
                          ldev.get(("b0", pr))] + w_st),
                self.dma("sync", rb[2 + pr],
                         rap(self.kfm_d, u * 4096, [[8 * T, 128], [1, 4096]]),
                         [bufev[("kfm", 0)], bufev[("kfm", 1)],
                          ldev.get(("b2", pr))] + w_st),
                self.dma("gpsimd", rb[4 + pr],
                         rap(self.qtm_d, u * BT * D,
                             [[D, 128], [128 * D, 4], [1, D]]),
                         [bufev[("qtm", 0)], bufev[("qtm", 1)],
                          ldev.get(("b4", pr))] + w_st),
                self.dma("gpsimd", rb[6 + pr],
                         rap(self.ktm_d, u * BT * D,
                             [[D, 128], [128 * D, 4], [1, D]]),
                         [bufev[("ktm", 0)], bufev[("ktm", 1)],
                          ldev.get(("b6", pr))] + w_st))

        def b_pd(u):
            pr = u % 2
            pa, pb_ = pdv[u % 2]
            ev_lq, ev_lk = lds[u][0], lds[u][1]
            evm1 = evm2 = None
            for c in range(8):
                evm1 = self.mm(pa, self.B1[:, 32 * c:32 * c + 16],
                               rb[0 + pr][:, c * BT:(c + 1) * BT],
                               start=(c == 0), stop=(c == 7),
                               waits=[ev_lq, ev_b1, pdfree[u % 2]]
                               if c == 0 else [])
            for c in range(8):
                evm2 = self.mm(pb_, self.B1[:, 32 * c + 16:32 * c + 32],
                               rb[2 + pr][:, c * BT:(c + 1) * BT],
                               start=(c == 0), stop=(c == 7),
                               waits=[ev_lk, ev_b1] if c == 0 else [])
            pdmm[u] = (evm1, evm2)
            ldev[("b0", pr)] = evm1
            ldev[("b2", pr)] = evm2

        def b_chain(u):
            nonlocal ev_dhm_war, sioT_war
            evm1, evm2 = pdmm[u]
            pa, pb_ = pdv[u % 2]
            ev_d1 = v(lambda g, p1=pa: g.tensor_scalar(
                out=self.dhm[0:16, 0:512], in0=p1,
                scalar1=self.cvec[0:16, 0:1], scalar2=None, op0=ALU.add),
                [evm1, ev_c01, ev_dhm_war])
            ev_d2 = v(lambda g, p2=pb_: g.tensor_scalar(
                out=self.dhm[0:16, 512:1024], in0=p2,
                scalar1=self.cvec[0:16, 1:2], scalar2=None, op0=ALU.add),
                [evm2, ev_d1])
            pdfree[u % 2] = ev_d2
            evt = None
            for cc in range(4):
                w0 = [ev_d2, self.ev_id]
                if cc == 0:
                    w0.append(self.ev_ptr[0])
                evt = self.tr(self.ptr[0][:, 32 * cc:32 * cc + 16],
                              self.dhm[0:16, 128 * cc:128 * cc + 128], w0)
                evt = self.tr(self.ptr[0][:, 32 * cc + 16:32 * cc + 32],
                              self.dhm[0:16, 512 + 128 * cc:
                                       512 + 128 * cc + 128],
                              [self.ev_id])
            ev_dhm_war = evt
            ev_rec = v(lambda g: g.reciprocal(self.siorec,
                                              self.ptr[0][:, 0:128]),
                       [evt, sioT_war])
            self.ev_ptr[0] = ev_rec
            bfev[u] = v(lambda g: g.tensor_copy(self.sio_tm[:], self.siorec),
                        [ev_rec])
            sist[u] = v(lambda g, uu=u: g.tensor_copy(
                rap(si_sbf, uu * 64, [[512, 128], [16, 4], [1, 16]]),
                rap(self.siorec, 0, [[self.AFW, 128], [32, 4], [1, 16]])),
                [ev_rec, self.last_qkv_ev if u == 0 else None])

        def b_ws(u):
            nonlocal sioT_war
            pr = u % 2
            ev_bf = bfev[u]
            ev_lqt, ev_lkt = lds[u][2], lds[u][3]
            evmq = evmk = None
            for cc in range(4):
                lhs = self.sio_tm[:, 32 * cc:32 * cc + 32]
                for half in range(2):
                    evmq = self.mm(self.pws_q[:, half * 512:(half + 1) * 512],
                                   lhs,
                                   rb[4 + pr][:, cc * 1024 + half * 512:
                                               cc * 1024 + half * 512 + 512],
                                   start=(u == 0 and cc == 0),
                                   stop=(u == NB - 1 and cc == 3),
                                   waits=[ev_bf, ev_lqt, ev_pfc[1]]
                                   if (u == 0 and cc == 0 and half == 0)
                                   else ([ev_bf] if half == 0 else []))
                for half in range(2):
                    evmk = self.mm(self.pws_k[:, half * 512:(half + 1) * 512],
                                   lhs,
                                   rb[6 + pr][:, cc * 1024 + half * 512:
                                               cc * 1024 + half * 512 + 512],
                                   start=(u == 0 and cc == 0),
                                   stop=(u == NB - 1 and cc == 3),
                                   waits=[ev_bf, ev_lkt, self.ev_ppj]
                                   if (u == 0 and cc == 0 and half == 0)
                                   else ([ev_bf] if half == 0 else []))
            sioT_war = evmk
            ldev[("b4", pr)] = evmq
            ldev[("b6", pr)] = evmk
            return evmk

        b_loads(0)
        b_pd(0)
        for u in range(NB):
            if u + 1 < NB:
                b_loads(u + 1)
            b_chain(u)
            if u + 1 < NB:
                b_pd(u + 1)
            ev_last_ws = b_ws(u)

        ev_wo2 = ldw_fm(self.wof, "wo")

        ev_e1 = v(lambda g: g.tensor_copy(self.qsi_sb, self.pws_q[0:16, :]),
                  [ev_last_ws])
        ev_e2 = v(lambda g: g.tensor_copy(self.kso_sb, self.pws_k[:]),
                  [ev_last_ws, ev_e1])
        ev_i1 = self.dma("gpsimd", self.ar2i[:, 0:1024], self.qsi_sb, [ev_e1])
        ev_i2 = self.dma("gpsimd", self.ar2i[:, 1024:2048],
                         rap(self.kso_sb, 16 * self.AFW,
                             [[self.AFW, 16], [1, 1024]]), [ev_e2])
        ev_cc2 = self.step("gpsimd",
                           lambda g: g.collective_compute(
                               "AllReduce", ALU.add, replica_groups=self.repl,
                               ins=[self.ar2i.opt()], outs=[self.ar2o.opt()]),
                           [ev_i1, ev_i2], [(self.CC, 1)])
        ev_bl = [None]
        for side, base in ((0, 1024), (1, 0)):
            ev_t1 = self.dma("gpsimd", self.warr[0:64, side * 8:side * 8 + 8],
                             rap(self.ar2o, base, [[1, 64], [2112, 8]]),
                             [ev_cc2, ev_b1, ev_c01, ev_bl[-1]], slow=True)
            ev_t2 = self.dma("gpsimd", self.warr[64:128, side * 8:side * 8 + 8],
                             rap(self.ar2o, base + 2048 + 64,
                                 [[1, 64], [2112, 8]]),
                             [ev_t1], slow=True)
            ev_bl.append(ev_t2)
        ev_we2 = v(lambda g: g.tensor_scalar(out=self.warr[:], in0=self.warr[:],
                                             scalar1=EPS, scalar2=None,
                                             op0=ALU.add), [ev_bl[-1]])
        ev_b2 = self._build_B(self.B2, [ev_we2])
        ev_c23 = self._perhead_c(ev_we2, 2, 3)
        ev_nc = v(lambda g: g.tensor_scalar(out=self.cvec[0:16, 4:5],
                                            in0=self.cvec[0:16, 2:3],
                                            scalar1=-1.0, scalar2=None,
                                            op0=ALU.mult), [ev_c23])

        ev_dhm_warC = ev_dhm_war
        ev_kv = None
        ev_z = ev_z0
        ev_etm_war = None
        ev_vsc_war = [None, None]
        ev_e = None
        eev = {}
        etmev = {}

        def c_loads(u):
            pr = u % 2
            lds[u] = (
                self.dma("sync", rb[0 + pr],
                         rap(self.qfm_d, u * 4096, [[8 * T, 128], [1, 4096]]),
                         [ldev.get(("b0", pr))]),
                self.dma("sync", rb[2 + pr],
                         rap(self.kfm_d, u * 4096, [[8 * T, 128], [1, 4096]]),
                         [ldev.get(("b2", pr))]),
                self.dma("gpsimd", rb[4 + pr],
                         rap(self.ktm_d, u * BT * D,
                             [[D, 128], [128 * D, 4], [1, D]]),
                         [ldev.get(("b4", pr))]),
                self.dma("gpsimd", rb[6 + pr],
                         rap(self.vtm_d, u * BT * D,
                             [[D, 128], [128 * D, 4], [1, D]]),
                         [bufev[("vtm", 0)], bufev[("vtm", 1)],
                          ldev.get(("b6", pr))]))

        def c_pd(u):
            pr = u % 2
            pa, pb_ = pdv[u % 2]
            ev_lq, ev_lk = lds[u][0], lds[u][1]
            evm1 = evm2 = None
            for c in range(8):
                evm1 = self.mm(pa, self.B2[:, 32 * c:32 * c + 16],
                               rb[0 + pr][:, c * BT:(c + 1) * BT],
                               start=(c == 0), stop=(c == 7),
                               waits=[ev_lq, ev_b2, pdfree[u % 2]]
                               if c == 0 else [])
            for c in range(8):
                evm2 = self.mm(pb_, self.B2[:, 32 * c + 16:32 * c + 32],
                               rb[2 + pr][:, c * BT:(c + 1) * BT],
                               start=(c == 0), stop=(c == 7),
                               waits=[ev_lk, ev_b2] if c == 0 else [])
            pdmm[u] = (evm1, evm2)
            ldev[("b0", pr)] = evm1
            ldev[("b2", pr)] = evm2

        def c_chain(u):
            nonlocal ev_dhm_warC, ev_etm_war, ev_z, ev_e
            evm1, evm2 = pdmm[u]
            pa, pb_ = pdv[u % 2]
            ev_em = a(lambda g, p1=pa: g.activation(
                out=self.dhm[0:16, 512:1024], in_=p1, func=AF.Exp,
                scale=self.cst[0:16, 1:2], bias=self.cvec[0:16, 4:5]),
                [evm1, ev_nc, ev_dhm_warC])
            ev_cs = v(lambda g, p2=pb_: g.tensor_scalar(
                out=self.qsi_sb[0:16, 0:512], in0=p2,
                scalar1=self.cvec[0:16, 3:4], scalar2=1.0,
                op0=ALU.add, op1=ALU.min), [evm2, ev_c23, ev_i1])
            ev_cs2 = v(lambda g: g.tensor_scalar(
                out=self.qsi_sb[0:16, 0:512], in0=self.qsi_sb[0:16, 0:512],
                scalar1=-1.0, scalar2=None, op0=ALU.max), [ev_cs])
            ev_e = a(lambda g: g.activation(
                out=self.dhm[0:16, 0:512], in_=self.qsi_sb[0:16, 0:512],
                func=AF.Exp, accum_out=self.zacc[0:16, 1:2]), [ev_cs2, ev_em])
            eev[u] = ev_e
            pdfree[u % 2] = ev_e
            ev_z = v(lambda g: g.tensor_add(self.zacc[0:16, 0:1],
                                            self.zacc[0:16, 0:1],
                                            self.zacc[0:16, 1:2]),
                     [ev_e, ev_z])
            evt = None
            for cc in range(4):
                w0 = [ev_e, self.ev_id]
                if cc == 0:
                    w0.append(self.ev_ptr[0])
                evt = self.tr(self.ptr[0][:, 32 * cc:32 * cc + 16],
                              self.dhm[0:16, 128 * cc:128 * cc + 128], w0)
                evt = self.tr(self.ptr[0][:, 32 * cc + 16:32 * cc + 32],
                              self.dhm[0:16, 512 + 128 * cc:
                                       512 + 128 * cc + 128],
                              [self.ev_id])
            ev_dhm_warC = evt
            ev_etm = v(lambda g: g.tensor_copy(self.etm[:],
                                               self.ptr[0][:, 0:128]),
                       [evt, ev_etm_war])
            etmev[u] = ev_etm
            self.ev_ptr[0] = ev_etm
            ev_sa = None
            for cc in range(4):
                ev_sa = v(lambda g, c2=cc: g.tensor_scalar(
                    out=rap(self.satm, 16 * c2, [[self.AFW, 128], [1, 16]]),
                    in0=self.etm[:, 32 * c2 + 16:32 * c2 + 32],
                    scalar1=1.0, scalar2=None, op0=ALU.add), [ev_etm, ev_sa])
            ev_sar = v(lambda g: g.reciprocal(self.satm, self.satm), [ev_sa])
            sast[u] = v(lambda g, uu=u: g.tensor_copy(
                rap(sa_sbf, uu * 64, [[512, 128], [16, 4], [1, 16]]),
                rap(self.satm, 0, [[self.AFW, 128], [16, 4], [1, 16]])),
                [ev_sar, self.last_qkv_ev if u == 0 else None])
            ev_etm_war = ev_sar

        def c_kv(u):
            nonlocal ev_kv
            pr = u % 2
            ev_lkt, ev_lvt = lds[u][2], lds[u][3]
            ev_etm = etmev[u]
            ev_vsc = None
            for cc in range(4):
                ebc = rap(self.etm, 32 * cc,
                          [[self.ATW, 128], [1, 16], [0, 64]])
                ev_vsc = v(lambda g, c2=cc, eb=ebc, pp=pr:
                           g.tensor_tensor(
                               out=self.vsc_b[pp][:].rearrange(
                                   "p (h j) -> p h j", j=64),
                               in0=rb[6 + pp][:, c2 * 1024:
                                              (c2 + 1) * 1024].rearrange(
                                   "p (h j) -> p h j", j=64),
                               in1=eb, op=ALU.mult),
                           [ev_etm, ev_lvt,
                            ev_vsc if ev_vsc else ev_vsc_war[pr]])
                for p2 in range(8):
                    w0 = []
                    if p2 == 0:
                        w0 = [ev_vsc, ev_lkt]
                        if u == 0 and cc == 0:
                            w0.append(ev_e2)
                    ev_kv = self.mm(
                        self.pkv[:, 128 * p2:128 * (p2 + 1)],
                        rb[4 + pr][:, cc * 1024 + 128 * p2:
                                   cc * 1024 + 128 * (p2 + 1)],
                        self.vsc_b[pr][:, 128 * p2:128 * (p2 + 1)],
                        start=(u == 0 and cc == 0),
                        stop=(u == NB - 1 and cc == 3),
                        waits=w0)
            ev_vsc_war[pr] = ev_kv
            ldev[("b4", pr)] = ev_kv
            ldev[("b6", pr)] = ev_kv

        c_loads(0)
        c_pd(0)
        for u in range(NB):
            if u + 1 < NB:
                c_loads(u + 1)
            c_chain(u)
            if u + 1 < NB:
                c_pd(u + 1)
            c_kv(u)

        self.pf_pj = []
        for k in range(3):
            self.pf_pj.append(self.dma(
                "gpsimd", self.w2t3[:, k * 8192:(k + 1) * 8192],
                rap(self.w2t_d, k * 8192, [[32 * D, 128], [1, 8192]]),
                [self.last_qkv_ev, self.ev_w2_st]))

        self.pf_fc = []
        w_rb = [ldev[("b2", 0)], ldev[("b2", 1)], ldev[("b4", 0)],
                ldev[("b4", 1)], ldev[("b6", 0)], ldev[("b6", 1)]]
        for k in range(3):
            self.pf_fc.append(self.dma(
                "gpsimd", self.wfcq3[:, 8192 + k * 8192:8192 + (k + 1) * 8192],
                rap(self.wfcq_d, 8192 + k * 8192, [[8 * HF, 128], [1, 8192]]),
                w_rb if k == 0 else []))

        ev_kev1 = v(lambda g: g.tensor_copy(
            rap(self.kvsb, 0, [[self.AFW, 64], [64, 8], [1, 64]]),
            rap(self.pkv, 0, [[1024, 64], [128, 8], [1, 64]])), [ev_kv])
        ev_kev2 = v(lambda g: g.tensor_copy(
            rap(self.kvsb, self.AFW * 64, [[self.AFW, 64], [64, 8], [1, 64]]),
            rap(self.pkv, 1024 * 64 + 64, [[1024, 64], [128, 8], [1, 64]])),
            [ev_kev1])
        ev_i3a = self.dma("gpsimd", self.ar3i[:, 0:512], self.kvsb,
                          [ev_kev2])
        ev_i3b = self.dma("gpsimd", self.ar3i[0:16, 512:513],
                          self.zacc[0:16, 0:1], [ev_z], slow=True)
        ev_i3c = self.dma("gpsimd",
                          rap(self.ar3i, 16 * 513 + 512, [[513, 112], [1, 1]]),
                          self.zeros[0:112, 0:1], [ev_z], slow=True)
        ev_cc3 = self.step("gpsimd",
                           lambda g: g.collective_compute(
                               "AllReduce", ALU.add, replica_groups=self.repl,
                               ins=[self.ar3i.opt()], outs=[self.ar3o.opt()]),
                           [ev_i3a, ev_i3b, ev_i3c], [(self.CC, 1)])
        ev_kvl = self.dma("gpsimd", self.kvr, self.ar3o[:, 0:512], [ev_cc3])
        ev_zl = self.dma("gpsimd", self.zacc[0:16, 2:3],
                         self.ar3o[0:16, 512:513], [ev_cc3], slow=True)
        ev_kb0 = v(lambda g: g.memset(self.kvblocks[:], 0.0),
                   [ev_vsc_war[0], ev_vsc_war[1], ev_kev2, ev_i3a, sast[NB - 1]])
        ev_kb1 = v(lambda g: g.tensor_copy(
            rap(self.kvblocks, 0, [[self.ATW, 64], [128, 8], [1, 64]]),
            rap(self.kvr, 0, [[self.AFW, 64], [64, 8], [1, 64]])),
            [ev_kvl, ev_kb0])
        ev_kb2 = v(lambda g: g.tensor_copy(
            rap(self.kvblocks, self.ATW * 64 + 64,
                [[self.ATW, 64], [128, 8], [1, 64]]),
            rap(self.kvr, self.AFW * 64, [[self.AFW, 64], [64, 8], [1, 64]])),
            [ev_kb1])
        ev_zf = v(lambda g: g.reciprocal(self.zacc[0:16, 3:4],
                                         self.zacc[0:16, 2:3]), [ev_zl])
        ev_zf2 = v(lambda g: g.tensor_scalar(out=self.zacc[0:16, 3:4],
                                             in0=self.zacc[0:16, 3:4],
                                             scalar1=LSEQ, scalar2=None,
                                             op0=ALU.mult), [ev_zf])
        ev_zs = self.dma("gpsimd", self.zf_d[:], self.zacc[0:16, 3:4], [ev_zf2])
        ev_zbc = self.dma("gpsimd", self.zbc[:],
                          rap(self.zf_d, 0, [[0, 128], [1, 16]]), [ev_zs])
        bo16 = self.art[0:1, 12032:13056]
        ev_bo = self.dma("gpsimd", bo16, self.dm["bo"], [ev_kb2, ev_zbc])

        x1_v2 = self.x1_d.rearrange("(n p) c -> n p c", p=128)
        dstD = self.y_out if _STOP_AFTER == 2 else self.x2_d
        x2_v = dstD.rearrange("(n p) c -> n p c", p=128)
        ev_p2 = [None] * NT
        ev_attw = [ev_e, ev_kev2]
        ev_sc_prev = ev_e
        ev_woev = [ev_e2, ev_kev2]
        ev_stot_war = None
        ev_xD = [None, None]
        ev_wo_done = None
        for u in range(NB):
            pr = u % 2
            qfm_t = rb[0 + pr]
            ev_lq = self.dma("sync", qfm_t,
                             rap(self.qfm_d, u * 4096,
                                 [[8 * T, 128], [1, 4096]]),
                             [ldev.get(("b0", pr))])
            ldev[("b0", pr)] = None
            ev_s1 = v(lambda g, uu=u: g.tensor_mul(
                self.stot,
                rap(si_sbf, uu * 64, [[512, 128], [16, 4], [1, 16]]),
                rap(sa_sbf, uu * 64, [[512, 128], [16, 4], [1, 16]])),
                [sist[u], sast[u], ev_stot_war])
            zb = rap(self.zbc, 0, [[16, 128], [0, 4], [1, 16]])
            ev_s2 = v(lambda g, z=zb: g.tensor_tensor(
                out=rap(self.stot, 0, [[self.AFW, 128], [16, 4], [1, 16]]),
                in0=rap(self.stot, 0, [[self.AFW, 128], [16, 4], [1, 16]]),
                in1=z, op=ALU.mult), [ev_s1, ev_zbc])
            ev_sc_last = None
            for tb in range(4):
                t128 = u * 4 + tb
                pD = self.pfc[0]
                wo_pb = self.ppj if tb % 2 == 0 else self.pfc[1]
                evo = None
                for p2 in range(8):
                    evo = self.mm(
                        pD[:, 128 * p2:128 * (p2 + 1)],
                        qfm_t[:, p2 * BT + tb * 128: p2 * BT + (tb + 1) * 128],
                        self.kvblocks[:, 128 * p2:128 * (p2 + 1)],
                        start=True, stop=True,
                        waits=[ev_lq, ev_kb2, ev_sc_prev]
                        if p2 == 0 else [])
                    self.last_outmm_ev = evo
                for half in range(2):
                    self.mm(wo_pb[:, half * 512:(half + 1) * 512],
                            self.onesrow[0:1, :],
                            bo16[0:1, half * 512:half * 512 + 512],
                            start=True, stop=False,
                            waits=[ev_bo, ev_woev[tb % 2]]
                            if half == 0 else [])
                sb_ap = rap(self.stot, 16 * tb,
                            [[self.AFW, 128], [1, 16], [0, 64]])
                ev_sc = v(lambda g, pD2=pD, sbb=sb_ap, t2=tb:
                          g.tensor_tensor(
                              out=self.attntm_b[t2 % 2][:].rearrange(
                                  "p (h j) -> p h j", j=64),
                              in0=pD2[:].rearrange("p (h j) -> p h j", j=64),
                              in1=sbb, op=ALU.mult),
                          [evo, ev_s2, ev_attw[tb % 2]])
                ev_sc_prev = ev_sc
                ev_sc_last = ev_sc
                evt = None
                for b in range(8):
                    w0 = [ev_sc, self.ev_id]
                    if b == 0:
                        w0.append(self.ev_ptr[tb % 2])
                    evt = self.tr(self.ptr[tb % 2][:, b * 128:(b + 1) * 128],
                                  self.attntm_b[tb % 2][:, b * 128:
                                                        (b + 1) * 128], w0)
                ev_attw[tb % 2] = evt
                ev_af = a(lambda g, t2=tb: g.copy(self.attnfm_b[t2 % 2][:],
                                                  self.ptr[t2 % 2][:]), [evt])
                self.ev_ptr[tb % 2] = ev_af
                evw = None
                for c in range(8):
                    for half in range(2):
                        evw = self.mm(
                            wo_pb[:, half * 512:(half + 1) * 512],
                            self.attnfm_b[tb % 2][:, c * 128:(c + 1) * 128],
                            self.wof[:, c * 1024 + half * 512:
                                     c * 1024 + half * 512 + 512],
                            start=False, stop=(c == 7),
                            waits=([ev_af] + (ev_wo2 if t128 == 0 else []))
                            if (c == 0 and half == 0) else [])
                ev_wo_done = evw
                ev_xl = self.dma("sync", self.xbuf[tb % 2][:], x1_v2[t128],
                                 [ev_xD[tb % 2]])
                ev_u = v(lambda g, t2=tb, wp=wo_pb: g.scalar_tensor_tensor(
                    out=self.xbuf[t2 % 2][:], in0=wp[:], scalar=1.0,
                    in1=self.xbuf[t2 % 2][:], op0=ALU.mult, op1=ALU.add),
                    [evw, ev_xl])
                ev_woev[tb % 2] = ev_u
                ev_st = self.dma("sync", x2_v[t128],
                                 self.xbuf[tb % 2][:], [ev_u])
                ev_xD[tb % 2] = ev_st
                ev_p2[t128] = ev_st
            ev_stot_war = ev_sc_last
        self.ev_p2 = ev_p2
        self.ev_wo_done = ev_wo_done
        self.ev_ppj = ev_p2[NT - 1]
        self.pf_fc.append(self.dma(
            "sync", self.wfcq3[:, 0:8192],
            rap(self.wfcq_d, 0, [[8 * HF, 128], [1, 8192]]),
            [self.last_outmm_ev, self.ev_wfc_st]))
        self.pf_pj.append(self.dma(
            "gpsimd", self.w2t3[:, 24576:32768],
            rap(self.w2t_d, 24576, [[32 * D, 128], [1, 8192]]),
            [ev_wo_done]))

    def _build_B(self, B, waits):
        v = self.dv
        ev = v(lambda g: g.memset(B[:], 0.0), list(waits))
        for c in range(8):
            for side in range(2):
                sc = side * 8 + c
                col = 32 * c + 16 * side + 2 * c
                ev = v(lambda g, b=B, s=sc, cl=col: g.tensor_copy(
                    b[0:64, cl:cl + 1], self.warr[0:64, s:s + 1]), [ev])
                ev = v(lambda g, b=B, s=sc, cl=col: g.tensor_copy(
                    b[64:128, cl + 1:cl + 2], self.warr[64:128, s:s + 1]),
                    [ev])
        return ev

    def _perhead_c(self, ev_warr_eps, col_a, col_b):
        v = self.dv
        evm = self.step(
            "tensor",
            lambda g: g.matmul(self.ppj[0:16, 0:2], self.warr[:],
                               self.ones2[:], start=True, stop=True,
                               skip_group_check=True),
            [ev_warr_eps, self.ev_ones2, self.ev_ppj], [(self.PE, 1)])
        ev_ev = v(lambda g: g.tensor_copy(self.hd16[0:16, 0:2],
                                          self.ppj[0:16, 0:2]), [evm])
        self.ev_ppj = ev_ev
        ev_st = self.dma("gpsimd", self.ph_d[:], self.hd16[0:16, 0:2], [ev_ev])
        ev = ev_st
        for i, dstcol in enumerate((col_a, col_b)):
            ev_l = self.dma("gpsimd", self.cvec[0:16, 6:7],
                            rap(self.ph_d, i * 16, [[1, 16], [1, 1]]),
                            [ev, ev_st])
            ev = v(lambda g, dc=dstcol: g.tensor_scalar(
                out=self.cvec[0:16, dc:dc + 1], in0=self.cvec[0:16, 6:7],
                scalar1=EPS, scalar2=None, op0=ALU.mult), [ev_l])
        return ev

    def _emit(self, es):
        block = es.enter_context(self.nc.Block())

        def mkrun(ename):
            def run(eng_obj):
                seen = {}
                for waits, emit, incs in self.q[ename]:
                    for s, vv in waits:
                        if seen.get(s.name, 0) < vv:
                            eng_obj.wait_ge(s.h, vv)
                            seen[s.name] = vv
                    inst = emit(eng_obj)
                    for s, k in incs:
                        inst.then_inc(s.h, k)
            return run

        block.sync(mkrun("sync"))
        block.gpsimd(mkrun("gpsimd"))
        block.tensor(mkrun("tensor"))
        block.vector(mkrun("vector"))
        block.scalar(mkrun("scalar"))


_CACHE = {}


def _build():
    if "nc" not in _CACHE:
        _CACHE["nc"] = Builder().build()
    return _CACHE["nc"]


def kernel(**inputs):
    xs = np.ascontiguousarray(
        np.asarray(inputs["L0_attn_x"], dtype=np.float32).reshape(8 * T, D))
    com = {
        "wq": np.ascontiguousarray(np.asarray(inputs["L0_attn_Wq"], np.float32)),
        "wk": np.ascontiguousarray(np.asarray(inputs["L0_attn_Wk"], np.float32)),
        "wv": np.ascontiguousarray(np.asarray(inputs["L0_attn_Wv"], np.float32)),
        "wo": np.ascontiguousarray(np.asarray(inputs["L0_attn_Wo"], np.float32)),
        "bq": np.ascontiguousarray(np.asarray(inputs["L0_attn_bq"], np.float32).reshape(1, D)),
        "bk": np.ascontiguousarray(np.asarray(inputs["L0_attn_bk"], np.float32).reshape(1, D)),
        "bv": np.ascontiguousarray(np.asarray(inputs["L0_attn_bv"], np.float32).reshape(1, D)),
        "bo": np.ascontiguousarray(np.asarray(inputs["L0_attn_bo"], np.float32).reshape(1, D)),
        "wfc": np.ascontiguousarray(np.asarray(inputs["L0_ffn_W_fc"], np.float32)),
        "lnw": np.ascontiguousarray(np.asarray(inputs["L0_ffn_ln_w"], np.float32).reshape(1, HF)),
        "lnb": np.ascontiguousarray(np.asarray(inputs["L0_ffn_ln_b"], np.float32).reshape(1, HF)),
        "wproj": np.ascontiguousarray(np.asarray(inputs["L0_ffn_W_proj"], np.float32)),
    }
    nc = _build()
    in_maps = [dict(x=np.ascontiguousarray(xs[i * T:(i + 1) * T]),
                    gwfc=np.ascontiguousarray(com["wfc"][i * 512:(i + 1) * 512]),
                    gwproj=np.ascontiguousarray(com["wproj"][i * 128:(i + 1) * 128]),
                    **com)
               for i in range(8)]
    res = run_bass_kernel_spmd(nc, in_maps, core_ids=list(range(8)))
    out = np.concatenate([np.asarray(res.results[i]["y"]) for i in range(8)],
                         axis=0)
    return out.reshape(2, 16384, D).astype(np.float32)
